# version 18
# baseline (speedup 1.0000x reference)
import numpy as np
import ml_dtypes
from contextlib import ExitStack
import concourse.bass as bass
import concourse.mybir as mybir
from concourse.bass_utils import run_bass_kernel_spmd

F32 = mybir.dt.float32
BF16 = mybir.dt.bfloat16
AF = mybir.ActivationFunctionType
ALU = mybir.AluOpType
AX = mybir.AxisListType.X

B, S, D, H, DH, FF = 2, 8192, 1024, 16, 64, 4096
NPAIR, NSLOT, QT, KBLK = 8, 8, 256, 64
KJ = [64 - 8 * j for j in range(NSLOT)]
GJ = [(k - 2) // 2 for k in KJ]
MACRO = 4
SHIFT = 4.0
KILL = -1e15
ENGS = ("tensor", "vector", "scalar", "sync", "gpsimd")


def tile_T(j, cc):
    return 31 - 4 * j - ((cc + j) % 4)


class Sem:
    def __init__(self, stack, nc, name):
        self.h = stack.enter_context(nc.semaphore(name))
        self.n = 0


_PLAN_LOGS = []


class Plan:
    def __init__(self):
        self.q = {k: [] for k in ENGS}
        self.log = {k: [] for k in ENGS}

    def wait(self, eng, sem, tgt=None):
        v = sem.n if tgt is None else tgt
        if v and v > 0:
            self.log[eng].append(("w", sem.h.name if hasattr(sem.h, "name") else id(sem), v))
            self.q[eng].append(lambda e, h=sem.h, v=v: e.wait_ge(h, v))

    def do(self, eng, builder, sem=None, k=1):
        if sem is not None:
            sem.n += k
            self.log[eng].append(("i", sem.h.name if hasattr(sem.h, "name") else id(sem), k))
            self.q[eng].append(lambda e, b=builder, h=sem.h, kk=k: b(e).then_inc(h, kk))
        else:
            self.log[eng].append(("i", None, 0))
            self.q[eng].append(lambda e, b=builder: b(e))

    def emit(self, nc):
        _PLAN_LOGS.append(self.log)
        with nc.Block() as block:
            for name in ENGS:
                fns = self.q[name]
                getattr(block, name)(lambda e, fns=fns: [fn(e) for fn in fns])


def build_nc(upto=None, taps=()):
    nc = bass.Bass("TRN2", target_bir_lowering=False)

    xq = nc.dram_tensor("xq", [NSLOT * QT, D], F32, kind="ExternalInput").ap()
    xb = nc.dram_tensor("xb", [S, D], F32, kind="ExternalInput").ap()
    wq = nc.dram_tensor("wq", [D, D], F32, kind="ExternalInput").ap()
    wk = nc.dram_tensor("wk", [D, D], F32, kind="ExternalInput").ap()
    wv = nc.dram_tensor("wv", [D, D], F32, kind="ExternalInput").ap()
    wo = nc.dram_tensor("wo", [D, D], F32, kind="ExternalInput").ap()
    w1 = nc.dram_tensor("w1", [FF, D], F32, kind="ExternalInput").ap()
    b1 = nc.dram_tensor("b1", [FF], F32, kind="ExternalInput").ap()
    w2 = nc.dram_tensor("w2", [D, FF], F32, kind="ExternalInput").ap()
    b2 = nc.dram_tensor("b2", [D], F32, kind="ExternalInput").ap()
    identb = nc.dram_tensor("identb", [128, 128], BF16, kind="ExternalInput").ap()
    identf = nc.dram_tensor("identf", [128, 128], F32, kind="ExternalInput").ap()
    onesf = nc.dram_tensor("onesf", [1, 128], F32, kind="ExternalInput").ap()
    onescf = nc.dram_tensor("onescf", [128, 1], F32, kind="ExternalInput").ap()
    maskdd = nc.dram_tensor("maskdd", [128, 1024], BF16, kind="ExternalInput").ap()
    padbias = nc.dram_tensor("padbias", [128, 137], F32, kind="ExternalInput").ap()
    y = nc.dram_tensor("y", [NSLOT * QT, D], F32, kind="ExternalOutput").ap()

    kt_d = nc.dram_tensor("kt_d", [KBLK, 128, 1024], BF16).ap()
    v1_d = nc.dram_tensor("v1_d", [KBLK, 128, 1040], BF16).ap()
    kto_d = nc.dram_tensor("kto_d", [16, 128, 1024], BF16).ap()
    v1o_d = nc.dram_tensor("v1o_d", [16, 128, 1040], BF16).ap()
    qt_d = nc.dram_tensor("qt_d", [NSLOT, 128, 2048], BF16).ap()
    xnt_d = nc.dram_tensor("xnt_d", [NSLOT, 128, 2048], BF16).ap()
    w1t_d = nc.dram_tensor("w1t_d", [8, 128, FF], BF16).ap()
    w2t_d = nc.dram_tensor("w2t_d", [32, 128, D], BF16).ap()

    tap_outs = {}
    for tname, tshape, tdt in taps:
        tap_outs[tname] = nc.dram_tensor("tap_" + tname, tshape, tdt,
                                         kind="ExternalOutput").ap()

    outer = ExitStack()
    OE = outer.enter_context

    def sem(name):
        return Sem(outer, nc, name)

    s_cl, s_wl, s_wt, s_we, s_ws = [sem(n) for n in ("s_cl", "s_wl", "s_wt", "s_we", "s_ws")]
    s_xl, s_ln, s_rs, s_ap, s_tr = [sem(n) for n in ("s_xl", "s_ln", "s_rs", "s_ap", "s_tr")]
    s_x2 = sem("s_x2")
    s_te, s_pj, s_pe, s_ss = [sem(n) for n in ("s_te", "s_pj", "s_pe", "s_ss")]
    s_kl, s_qk, s_ex, s_mk, s_pv = [sem(n) for n in ("s_kl", "s_qk", "s_ex", "s_mk", "s_pv")]
    s_ac, s_nm, s_rp, s_am = [sem(n) for n in ("s_ac", "s_nm", "s_rp", "s_am")]
    s_el, s_f1, s_gl, s_f2, s_yv, s_ys = [sem(n) for n in
                                          ("s_el", "s_f1", "s_gl", "s_f2", "s_yv", "s_ys")]
    s_ph = sem("s_ph")

    def barrier(P):
        for _ in range(2):
            for eng in ENGS:
                s_ph.n += 1
                P.log[eng].append(("i", "s_ph", 1))
                P.q[eng].append(lambda e, h=s_ph.h: e.nop().then_inc(h, 1))
            tgt = s_ph.n
            for eng in ENGS:
                P.log[eng].append(("w", "s_ph", tgt))
                P.q[eng].append(lambda e, h=s_ph.h, v=tgt: e.wait_ge(h, v))

    def finish(sources):
        Pt = Plan()
        s_tp = sem("s_tp")
        for name, src in sources.items():
            if name in tap_outs:
                Pt.do("gpsimd", lambda e, d=tap_outs[name], s=src: e.dma_start(
                    out=d, in_=s), s_tp, 16)
        Pt.q["gpsimd"].append(lambda e, h=s_tp.h, v=s_tp.n: e.wait_ge(h, v) if v else None)
        Pt.emit(nc)
        return nc

    woT = OE(nc.sbuf_tensor("woT", [128, 8, 1024], BF16))
    sb_identb = OE(nc.sbuf_tensor("sb_identb", [128, 128], BF16))
    sb_identf = OE(nc.sbuf_tensor("sb_identf", [128, 128], F32))
    sb_onesf = OE(nc.sbuf_tensor("sb_onesf", [1, 128], F32))
    sb_onescf = OE(nc.sbuf_tensor("sb_onescf", [128, 1], F32))
    sb_maskdd = OE(nc.sbuf_tensor("sb_maskdd", [128, 1024], BF16))
    sb_padbias = OE(nc.sbuf_tensor("sb_padbias", [128, 137], F32))
    sb_b1 = OE(nc.sbuf_tensor("sb_b1", [128, 32], F32))
    sb_b2 = OE(nc.sbuf_tensor("sb_b2", [1, 1024], F32))
    sb_eps = OE(nc.sbuf_tensor("sb_eps", [128, 1], F32))
    stats = OE(nc.sbuf_tensor("stats", [128, 64], F32))

    A4a = OE(nc.psum_tensor("A4a", [128, 1024], F32))
    A4b = OE(nc.psum_tensor("A4b", [128, 1024], F32))
    A4c = OE(nc.psum_tensor("A4c", [128, 1024], F32))
    B2a = OE(nc.psum_tensor("B2a", [128, 512], F32))
    B2b = OE(nc.psum_tensor("B2b", [128, 512], F32))

    wab = ExitStack()
    WE = wab.enter_context
    trash = WE(nc.sbuf_tensor("trash", [128, 2, 1024], F32))
    wqT = WE(nc.sbuf_tensor("wqT", [128, 8, 1024], BF16))
    wkT = WE(nc.sbuf_tensor("wkT", [128, 8, 1024], BF16))
    wvT = WE(nc.sbuf_tensor("wvT", [128, 8, 1024], BF16))

    wtmp = ExitStack()
    WT = wtmp.enter_context
    wnat = WT(nc.sbuf_tensor("wnat", [128, 8, 1024], BF16))
    wfb = WT(nc.sbuf_tensor("wfb", [128, 2, 4096], F32))
    absa = WT(nc.sbuf_tensor("absa", [128, 40], F32))
    sc1 = WT(nc.sbuf_tensor("sc1", [128, 8], F32))
    sinv = WT(nc.sbuf_tensor("sinv", [128, 2], F32))
    ternb = WT(nc.sbuf_tensor("ternb", [128, 4096], F32))
    tble = WT(nc.sbuf_tensor("tble", [128, 4096], F32))
    ternt = WT(nc.sbuf_tensor("ternt", [128, 4096], BF16))
    wstg = WT(nc.sbuf_tensor("wstg", [128, 4096], BF16))
    b1n = WT(nc.sbuf_tensor("b1n", [32, 128], F32))

    P = Plan()
    for dst, src in ((sb_identb, identb), (sb_identf, identf), (sb_onesf, onesf),
                     (sb_onescf, onescf), (sb_maskdd, maskdd),
                     (sb_padbias, padbias), (sb_b2, b2[None, :])):
        P.do("sync", lambda e, d=dst, s=src: e.dma_start(out=d[:], in_=s), s_cl, 16)
    P.do("sync", lambda e: e.dma_start(
        out=b1n[:], in_=b1.rearrange("(c p) -> c p", p=128)), s_cl, 16)
    P.do("vector", lambda e: e.memset(sb_eps[:], 1e-5), s_ln, 1)

    for wi, (wsrc, wdst) in enumerate(((wq, wqT), (wk, wkT), (wv, wvT), (wo, woT))):
        if wi > 0:
            P.wait("gpsimd", s_we)
        P.do("gpsimd", lambda e, s=wsrc: e.dma_start(
            out=wnat[:], in_=s.rearrange("(r p) d -> p r d", p=128)), s_wl, 16)
        P.wait("tensor", s_wl)
        for r in range(8):
            ps = (A4a, A4b)[r % 2]
            if r >= 2:
                P.wait("tensor", s_we)
            for dc in range(8):
                P.do("tensor", lambda e, r=r, dc=dc, ps=ps: e.matmul(
                    ps[:, dc * 128:(dc + 1) * 128],
                    wnat[:, r, dc * 128:(dc + 1) * 128], sb_identb[:],
                    start=True, stop=True), s_wt if dc == 7 else None)
            P.wait("vector", s_wt)
            P.do("vector", lambda e, r=r, ps=ps, wd=wdst: e.tensor_copy(
                wd[:, :, r * 128:(r + 1) * 128],
                ps[:].rearrange("p (c f) -> p c f", f=128)), s_we, 1)

    P.wait("tensor", s_cl)
    P.do("tensor", lambda e: e.matmul(A4c[:, 0:32], b1n[:], sb_identf[0:32, 0:32],
                                      start=True, stop=True), s_wt, 1)
    P.wait("vector", s_wt)
    P.do("vector", lambda e: e.tensor_copy(sb_b1[:], A4c[:, 0:32]), s_we, 1)

    def w_abs_pass(wsrc, ntile, tw, acol0, fidx, nrows=1):
        for t in range(ntile):
            par = t % 2
            q = ("sync", "scalar")[t % 2]
            qs = (s_xl, s_x2)[t % 2]
            P.wait(q, s_ln, wfb_free[par])
            P.do(q, lambda e, t=t, par=par, s=wsrc, tw=tw, nr=nrows: e.dma_start(
                out=wfb[:, par, 0:tw].rearrange("p (r d) -> p r d", r=nr),
                in_=s.rearrange("(r p) d -> p r d", p=128)[:, t * nr:(t + 1) * nr]),
                qs, 16)
            P.wait("vector", qs)
            P.do("vector", lambda e, t=t, par=par, tw=tw, c=acol0: e.reduce_sum(
                absa[:, c + t:c + t + 1], wfb[:, par, 0:tw], axis=AX,
                apply_absolute_value=True), s_ln, 1)
            wfb_free[par] = s_ln.n
        P.wait("vector", s_ln)
        P.do("vector", lambda e, c=acol0, n=ntile, f=fidx: e.reduce_sum(
            sc1[:, f:f + 1], absa[:, c:c + n], axis=AX), s_ln, 1)
        P.wait("tensor", s_ln)
        P.do("tensor", lambda e, f=fidx: e.matmul(
            B2a[0:1, f:f + 1], sc1[:, f:f + 1], sb_onescf[:],
            start=True, stop=True), s_wt, 1)
        P.wait("vector", s_wt)
        P.do("vector", lambda e, f=fidx: e.tensor_scalar(
            sc1[0:1, 4 + f:5 + f], B2a[0:1, f:f + 1], 1.0 / (FF * D), 1e-8,
            ALU.mult, ALU.max), s_ln, 1)
        P.wait("vector", s_ln)
        P.do("vector", lambda e, f=fidx: e.reciprocal(
            sc1[0:1, 6 + f:7 + f], sc1[0:1, 4 + f:5 + f]), s_ln, 1)
        P.wait("tensor", s_ln)
        P.do("tensor", lambda e, f=fidx: e.matmul(
            B2b[:, f:f + 1], sb_onesf[:], sc1[0:1, 6 + f:7 + f],
            start=True, stop=True), s_wt, 1)
        P.wait("vector", s_wt)
        P.do("vector", lambda e, f=fidx: e.tensor_copy(
            sinv[:, f:f + 1], B2b[:, f:f + 1]), s_ln, 1)

    wfb_free = [0, 0]
    w_abs_pass(w1, 8, 4096, 0, 0, nrows=4)
    w_abs_pass(w2, 8, 4096, 32, 1)

    def w_tern_pass(wsrc, ntile, tw, fidx, wtd, nrows=1):
        nchunk = tw // 128
        for t in range(ntile):
            par = t % 2
            q = ("sync", "scalar")[t % 2]
            qs = (s_xl, s_x2)[t % 2]
            P.wait(q, s_ln, wfb_free[par])
            P.do(q, lambda e, t=t, par=par, s=wsrc, tw=tw, nr=nrows: e.dma_start(
                out=wfb[:, par, 0:tw].rearrange("p (r d) -> p r d", r=nr),
                in_=s.rearrange("(r p) d -> p r d", p=128)[:, t * nr:(t + 1) * nr]),
                qs, 16)
            P.wait("vector", qs)
            P.wait("vector", s_wt, tern_tr[0])
            P.do("vector", lambda e, par=par, tw=tw, f=fidx: e.tensor_scalar_mul(
                ternb[:, 0:tw], wfb[:, par, 0:tw], sinv[:, f:f + 1]), s_ln, 1)
            wfb_free[par] = s_ln.n
            P.wait("vector", s_ln)
            P.do("vector", lambda e, tw=tw: e.tensor_single_scalar(
                tble[:, 0:tw], ternb[:, 0:tw], -0.5, ALU.is_le), s_ln, 1)
            P.wait("vector", s_ln)
            P.do("vector", lambda e, tw=tw: e.scalar_tensor_tensor(
                ternt[:, 0:tw], ternb[:, 0:tw], 0.5, tble[:, 0:tw],
                ALU.is_ge, ALU.subtract), s_ap, 1)
            P.wait("tensor", s_ap)
            P.wait("tensor", s_ws, stg_done[0])
            for ch in range(nchunk):
                bank = (t * ((nchunk + 7) // 8) + ch // 8) % 2
                p8 = (A4a, A4b)[bank]
                if ch % 8 == 0:
                    P.wait("tensor", s_we, trn_bank_free[bank])
                P.do("tensor", lambda e, ch=ch, p8=p8: e.matmul(
                    p8[:, (ch % 8) * 128:(ch % 8 + 1) * 128],
                    ternt[:, ch * 128:(ch + 1) * 128], sb_identb[:],
                    start=True, stop=True), s_wt if ch % 8 == 7 or ch == nchunk - 1 else None)
                if ch % 8 == 7 or ch == nchunk - 1:
                    n8 = ch % 8 + 1
                    P.wait("vector", s_wt)
                    P.do("vector", lambda e, ch=ch, n8=n8, p8=p8: e.tensor_copy(
                        wstg[:, (ch - n8 + 1) * 128:(ch + 1) * 128],
                        p8[:, 0:n8 * 128]), s_we, 1)
                    trn_bank_free[bank] = s_we.n
            tern_tr[0] = s_wt.n
            P.wait("gpsimd", s_we)
            if nrows == 1:
                P.do("gpsimd", lambda e, t=t, w=wtd, nch=nchunk: e.dma_start(
                    out=w.rearrange("c p f -> p c f")[:, :, t * 128:(t + 1) * 128],
                    in_=wstg[:, 0:nch * 128].rearrange("p (c f) -> p c f", f=128)),
                    s_ws, 16)
            else:
                for r in range(nrows):
                    P.do("gpsimd", lambda e, t=t, r=r, w=wtd, nr=nrows: e.dma_start(
                        out=w.rearrange("c p f -> p c f")[
                            :, :, (t * nr + r) * 128:(t * nr + r + 1) * 128],
                        in_=wstg[:, r * 1024:(r + 1) * 1024].rearrange(
                            "p (c k) -> p c k", k=128)), s_ws, 16)
            stg_done[0] = s_ws.n

    tern_tr, stg_done = [0], [0]
    trn_bank_free = {0: 0, 1: 0}
    w_tern_pass(w1, 8, 4096, 0, w1t_d, nrows=4)
    w_tern_pass(w2, 8, 4096, 1, w2t_d)

    P.wait("gpsimd", s_ws)
    barrier(P)
    P.emit(nc)
    if upto == "w":
        return finish(dict(wqT=wqT[:], wvT=wvT[:], sinv=sinv[:], b1s=sb_b1[:],
                           w1t=w1t_d, w2t=w2t_d))
    wtmp.close()

    ab = ExitStack()
    AE = ab.enter_context
    xbf = AE(nc.sbuf_tensor("xbf", [128, 2, 4, 1024], F32))
    xnb = AE(nc.sbuf_tensor("xnb", [128, 2, 4, 1024], BF16))
    xnT = AE(nc.sbuf_tensor("xnT", [128, 2, 8, 512], BF16))
    kst = AE(nc.sbuf_tensor("kst", [128, 2, 4, 1024], BF16))
    v1s = AE(nc.sbuf_tensor("v1s", [128, 2, 4, 1040], BF16))
    qst = AE(nc.sbuf_tensor("qst", [128, 2, 2048], BF16))

    P = Plan()

    def ln_tiles(par, nrow, ap_src):
        sb = par * 32
        P.wait("scalar", s_xl)
        for r in range(nrow):
            P.do("scalar", lambda e, r=r: e.activation(
                trash[:, r % 2, :], ap_src(r), AF.Square,
                accum_out=stats[:, sb + r * 8 + 1:sb + r * 8 + 2]))
        for r in range(nrow):
            P.do("scalar", lambda e, r=r: e.activation(
                trash[:, r % 2, :], ap_src(r), AF.Copy,
                accum_out=stats[:, sb + r * 8:sb + r * 8 + 1]),
                s_rs if r == nrow - 1 else None)
        P.wait("vector", s_rs)
        for r in range(nrow):
            P.do("vector", lambda e, r=r: e.tensor_scalar_mul(
                stats[:, sb + r * 8 + 3:sb + r * 8 + 4],
                stats[:, sb + r * 8:sb + r * 8 + 1], -1.0 / D))
            P.do("vector", lambda e, r=r: e.tensor_scalar(
                stats[:, sb + r * 8 + 4:sb + r * 8 + 5],
                stats[:, sb + r * 8:sb + r * 8 + 1],
                stats[:, sb + r * 8:sb + r * 8 + 1], 1.0 / (D * D),
                ALU.mult, ALU.mult), s_ln if r == nrow - 1 else None)
        P.wait("vector", s_ln)
        for r in range(nrow):
            P.do("vector", lambda e, r=r: e.scalar_tensor_tensor(
                stats[:, sb + r * 8 + 5:sb + r * 8 + 6],
                stats[:, sb + r * 8 + 1:sb + r * 8 + 2], 1.0 / D,
                stats[:, sb + r * 8 + 4:sb + r * 8 + 5], ALU.mult, ALU.subtract),
                s_ln if r == nrow - 1 else None)
        P.wait("scalar", s_ln)
        for r in range(nrow):
            P.do("scalar", lambda e, r=r: e.activation(
                stats[:, sb + r * 8 + 6:sb + r * 8 + 7],
                stats[:, sb + r * 8 + 5:sb + r * 8 + 6],
                AF.Ln, bias=sb_eps[:]), s_rs if r == nrow - 1 else None)
        P.wait("scalar", s_rs)
        for r in range(nrow):
            P.do("scalar", lambda e, r=r: e.activation(
                stats[:, sb + r * 8 + 7:sb + r * 8 + 8],
                stats[:, sb + r * 8 + 6:sb + r * 8 + 7],
                AF.Exp, scale=-0.5), s_rs if r == nrow - 1 else None)
        P.wait("vector", s_rs)
        for r in range(nrow):
            P.do("vector", lambda e, r=r, par=par: e.tensor_scalar(
                xnb[:, par, r, :], ap_src(r), stats[:, sb + r * 8 + 3:sb + r * 8 + 4],
                stats[:, sb + r * 8 + 7:sb + r * 8 + 8], ALU.add, ALU.mult),
                s_ap if r == nrow - 1 else None)

    def transpose_tiles(par, nrow):
        P.wait("tensor", s_pe, vpe_free[0])
        P.wait("tensor", s_pe, vpe_free[1])
        for r in range(nrow):
            ps = (A4a, A4b)[r % 2]
            if r >= 2:
                P.wait("tensor", s_te, te_free[r % 2])
            for dc in range(8):
                P.do("tensor", lambda e, r=r, dc=dc, ps=ps, par=par: e.matmul(
                    ps[:, dc * 128:(dc + 1) * 128],
                    xnb[:, par, r, dc * 128:(dc + 1) * 128], sb_identb[:],
                    start=True, stop=True), s_tr if dc == 7 else None)
            P.wait("vector", s_tr)
            P.do("vector", lambda e, r=r, par=par, ps=ps: e.tensor_copy(
                xnT[:, par, :, r * 128:(r + 1) * 128],
                ps[:].rearrange("p (c f) -> p c f", f=128)), s_te, 1)
            te_free[r % 2] = s_te.n

    def kq_proj(par, ntok, wT, dst_cb):
        for pr in range(NPAIR):
            ps = (B2a, B2b)[pr % 2]
            P.wait("tensor", s_pe, pe_free[pr % 2])
            for dc in range(8):
                P.do("tensor", lambda e, pr=pr, dc=dc, ps=ps, par=par, wT=wT, n=ntok:
                     e.matmul(ps[:, 0:n], wT[:, dc, pr * 128:(pr + 1) * 128],
                              xnT[:, par, dc, 0:n], start=(dc == 0), stop=(dc == 7)),
                     s_pj if dc == 7 else None)
            P.wait("vector", s_pj)
            dst_cb(pr, ps)
            pe_free[pr % 2] = s_pe.n

    def v_proj(par, ntsub):
        for tsub in range(ntsub):
            ps = (A4a, A4b)[tsub % 2]
            P.wait("tensor", s_pe, vpe_free[tsub % 2])
            P.wait("tensor", s_te, te_free[tsub % 2])
            for dc in range(8):
                for nh in range(2):
                    P.do("tensor", lambda e, dc=dc, nh=nh, tsub=tsub, ps=ps, par=par:
                         e.matmul(ps[:, nh * 512:(nh + 1) * 512],
                                  xnT[:, par, dc, tsub * 128:(tsub + 1) * 128],
                                  wvT[:, dc, nh * 512:(nh + 1) * 512],
                                  start=(dc == 0), stop=(dc == 7)),
                         s_pj if (dc == 7 and nh == 1) else None)
            P.wait("vector", s_pj)
            P.do("vector", lambda e, tsub=tsub, par=par, ps=ps: e.tensor_copy(
                v1s[:, par, tsub, :].rearrange("p (h c) -> p h c", c=65)[:, :, 0:64],
                ps[:].rearrange("p (h c) -> p h c", c=64)), s_pe, 1)
            P.do("vector", lambda e, tsub=tsub, par=par: e.memset(
                v1s[:, par, tsub, :].rearrange("p (h c) -> p h c", c=65)[:, :, 64:65],
                1.0), s_pe, 1)
            vpe_free[tsub % 2] = s_pe.n

    te_free, pe_free, vpe_free = [0, 0], [0, 0], [0, 0]
    xbf_free = [0, 0]
    xnb_free = [0, 0]
    ss_done = [0, 0]
    ap_done = {}

    chunks = ([("a", j) for j in range(NSLOT)] + [("b", t) for t in range(16)])

    def load_chunk(i):
        kind, idx = chunks[i]
        par = i % 2
        P.wait("sync", s_ap, xbf_free[par])
        if kind == "a":
            P.do("sync", lambda e, idx=idx, par=par: e.dma_start(
                out=xbf[:, par, 0:2, :],
                in_=xq[idx * 256:(idx + 1) * 256, :].rearrange(
                    "(r p) d -> p r d", p=128)), s_xl, 16)
        else:
            P.do("sync", lambda e, idx=idx, par=par: e.dma_start(
                out=xbf[:, par, :, :],
                in_=xb[idx * 512:(idx + 1) * 512, :].rearrange(
                    "(r p) d -> p r d", p=128)), s_xl, 16)

    def ln_chunk(i):
        kind, idx = chunks[i]
        par = i % 2
        nrow = 2 if kind == "a" else 4
        P.wait("vector", s_xl)
        P.wait("vector", s_tr, xnb_free[par])
        ln_tiles(par, nrow, lambda r, par=par: xbf[:, par, r, :])
        xbf_free[par] = s_ap.n
        ap_done[i] = s_ap.n

    def work_chunk(i):
        kind, idx = chunks[i]
        par = i % 2
        nrow = 2 if kind == "a" else 4
        P.wait("tensor", s_ap, ap_done[i])
        transpose_tiles(par, nrow)
        xnb_free[par] = s_tr.n
        P.wait("tensor", s_te)
        P.wait("vector", s_ss, ss_done[par])

        if kind == "a":
            def q_evac(pr, ps, par=par):
                P.do("vector", lambda e, pr=pr, ps=ps: e.tensor_copy(
                    qst[:, par, pr * 256:(pr + 1) * 256], ps[:, 0:256]), s_pe, 1)

            def k_evac(pr, ps, par=par):
                P.do("vector", lambda e, pr=pr, ps=ps: e.tensor_copy(
                    kst[:, par, 0:2, pr * 128:(pr + 1) * 128],
                    ps[:, 0:256].rearrange("p (t k) -> p t k", k=128)), s_pe, 1)

            kq_proj(par, 256, wqT, q_evac)
            kq_proj(par, 256, wkT, k_evac)
            v_proj(par, 2)
            P.wait("gpsimd", s_pe)
            P.do("gpsimd", lambda e, idx=idx, par=par: e.dma_start(
                out=qt_d[idx], in_=qst[:, par, :]), s_ss, 16)
            P.wait("gpsimd", s_te)
            P.do("gpsimd", lambda e, idx=idx, par=par: e.dma_start(
                out=xnt_d[idx].rearrange("p (c f) -> p c f", f=256),
                in_=xnT[:, par, :, 0:256]), s_ss, 16)
            P.do("gpsimd", lambda e, idx=idx, par=par: e.dma_start(
                out=kto_d[2 * idx:2 * idx + 2].rearrange("t p c -> p t c"),
                in_=kst[:, par, 0:2, :]), s_ss, 16)
            P.do("gpsimd", lambda e, idx=idx, par=par: e.dma_start(
                out=v1o_d[2 * idx:2 * idx + 2].rearrange("t p c -> p t c"),
                in_=v1s[:, par, 0:2, :]), s_ss, 16)
        else:
            def k_evac_b(pr, ps, par=par):
                P.do("vector", lambda e, pr=pr, ps=ps: e.tensor_copy(
                    kst[:, par, :, pr * 128:(pr + 1) * 128],
                    ps[:, 0:512].rearrange("p (t k) -> p t k", k=128)), s_pe, 1)

            kq_proj(par, 512, wkT, k_evac_b)
            v_proj(par, 4)
            P.wait("gpsimd", s_pe)
            P.do("gpsimd", lambda e, idx=idx, par=par: e.dma_start(
                out=kt_d[4 * idx:4 * idx + 4].rearrange("t p c -> p t c"),
                in_=kst[:, par, :, :]), s_ss, 16)
            P.do("gpsimd", lambda e, idx=idx, par=par: e.dma_start(
                out=v1_d[4 * idx:4 * idx + 4].rearrange("t p c -> p t c"),
                in_=v1s[:, par, :, :]), s_ss, 16)
        ss_done[par] = s_ss.n

    for i in range(len(chunks) + 1):
        if i < len(chunks):
            load_chunk(i)
            ln_chunk(i)
        if i >= 1:
            work_chunk(i - 1)

    P.wait("gpsimd", s_ss)
    barrier(P)
    P.emit(nc)
    if upto == "ab":
        return finish(dict(kt=kt_d, v1=v1_d, kto=kto_d, v1o=v1o_d, qt=qt_d,
                           xnt=xnt_d))
    ab.close()
    wab.close()

    dph = ExitStack()
    DP = dph.enter_context
    amidT = DP(nc.sbuf_tensor("amidT", [128, 2, 8, 256], BF16))
    ktm = DP(nc.sbuf_tensor("ktm", [128, 2, 8, 1024], BF16))
    v1m = DP(nc.sbuf_tensor("v1m", [128, 2, 8, 1040], BF16))
    qts = DP(nc.sbuf_tensor("qts", [128, 2048], BF16))
    ptb = DP(nc.sbuf_tensor("ptb", [128, 3, 1024], BF16))
    acc = DP(nc.sbuf_tensor("acc", [65, 4096], F32))
    rcp = DP(nc.sbuf_tensor("rcp", [1, 16, 256], F32))
    w1s = DP(nc.sbuf_tensor("w1s", [128, 8, 512], BF16))
    w2s = DP(nc.sbuf_tensor("w2s", [128, 32, 512], BF16))
    xnte = DP(nc.sbuf_tensor("xnte", [128, 8, 256], BF16))
    hts = DP(nc.sbuf_tensor("hts", [128, 32, 256], BF16))
    xqe = DP(nc.sbuf_tensor("xqe", [128, 2, 512], F32))
    yb = DP(nc.sbuf_tensor("yb", [128, 2, 512], F32))

    P = Plan()
    PBCOL = [sum(GJ[:k]) for k in range(NSLOT)]
    LOOK = 2
    sg_free = [0, 0]
    pt_free = [0, 0, 0]
    pv_free = [0, 0]
    km_free = [[0, 0], [0, 0]]
    qt_free = [0, 0]
    am_free = [0, 0]
    rp_all = [0]
    am_rd = [0, 0]
    acc_free = [0]
    pending_norm = [None]
    pending_e = [None]
    uidx = 0
    xnte_free = [0, 0]
    xq_free = [0]
    yb_free = [0]
    ht_free = {}
    w1_free = [0, 0]
    w2_free = [0, 0]
    a4c_norm_done = [0]

    def make_epieces(jj):
        epar = jj % 2
        pieces = []
        st = {}

        def p_loads():
            P.wait("sync", s_f1, xnte_free[0])
            P.do("sync", lambda e, jj=jj: e.dma_start(
                out=xnte[:],
                in_=xnt_d[jj].rearrange("p (c f) -> p c f", f=256)), s_el, 16)
            st["xnte"] = s_el.n
            P.wait("sync", s_yv, xq_free[0])
            P.do("sync", lambda e, jj=jj: e.dma_start(
                out=xqe[:],
                in_=xq[jj * 256:(jj + 1) * 256, 0:512].rearrange(
                    "(r p) d -> p r d", p=128)), s_el, 16)
            st["xqe"] = s_el.n
        pieces.append(p_loads)

        gl_tgt = {}
        for c in range(8):
            def p_ffn1(c=c):
                fc0 = c * 4
                P.wait("sync", s_f1, w1_free[0])
                P.do("sync", lambda e, fc0=fc0: e.dma_start(
                    out=w1s[:],
                    in_=w1t_d.rearrange("c p f -> p c f")[:, :, fc0 * 128:(fc0 + 4) * 128]),
                    s_el, 16)
                ld = s_el.n
                P.wait("tensor", s_el, ld)
                P.wait("tensor", s_el, st["xnte"])
                if c == 0:
                    P.wait("tensor", s_am, a4c_norm_done[0])
                for q in range(4):
                    fc = fc0 + q
                    hb = (fc % 2) * 512
                    P.wait("tensor", s_gl, gl_tgt.get(fc - 2, 0))
                    for dc in range(8):
                        P.do("tensor", lambda e, fc=fc, hb=hb, dc=dc,
                             epar=epar, q=q: e.matmul(
                                 A4c[:, hb:hb + 256],
                                 w1s[:, dc, q * 128:(q + 1) * 128],
                                 xnte[:, dc, :],
                                 start=(dc == 0), stop=(dc == 7)),
                             s_f1 if dc == 7 else None)
                    P.wait("scalar", s_f1)
                    P.wait("scalar", s_f2, ht_free.get(fc, 0))
                    P.do("scalar", lambda e, fc=fc, hb=hb: e.activation(
                        hts[:, fc, :], A4c[:, hb:hb + 256], AF.Gelu,
                        bias=sb_b1[:, fc:fc + 1]), s_gl, 1)
                    gl_tgt[fc] = s_gl.n
                w1_free[0] = s_f1.n
            pieces.append(p_ffn1)

        def p_gl_all():
            P.do("scalar", lambda e: e.copy(stats[0:1, 28:29], stats[0:1, 28:29]),
                 s_gl, 1)
            st["gl_all"] = s_gl.n
        pieces.append(p_gl_all)

        def p_w2load(h):
            def fn():
                P.wait("sync", s_f2, w2_free[0])
                P.do("sync", lambda e, h=h: e.dma_start(
                    out=w2s[:],
                    in_=w2t_d[:, :, h * 512:(h + 1) * 512].rearrange(
                        "c p f -> p c f")), s_el, 16)
                st["w2"] = s_el.n
            return fn
        pieces.append(p_w2load(0))

        for h in range(2):
            def p_pass(h=h):
                P.wait("tensor", s_el, st["w2"])
                P.wait("tensor", s_gl, st["gl_all"])
                P.wait("tensor", s_am, a4c_norm_done[0])
                if h == 1:
                    P.wait("tensor", s_yv, st["yv0"])
                for t in range(2):
                    P.do("tensor", lambda e, t=t, h=h: e.matmul(
                        A4c[:, t * 512:(t + 1) * 512], sb_onesf[0:1, :],
                        sb_b2[0:1, h * 512:(h + 1) * 512], start=True, stop=False))
                for fc in range(32):
                    for t in range(2):
                        P.do("tensor", lambda e, fc=fc, t=t: e.matmul(
                            A4c[:, t * 512:(t + 1) * 512],
                            hts[:, fc, t * 128:(t + 1) * 128],
                            w2s[:, fc, :], start=False, stop=False))
                for pr in range(NPAIR):
                    for t in range(2):
                        P.do("tensor", lambda e, pr=pr, t=t, epar=epar:
                             e.matmul(A4c[:, t * 512:(t + 1) * 512],
                                      amidT[:, epar, pr, t * 128:(t + 1) * 128],
                                      woT[:, pr, h * 512:(h + 1) * 512],
                                      start=False, stop=(pr == NPAIR - 1)),
                             s_f2 if (pr == NPAIR - 1 and t == 1) else None)
                w2_free[0] = s_f2.n
                if h == 1:
                    am_rd[epar] = s_f2.n
                    for fc in range(32):
                        ht_free[fc] = s_f2.n
            pieces.append(p_pass)

            def p_resid(h=h, jj=jj):
                P.wait("vector", s_f2)
                P.wait("vector", s_el, st["xqe"])
                P.wait("vector", s_ys, yb_free[0])
                for t in range(2):
                    P.do("vector", lambda e, t=t: e.tensor_add(
                        yb[:, t, :], A4c[:, t * 512:(t + 1) * 512],
                        xqe[:, t, :]), s_yv, 1)
                if h == 0:
                    st["yv0"] = s_yv.n
                else:
                    xq_free[0] = s_yv.n
                P.wait("gpsimd", s_yv)
                P.do("gpsimd", lambda e, jj=jj, h=h: e.dma_start(
                    out=y[jj * 256:(jj + 1) * 256,
                          h * 512:(h + 1) * 512].rearrange(
                        "(r p) d -> p r d", p=128),
                    in_=yb[:]), s_ys, 16)
                yb_free[0] = s_ys.n
            pieces.append(p_resid)
            if h == 0:
                def p_xq1(jj=jj):
                    P.wait("sync", s_yv, st["yv0"])
                    P.do("sync", lambda e, jj=jj: e.dma_start(
                        out=xqe[:],
                        in_=xq[jj * 256:(jj + 1) * 256, 512:1024].rearrange(
                            "(r p) d -> p r d", p=128)), s_el, 16)
                    st["xqe"] = s_el.n
                pieces.append(p_xq1)
                pieces.append(p_w2load(1))

        def p_close():
            xnte_free[0] = s_f1.n
        pieces.append(p_close)
        return pieces

    for j in reversed(range(NSLOT)):
        spar = j % 2
        P.wait("sync", s_qk, qt_free[0])
        P.do("sync", lambda e, j=j: e.dma_start(
            out=qts[:], in_=qt_d[j]), s_kl, 16)
        groups = GJ[j]
        macros = [list(range(m, min(m + MACRO, groups)))
                  for m in range(0, groups, MACRO)]
        macros.append(None)

        units = []
        for mi, mg in enumerate(macros):
            diag = mg is None
            ng = 1 if diag else len(mg)
            for pr in range(NPAIR):
                for gi in range(ng):
                    units.append(dict(mi=mi, mg=mg, diag=diag, ng=ng, pr=pr,
                                      gi=gi, mpar=mi % 2))
        nun = len(units)
        kl_tgt = {}

        def emit_macro_load(mi, j=j, macros=macros):
            mg = macros[mi]
            mpar = mi % 2
            P.wait("sync", s_qk, km_free[mpar][0])
            P.wait("sync", s_pv, km_free[mpar][1])
            if mg is None:
                P.do("sync", lambda e, j=j, mpar=mpar: e.dma_start(
                    out=ktm[:, mpar, 0:2, :],
                    in_=kto_d[2 * j:2 * j + 2].rearrange("t p c -> p t c")), s_kl, 16)
                P.do("sync", lambda e, j=j, mpar=mpar: e.dma_start(
                    out=v1m[:, mpar, 0:2, :],
                    in_=v1o_d[2 * j:2 * j + 2].rearrange("t p c -> p t c")), s_kl, 16)
            else:
                k0, nb = 2 * mg[0], 2 * len(mg)
                P.do("sync", lambda e, k0=k0, nb=nb, mpar=mpar: e.dma_start(
                    out=ktm[:, mpar, 0:nb, :],
                    in_=kt_d[k0:k0 + nb].rearrange("t p c -> p t c")), s_kl, 16)
                P.do("sync", lambda e, k0=k0, nb=nb, mpar=mpar: e.dma_start(
                    out=v1m[:, mpar, 0:nb, :],
                    in_=v1_d[k0:k0 + nb].rearrange("t p c -> p t c")), s_kl, 16)
            kl_tgt[mi] = s_kl.n

        qk_done = {}
        ex_done = {}
        mk_done = {}
        last_qk_of_macro = {}
        epieces = pending_e[0] or []
        pending_e[0] = None
        cadence = max(1, (nun - 4) // max(1, len(epieces))) if epieces else 0
        ep_i = 0

        for step in range(nun + LOOK):
            if step == 2 and pending_norm[0] is not None:
                pending_norm[0]()
                pending_norm[0] = None
            if (epieces and step >= 3 and ep_i < len(epieces)
                    and (step - 3) % cadence == 0):
                epieces[ep_i]()
                ep_i += 1
            if step == 0:
                emit_macro_load(0)
            if step < nun:
                u = units[step]
                mi, pr, gi, mpar, diag = u["mi"], u["pr"], u["gi"], u["mpar"], u["diag"]
                if pr == 0 and gi == 0 and mi + 1 < len(macros):
                    emit_macro_load(mi + 1)
                sg_i = uidx % 2
                ppar = uidx % 3
                u["sg_i"], u["ppar"] = sg_i, ppar
                uidx += 1
                sgp = (A4a, A4b)[sg_i]
                u["sgp"] = sgp
                if pr == 0 and gi == 0:
                    P.wait("tensor", s_kl, kl_tgt[mi])
                P.wait("tensor", s_ex, sg_free[sg_i])
                for (kb, hp, off) in ((0, 0, 0), (1, 0, 256), (0, 1, 512), (1, 1, 768)):
                    first = off % 512 == 0
                    tp = (hp * 64, 0)
                    P.do("tensor", lambda e, kb=kb, hp=hp, off=off, tp=tp, gi=gi,
                         pr=pr, sgp=sgp, mpar=mpar, spar=spar, first=first:
                         e.matmul(sgp[:, off:off + 256],
                                  ktm[hp * 64:(hp + 1) * 64, mpar, 2 * gi + kb,
                                      pr * 128:(pr + 1) * 128],
                                  qts[hp * 64:(hp + 1) * 64,
                                      pr * 256:(pr + 1) * 256],
                                  start=first, stop=not first, tile_position=tp),
                         s_qk if off == 768 else None)
                qk_done[step] = s_qk.n
                last_qk_of_macro[mi] = s_qk.n
                col = 136 if diag else PBCOL[j] + u["mg"][gi]
                P.wait("scalar", s_qk, qk_done[step])
                P.wait("scalar", s_pv, pt_free[ppar])
                P.do("scalar", lambda e, ppar=ppar, sgp=sgp, col=col: e.activation(
                    ptb[:, ppar, :], sgp[:], AF.Exp,
                    bias=sb_padbias[:, col:col + 1], scale=0.125), s_ex, 1)
                ex_done[step] = s_ex.n
                sg_free[sg_i] = s_ex.n
                if diag:
                    P.wait("vector", s_ex, ex_done[step])
                    P.do("vector", lambda e, ppar=ppar: e.tensor_mul(
                        ptb[:, ppar, :], ptb[:, ppar, :], sb_maskdd[:]), s_mk, 1)
                    P.do("vector", lambda e: e.tensor_copy(
                        stats[0:1, 29:30], stats[0:1, 29:30]), s_mk, 1)
                    mk_done[step] = s_mk.n
            if step >= LOOK:
                vstep = step - LOOK
                v = units[vstep]
                mi, pr, gi, mpar, diag = v["mi"], v["pr"], v["gi"], v["mpar"], v["diag"]
                pvp = (B2a, B2b)[pr % 2]
                ppar = v["ppar"]
                if gi == 0:
                    P.wait("tensor", s_ac, pv_free[pr % 2])
                if diag:
                    P.wait("tensor", s_mk, mk_done[vstep])
                else:
                    P.wait("tensor", s_ex, ex_done[vstep] + 1)
                for (kb, hp, poff) in ((0, 0, 0), (1, 0, 0), (0, 1, 256), (1, 1, 256)):
                    first = gi == 0 and kb == 0 and hp == 0
                    P.do("tensor", lambda e, kb=kb, hp=hp, poff=poff, gi=gi, pr=pr,
                         pvp=pvp, mpar=mpar, ppar=ppar, first=first:
                         e.matmul(pvp[0:65, poff:poff + 256],
                                  v1m[:, mpar, 2 * gi + kb,
                                      (2 * pr + hp) * 65:(2 * pr + hp + 1) * 65],
                                  ptb[:, ppar, hp * 512 + kb * 256:
                                      hp * 512 + kb * 256 + 256],
                                  start=first, stop=False),
                         s_pv if (kb == 1 and hp == 1) else None)
                pt_free[ppar] = s_pv.n
                if gi == v["ng"] - 1:
                    P.wait("vector", s_pv)
                    if mi == 0:
                        P.wait("vector", s_am, acc_free[0])
                        P.do("vector", lambda e, pr=pr, pvp=pvp: e.tensor_copy(
                            acc[:, pr * 512:(pr + 1) * 512], pvp[0:65, :]), s_ac, 1)
                    else:
                        P.do("vector", lambda e, pr=pr, pvp=pvp: e.tensor_add(
                            acc[:, pr * 512:(pr + 1) * 512],
                            acc[:, pr * 512:(pr + 1) * 512], pvp[0:65, :]), s_ac, 1)
                    pv_free[pr % 2] = s_ac.n
                    if diag:
                        if pr == 0:
                            P.wait("vector", s_rp, rp_all[0])
                        P.do("vector", lambda e, pr=pr: e.reciprocal(
                            rcp[0:1, 2 * pr:2 * pr + 2, :].rearrange(
                                "a h c -> a (h c)"),
                            acc[64:65, pr * 512:(pr + 1) * 512]), s_nm, 1)
                    if pr == NPAIR - 1:
                        km_free[mpar] = [last_qk_of_macro[mi], s_pv.n]
        while epieces and ep_i < len(epieces):
            epieces[ep_i]()
            ep_i += 1
        P.do("scalar", lambda e: e.copy(stats[0:1, 28:29], stats[0:1, 28:29]),
             s_ex, 1)
        qt_free[0] = s_qk.n
        P.do("vector", lambda e: e.tensor_copy(
            stats[0:1, 29:30], stats[0:1, 29:30]), s_nm, 1)
        nm_tgt = s_nm.n

        def make_norm(j=j, spar=spar, nm_tgt=nm_tgt):
            def emit_norm():
                P.wait("tensor", s_nm, nm_tgt)
                P.wait("tensor", s_gl)
                P.wait("tensor", s_yv)
                P.wait("vector", s_f2, am_rd[spar])
                rp_tgt = {}
                for hp2 in range(H // 2 + 1):
                    if hp2 < H // 2:
                        P.wait("tensor", s_am, am_free[0])
                        P.do("tensor", lambda e, hp2=hp2: e.matmul(
                            A4c[0:64, 0:512] if hp2 % 2 == 0 else A4c[0:64, 512:1024],
                            sb_onesf[0:1, 0:64],
                            rcp[0:1, 2 * hp2, :].rearrange("a c -> a c") if False else
                            rcp[0:1, :, :].rearrange("a h c -> a (h c)")[
                                :, hp2 * 512:(hp2 + 1) * 512],
                            start=True, stop=True), s_rp, 1)
                        rp_tgt[hp2] = s_rp.n
                    if hp2 >= 1:
                        k = hp2 - 1
                        P.wait("vector", s_rp, rp_tgt[k])
                        for e2 in range(2):
                            hh = 2 * k + e2
                            P.do("vector", lambda e, hh=hh, e2=e2, k=k, spar=spar:
                                 e.tensor_mul(
                                amidT[(hh % 2) * 64:(hh % 2) * 64 + 64, spar,
                                      hh // 2, :],
                                acc[0:64, hh * 256:(hh + 1) * 256],
                                (A4c[0:64, 0:512] if k % 2 == 0 else
                                 A4c[0:64, 512:1024])[:, e2 * 256:(e2 + 1) * 256]),
                                s_am, 1)
                        am_free[0] = s_am.n
                P.do("vector", lambda e: e.tensor_copy(
                    stats[0:1, 29:30], stats[0:1, 29:30]), s_am, 1)
                acc_free[0] = s_am.n
                a4c_norm_done[0] = s_am.n
                rp_all[0] = s_rp.n
            return emit_norm

        pending_norm[0] = make_norm()
        pending_e[0] = make_epieces(j)

    if pending_norm[0] is not None:
        pending_norm[0]()
        pending_norm[0] = None
    for pc in (pending_e[0] or []):
        pc()
    pending_e[0] = None
    P.wait("gpsimd", s_ys)
    barrier(P)
    P.emit(nc)
    if upto == "d":
        return finish(dict(acc=acc[:], rcp=rcp[:]))
    dph.close()
    outer.close()
    return nc



def _consts():
    identb = np.eye(128, dtype=ml_dtypes.bfloat16)
    identf = np.eye(128, dtype=np.float32)
    onesf = np.ones((1, 128), dtype=np.float32)
    onescf = np.ones((128, 1), dtype=np.float32)
    tri = np.triu(np.ones((128, 128), np.float32))
    ones = np.ones((128, 128), np.float32)
    zero = np.zeros((128, 128), np.float32)
    m1 = np.concatenate([tri, ones], axis=1)
    m0 = np.concatenate([zero, tri], axis=1)
    maskdd = np.concatenate([m1, m0, m1, m0], axis=1).astype(ml_dtypes.bfloat16)
    return identb, identf, onesf, onescf, maskdd


def _padbias(cc):
    pb = np.full((128, 137), KILL, np.float32)
    col = 0
    for j in range(NSLOT):
        T = tile_T(j, cc)
        pb[:, col:col + T] = -SHIFT
        col += GJ[j]
    pb[:, 136] = -SHIFT
    return pb


def kernel(L0_attn_in_features, L0_attn_Wq, L0_attn_Wk, L0_attn_Wv, L0_attn_Wo,
           L0_ffn_W1, L0_ffn_b1, L0_ffn_W2, L0_ffn_b2):
    x = np.ascontiguousarray(np.asarray(L0_attn_in_features, dtype=np.float32))
    identb, identf, onesf, onescf, maskdd = _consts()
    in_maps = []
    for c in range(8):
        b, cc = c // 4, c % 4
        rows = np.concatenate([np.arange(256 * tile_T(j, cc),
                                         256 * tile_T(j, cc) + 256)
                               for j in range(NSLOT)])
        in_maps.append(dict(
            xq=np.ascontiguousarray(x[b][rows]),
            xb=np.ascontiguousarray(x[b]),
            wq=np.ascontiguousarray(np.asarray(L0_attn_Wq, np.float32)),
            wk=np.ascontiguousarray(np.asarray(L0_attn_Wk, np.float32)),
            wv=np.ascontiguousarray(np.asarray(L0_attn_Wv, np.float32)),
            wo=np.ascontiguousarray(np.asarray(L0_attn_Wo, np.float32)),
            w1=np.ascontiguousarray(np.asarray(L0_ffn_W1, np.float32)),
            b1=np.ascontiguousarray(np.asarray(L0_ffn_b1, np.float32)),
            w2=np.ascontiguousarray(np.asarray(L0_ffn_W2, np.float32)),
            b2=np.ascontiguousarray(np.asarray(L0_ffn_b2, np.float32)),
            identb=identb, identf=identf, onesf=onesf, onescf=onescf,
            maskdd=maskdd, padbias=_padbias(cc),
        ))
    nc = build_nc()
    res = run_bass_kernel_spmd(nc, in_maps, core_ids=list(range(8)))
    out = np.empty((B, S, D), np.float32)
    for c in range(8):
        b, cc = c // 4, c % 4
        yc = res.results[c]["y"]
        for j in range(NSLOT):
            T = tile_T(j, cc)
            out[b, 256 * T:256 * T + 256] = yc[256 * j:256 * j + 256]
    return out


if __name__ == "__main__":
    nc = build_nc()
    print("built ok,", nc.n_instructions(), "instructions")


# version 19
# speedup vs baseline: 1.0267x; 1.0267x over previous
import numpy as np
import ml_dtypes
from contextlib import ExitStack
import concourse.bass as bass
import concourse.mybir as mybir
from concourse.bass_utils import run_bass_kernel_spmd

F32 = mybir.dt.float32
BF16 = mybir.dt.bfloat16
AF = mybir.ActivationFunctionType
ALU = mybir.AluOpType
AX = mybir.AxisListType.X

B, S, D, H, DH, FF = 2, 8192, 1024, 16, 64, 4096
NPAIR, NSLOT, QT, KBLK = 8, 8, 256, 64
KJ = [64 - 8 * j for j in range(NSLOT)]
GJ = [(k - 2) // 2 for k in KJ]
MACRO = 4
SHIFT = 4.0
KILL = -1e15
ENGS = ("tensor", "vector", "scalar", "sync", "gpsimd")


def tile_T(j, cc):
    return 31 - 4 * j - ((cc + j) % 4)


class Sem:
    def __init__(self, stack, nc, name):
        self.h = stack.enter_context(nc.semaphore(name))
        self.n = 0


_PLAN_LOGS = []


class Plan:
    def __init__(self):
        self.q = {k: [] for k in ENGS}
        self.log = {k: [] for k in ENGS}

    def wait(self, eng, sem, tgt=None):
        v = sem.n if tgt is None else tgt
        if v and v > 0:
            self.log[eng].append(("w", sem.h.name if hasattr(sem.h, "name") else id(sem), v))
            self.q[eng].append(lambda e, h=sem.h, v=v: e.wait_ge(h, v))

    def do(self, eng, builder, sem=None, k=1):
        if sem is not None:
            sem.n += k
            self.log[eng].append(("i", sem.h.name if hasattr(sem.h, "name") else id(sem), k))
            self.q[eng].append(lambda e, b=builder, h=sem.h, kk=k: b(e).then_inc(h, kk))
        else:
            self.log[eng].append(("i", None, 0))
            self.q[eng].append(lambda e, b=builder: b(e))

    def emit(self, nc):
        _PLAN_LOGS.append(self.log)
        with nc.Block() as block:
            for name in ENGS:
                fns = self.q[name]
                getattr(block, name)(lambda e, fns=fns: [fn(e) for fn in fns])


def build_nc(upto=None, taps=()):
    nc = bass.Bass("TRN2", target_bir_lowering=False)

    xq = nc.dram_tensor("xq", [NSLOT * QT, D], F32, kind="ExternalInput").ap()
    xb = nc.dram_tensor("xb", [S, D], F32, kind="ExternalInput").ap()
    wq = nc.dram_tensor("wq", [D, D], F32, kind="ExternalInput").ap()
    wk = nc.dram_tensor("wk", [D, D], F32, kind="ExternalInput").ap()
    wv = nc.dram_tensor("wv", [D, D], F32, kind="ExternalInput").ap()
    wo = nc.dram_tensor("wo", [D, D], F32, kind="ExternalInput").ap()
    w1 = nc.dram_tensor("w1", [FF, D], F32, kind="ExternalInput").ap()
    b1 = nc.dram_tensor("b1", [FF], F32, kind="ExternalInput").ap()
    w2 = nc.dram_tensor("w2", [D, FF], F32, kind="ExternalInput").ap()
    b2 = nc.dram_tensor("b2", [D], F32, kind="ExternalInput").ap()
    identb = nc.dram_tensor("identb", [128, 128], BF16, kind="ExternalInput").ap()
    identf = nc.dram_tensor("identf", [128, 128], F32, kind="ExternalInput").ap()
    onesf = nc.dram_tensor("onesf", [1, 128], F32, kind="ExternalInput").ap()
    onescf = nc.dram_tensor("onescf", [128, 1], F32, kind="ExternalInput").ap()
    maskdd = nc.dram_tensor("maskdd", [128, 1024], BF16, kind="ExternalInput").ap()
    padbias = nc.dram_tensor("padbias", [128, 137], F32, kind="ExternalInput").ap()
    y = nc.dram_tensor("y", [NSLOT * QT, D], F32, kind="ExternalOutput").ap()

    kt_d = nc.dram_tensor("kt_d", [KBLK, 128, 1024], BF16).ap()
    v1_d = nc.dram_tensor("v1_d", [KBLK, 128, 1040], BF16).ap()
    kto_d = nc.dram_tensor("kto_d", [16, 128, 1024], BF16).ap()
    v1o_d = nc.dram_tensor("v1o_d", [16, 128, 1040], BF16).ap()
    qt_d = nc.dram_tensor("qt_d", [NSLOT, 128, 2048], BF16).ap()
    xnt_d = nc.dram_tensor("xnt_d", [NSLOT, 128, 2048], BF16).ap()
    w1t_d = nc.dram_tensor("w1t_d", [8, 128, FF], BF16).ap()
    w2t_d = nc.dram_tensor("w2t_d", [32, 128, D], BF16).ap()

    tap_outs = {}
    for tname, tshape, tdt in taps:
        tap_outs[tname] = nc.dram_tensor("tap_" + tname, tshape, tdt,
                                         kind="ExternalOutput").ap()

    outer = ExitStack()
    OE = outer.enter_context

    def sem(name):
        return Sem(outer, nc, name)

    s_cl, s_wl, s_wt, s_we, s_ws = [sem(n) for n in ("s_cl", "s_wl", "s_wt", "s_we", "s_ws")]
    s_xl, s_ln, s_rs, s_ap, s_tr = [sem(n) for n in ("s_xl", "s_ln", "s_rs", "s_ap", "s_tr")]
    s_x2 = sem("s_x2")
    s_te, s_pj, s_pe, s_ss = [sem(n) for n in ("s_te", "s_pj", "s_pe", "s_ss")]
    s_kl, s_qk, s_ex, s_mk, s_pv = [sem(n) for n in ("s_kl", "s_qk", "s_ex", "s_mk", "s_pv")]
    s_ac, s_nm, s_rp, s_am = [sem(n) for n in ("s_ac", "s_nm", "s_rp", "s_am")]
    s_el, s_f1, s_gl, s_f2, s_yv, s_ys = [sem(n) for n in
                                          ("s_el", "s_f1", "s_gl", "s_f2", "s_yv", "s_ys")]
    s_ph = sem("s_ph")

    def barrier(P):
        for _ in range(2):
            for eng in ENGS:
                s_ph.n += 1
                P.log[eng].append(("i", "s_ph", 1))
                P.q[eng].append(lambda e, h=s_ph.h: e.nop().then_inc(h, 1))
            tgt = s_ph.n
            for eng in ENGS:
                P.log[eng].append(("w", "s_ph", tgt))
                P.q[eng].append(lambda e, h=s_ph.h, v=tgt: e.wait_ge(h, v))

    def finish(sources):
        Pt = Plan()
        s_tp = sem("s_tp")
        for name, src in sources.items():
            if name in tap_outs:
                Pt.do("gpsimd", lambda e, d=tap_outs[name], s=src: e.dma_start(
                    out=d, in_=s), s_tp, 16)
        Pt.q["gpsimd"].append(lambda e, h=s_tp.h, v=s_tp.n: e.wait_ge(h, v) if v else None)
        Pt.emit(nc)
        return nc

    woT = OE(nc.sbuf_tensor("woT", [128, 8, 1024], BF16))
    sb_identb = OE(nc.sbuf_tensor("sb_identb", [128, 128], BF16))
    sb_identf = OE(nc.sbuf_tensor("sb_identf", [128, 128], F32))
    sb_onesf = OE(nc.sbuf_tensor("sb_onesf", [1, 128], F32))
    sb_onescf = OE(nc.sbuf_tensor("sb_onescf", [128, 1], F32))
    sb_maskdd = OE(nc.sbuf_tensor("sb_maskdd", [128, 1024], BF16))
    sb_padbias = OE(nc.sbuf_tensor("sb_padbias", [128, 137], F32))
    sb_b1 = OE(nc.sbuf_tensor("sb_b1", [128, 32], F32))
    sb_b2 = OE(nc.sbuf_tensor("sb_b2", [1, 1024], F32))
    sb_eps = OE(nc.sbuf_tensor("sb_eps", [128, 1], F32))
    stats = OE(nc.sbuf_tensor("stats", [128, 64], F32))

    A4a = OE(nc.psum_tensor("A4a", [128, 1024], F32))
    A4b = OE(nc.psum_tensor("A4b", [128, 1024], F32))
    A4c = OE(nc.psum_tensor("A4c", [128, 1024], F32))
    B2a = OE(nc.psum_tensor("B2a", [128, 512], F32))
    B2b = OE(nc.psum_tensor("B2b", [128, 512], F32))

    wab = ExitStack()
    WE = wab.enter_context
    trash = WE(nc.sbuf_tensor("trash", [128, 2, 1024], F32))
    wqT = WE(nc.sbuf_tensor("wqT", [128, 8, 1024], BF16))
    wkT = WE(nc.sbuf_tensor("wkT", [128, 8, 1024], BF16))
    wvT = WE(nc.sbuf_tensor("wvT", [128, 8, 1024], BF16))

    wtmp = ExitStack()
    WT = wtmp.enter_context
    wnat = WT(nc.sbuf_tensor("wnat", [128, 8, 1024], BF16))
    wfb = WT(nc.sbuf_tensor("wfb", [128, 2, 4096], F32))
    absa = WT(nc.sbuf_tensor("absa", [128, 40], F32))
    sc1 = WT(nc.sbuf_tensor("sc1", [128, 8], F32))
    sinv = WT(nc.sbuf_tensor("sinv", [128, 2], F32))
    ternb = WT(nc.sbuf_tensor("ternb", [128, 4096], F32))
    tble = WT(nc.sbuf_tensor("tble", [128, 4096], F32))
    ternt = WT(nc.sbuf_tensor("ternt", [128, 4096], BF16))
    wstg = WT(nc.sbuf_tensor("wstg", [128, 4096], BF16))
    b1n = WT(nc.sbuf_tensor("b1n", [32, 128], F32))

    P = Plan()
    for dst, src in ((sb_identb, identb), (sb_identf, identf), (sb_onesf, onesf),
                     (sb_onescf, onescf), (sb_maskdd, maskdd),
                     (sb_padbias, padbias), (sb_b2, b2[None, :])):
        P.do("sync", lambda e, d=dst, s=src: e.dma_start(out=d[:], in_=s), s_cl, 16)
    P.do("sync", lambda e: e.dma_start(
        out=b1n[:], in_=b1.rearrange("(c p) -> c p", p=128)), s_cl, 16)
    P.do("vector", lambda e: e.memset(sb_eps[:], 1e-5), s_ln, 1)

    for wi, (wsrc, wdst) in enumerate(((wq, wqT), (wk, wkT), (wv, wvT), (wo, woT))):
        if wi > 0:
            P.wait("gpsimd", s_we)
        P.do("gpsimd", lambda e, s=wsrc: e.dma_start(
            out=wnat[:], in_=s.rearrange("(r p) d -> p r d", p=128)), s_wl, 16)
        P.wait("tensor", s_wl)
        for r in range(8):
            ps = (A4a, A4b)[r % 2]
            if r >= 2:
                P.wait("tensor", s_we)
            for dc in range(8):
                P.do("tensor", lambda e, r=r, dc=dc, ps=ps: e.matmul(
                    ps[:, dc * 128:(dc + 1) * 128],
                    wnat[:, r, dc * 128:(dc + 1) * 128], sb_identb[:],
                    start=True, stop=True), s_wt if dc == 7 else None)
            P.wait("vector", s_wt)
            P.do("vector", lambda e, r=r, ps=ps, wd=wdst: e.tensor_copy(
                wd[:, :, r * 128:(r + 1) * 128],
                ps[:].rearrange("p (c f) -> p c f", f=128)), s_we, 1)

    P.wait("tensor", s_cl)
    P.do("tensor", lambda e: e.matmul(A4c[:, 0:32], b1n[:], sb_identf[0:32, 0:32],
                                      start=True, stop=True), s_wt, 1)
    P.wait("vector", s_wt)
    P.do("vector", lambda e: e.tensor_copy(sb_b1[:], A4c[:, 0:32]), s_we, 1)

    def w_abs_pass(wsrc, ntile, tw, acol0, fidx, nrows=1):
        for t in range(ntile):
            par = t % 2
            q = ("sync", "scalar")[t % 2]
            qs = (s_xl, s_x2)[t % 2]
            P.wait(q, s_ln, wfb_free[par])
            P.do(q, lambda e, t=t, par=par, s=wsrc, tw=tw, nr=nrows: e.dma_start(
                out=wfb[:, par, 0:tw].rearrange("p (r d) -> p r d", r=nr),
                in_=s.rearrange("(r p) d -> p r d", p=128)[:, t * nr:(t + 1) * nr]),
                qs, 16)
            P.wait("vector", qs)
            P.do("vector", lambda e, t=t, par=par, tw=tw, c=acol0: e.reduce_sum(
                absa[:, c + t:c + t + 1], wfb[:, par, 0:tw], axis=AX,
                apply_absolute_value=True), s_ln, 1)
            wfb_free[par] = s_ln.n
        P.wait("vector", s_ln)
        P.do("vector", lambda e, c=acol0, n=ntile, f=fidx: e.reduce_sum(
            sc1[:, f:f + 1], absa[:, c:c + n], axis=AX), s_ln, 1)
        P.wait("tensor", s_ln)
        P.do("tensor", lambda e, f=fidx: e.matmul(
            B2a[0:1, f:f + 1], sc1[:, f:f + 1], sb_onescf[:],
            start=True, stop=True), s_wt, 1)
        P.wait("vector", s_wt)
        P.do("vector", lambda e, f=fidx: e.tensor_scalar(
            sc1[0:1, 4 + f:5 + f], B2a[0:1, f:f + 1], 1.0 / (FF * D), 1e-8,
            ALU.mult, ALU.max), s_ln, 1)
        P.wait("vector", s_ln)
        P.do("vector", lambda e, f=fidx: e.reciprocal(
            sc1[0:1, 6 + f:7 + f], sc1[0:1, 4 + f:5 + f]), s_ln, 1)
        P.wait("tensor", s_ln)
        P.do("tensor", lambda e, f=fidx: e.matmul(
            B2b[:, f:f + 1], sb_onesf[:], sc1[0:1, 6 + f:7 + f],
            start=True, stop=True), s_wt, 1)
        P.wait("vector", s_wt)
        P.do("vector", lambda e, f=fidx: e.tensor_copy(
            sinv[:, f:f + 1], B2b[:, f:f + 1]), s_ln, 1)

    wfb_free = [0, 0]
    w_abs_pass(w1, 8, 4096, 0, 0, nrows=4)
    w_abs_pass(w2, 8, 4096, 32, 1)

    def w_tern_pass(wsrc, ntile, tw, fidx, wtd, nrows=1):
        nchunk = tw // 128
        for t in range(ntile):
            par = t % 2
            q = ("sync", "scalar")[t % 2]
            qs = (s_xl, s_x2)[t % 2]
            P.wait(q, s_ln, wfb_free[par])
            P.do(q, lambda e, t=t, par=par, s=wsrc, tw=tw, nr=nrows: e.dma_start(
                out=wfb[:, par, 0:tw].rearrange("p (r d) -> p r d", r=nr),
                in_=s.rearrange("(r p) d -> p r d", p=128)[:, t * nr:(t + 1) * nr]),
                qs, 16)
            P.wait("vector", qs)
            P.wait("vector", s_wt, tern_tr[0])
            P.do("vector", lambda e, par=par, tw=tw, f=fidx: e.tensor_scalar_mul(
                ternb[:, 0:tw], wfb[:, par, 0:tw], sinv[:, f:f + 1]), s_ln, 1)
            wfb_free[par] = s_ln.n
            P.wait("vector", s_ln)
            P.do("vector", lambda e, tw=tw: e.tensor_single_scalar(
                tble[:, 0:tw], ternb[:, 0:tw], -0.5, ALU.is_le), s_ln, 1)
            P.wait("vector", s_ln)
            P.do("vector", lambda e, tw=tw: e.scalar_tensor_tensor(
                ternt[:, 0:tw], ternb[:, 0:tw], 0.5, tble[:, 0:tw],
                ALU.is_ge, ALU.subtract), s_ap, 1)
            P.wait("tensor", s_ap)
            P.wait("tensor", s_ws, stg_done[0])
            for ch in range(nchunk):
                bank = (t * ((nchunk + 7) // 8) + ch // 8) % 2
                p8 = (A4a, A4b)[bank]
                if ch % 8 == 0:
                    P.wait("tensor", s_we, trn_bank_free[bank])
                P.do("tensor", lambda e, ch=ch, p8=p8: e.matmul(
                    p8[:, (ch % 8) * 128:(ch % 8 + 1) * 128],
                    ternt[:, ch * 128:(ch + 1) * 128], sb_identb[:],
                    start=True, stop=True), s_wt if ch % 8 == 7 or ch == nchunk - 1 else None)
                if ch % 8 == 7 or ch == nchunk - 1:
                    n8 = ch % 8 + 1
                    P.wait("vector", s_wt)
                    P.do("vector", lambda e, ch=ch, n8=n8, p8=p8: e.tensor_copy(
                        wstg[:, (ch - n8 + 1) * 128:(ch + 1) * 128],
                        p8[:, 0:n8 * 128]), s_we, 1)
                    trn_bank_free[bank] = s_we.n
            tern_tr[0] = s_wt.n
            P.wait("gpsimd", s_we)
            if nrows == 1:
                P.do("gpsimd", lambda e, t=t, w=wtd, nch=nchunk: e.dma_start(
                    out=w.rearrange("c p f -> p c f")[:, :, t * 128:(t + 1) * 128],
                    in_=wstg[:, 0:nch * 128].rearrange("p (c f) -> p c f", f=128)),
                    s_ws, 16)
            else:
                for r in range(nrows):
                    P.do("gpsimd", lambda e, t=t, r=r, w=wtd, nr=nrows: e.dma_start(
                        out=w.rearrange("c p f -> p c f")[
                            :, :, (t * nr + r) * 128:(t * nr + r + 1) * 128],
                        in_=wstg[:, r * 1024:(r + 1) * 1024].rearrange(
                            "p (c k) -> p c k", k=128)), s_ws, 16)
            stg_done[0] = s_ws.n

    tern_tr, stg_done = [0], [0]
    trn_bank_free = {0: 0, 1: 0}
    w_tern_pass(w1, 8, 4096, 0, w1t_d, nrows=4)
    w_tern_pass(w2, 8, 4096, 1, w2t_d)

    P.wait("gpsimd", s_ws)
    barrier(P)
    P.emit(nc)
    if upto == "w":
        return finish(dict(wqT=wqT[:], wvT=wvT[:], sinv=sinv[:], b1s=sb_b1[:],
                           w1t=w1t_d, w2t=w2t_d))
    wtmp.close()

    ab = ExitStack()
    AE = ab.enter_context
    xbf = AE(nc.sbuf_tensor("xbf", [128, 2, 4, 1024], F32))
    xnb = AE(nc.sbuf_tensor("xnb", [128, 2, 4, 1024], BF16))
    xnT = AE(nc.sbuf_tensor("xnT", [128, 2, 8, 512], BF16))
    kst = AE(nc.sbuf_tensor("kst", [128, 2, 4, 1024], BF16))
    v1s = AE(nc.sbuf_tensor("v1s", [128, 2, 4, 1040], BF16))
    qst = AE(nc.sbuf_tensor("qst", [128, 2, 2048], BF16))

    P = Plan()

    def ln_tiles(par, nrow, ap_src):
        sb = par * 32
        P.wait("scalar", s_xl)
        for r in range(nrow):
            P.do("scalar", lambda e, r=r: e.activation(
                trash[:, r % 2, :], ap_src(r), AF.Square,
                accum_out=stats[:, sb + r * 8 + 1:sb + r * 8 + 2]))
        for r in range(nrow):
            P.do("scalar", lambda e, r=r: e.activation(
                trash[:, r % 2, :], ap_src(r), AF.Copy,
                accum_out=stats[:, sb + r * 8:sb + r * 8 + 1]),
                s_rs if r == nrow - 1 else None)
        P.wait("vector", s_rs)
        for r in range(nrow):
            P.do("vector", lambda e, r=r: e.tensor_scalar_mul(
                stats[:, sb + r * 8 + 3:sb + r * 8 + 4],
                stats[:, sb + r * 8:sb + r * 8 + 1], -1.0 / D))
            P.do("vector", lambda e, r=r: e.tensor_scalar(
                stats[:, sb + r * 8 + 4:sb + r * 8 + 5],
                stats[:, sb + r * 8:sb + r * 8 + 1],
                stats[:, sb + r * 8:sb + r * 8 + 1], 1.0 / (D * D),
                ALU.mult, ALU.mult), s_ln if r == nrow - 1 else None)
        P.wait("vector", s_ln)
        for r in range(nrow):
            P.do("vector", lambda e, r=r: e.scalar_tensor_tensor(
                stats[:, sb + r * 8 + 5:sb + r * 8 + 6],
                stats[:, sb + r * 8 + 1:sb + r * 8 + 2], 1.0 / D,
                stats[:, sb + r * 8 + 4:sb + r * 8 + 5], ALU.mult, ALU.subtract),
                s_ln if r == nrow - 1 else None)
        P.wait("scalar", s_ln)
        for r in range(nrow):
            P.do("scalar", lambda e, r=r: e.activation(
                stats[:, sb + r * 8 + 6:sb + r * 8 + 7],
                stats[:, sb + r * 8 + 5:sb + r * 8 + 6],
                AF.Ln, bias=sb_eps[:]), s_rs if r == nrow - 1 else None)
        P.wait("scalar", s_rs)
        for r in range(nrow):
            P.do("scalar", lambda e, r=r: e.activation(
                stats[:, sb + r * 8 + 7:sb + r * 8 + 8],
                stats[:, sb + r * 8 + 6:sb + r * 8 + 7],
                AF.Exp, scale=-0.5), s_rs if r == nrow - 1 else None)
        P.wait("vector", s_rs)
        for r in range(nrow):
            P.do("vector", lambda e, r=r, par=par: e.tensor_scalar(
                xnb[:, par, r, :], ap_src(r), stats[:, sb + r * 8 + 3:sb + r * 8 + 4],
                stats[:, sb + r * 8 + 7:sb + r * 8 + 8], ALU.add, ALU.mult),
                s_ap if r == nrow - 1 else None)

    def transpose_tiles(par, nrow):
        P.wait("tensor", s_pe, vpe_free[0])
        P.wait("tensor", s_pe, vpe_free[1])
        for r in range(nrow):
            ps = (A4a, A4b)[r % 2]
            if r >= 2:
                P.wait("tensor", s_te, te_free[r % 2])
            for dc in range(8):
                P.do("tensor", lambda e, r=r, dc=dc, ps=ps, par=par: e.matmul(
                    ps[:, dc * 128:(dc + 1) * 128],
                    xnb[:, par, r, dc * 128:(dc + 1) * 128], sb_identb[:],
                    start=True, stop=True), s_tr if dc == 7 else None)
            P.wait("vector", s_tr)
            P.do("vector", lambda e, r=r, par=par, ps=ps: e.tensor_copy(
                xnT[:, par, :, r * 128:(r + 1) * 128],
                ps[:].rearrange("p (c f) -> p c f", f=128)), s_te, 1)
            te_free[r % 2] = s_te.n

    def kq_proj(par, ntok, wT, dst_cb):
        for pr in range(NPAIR):
            ps = (B2a, B2b)[pr % 2]
            P.wait("tensor", s_pe, pe_free[pr % 2])
            for dc in range(8):
                P.do("tensor", lambda e, pr=pr, dc=dc, ps=ps, par=par, wT=wT, n=ntok:
                     e.matmul(ps[:, 0:n], wT[:, dc, pr * 128:(pr + 1) * 128],
                              xnT[:, par, dc, 0:n], start=(dc == 0), stop=(dc == 7)),
                     s_pj if dc == 7 else None)
            P.wait("vector", s_pj)
            dst_cb(pr, ps)
            pe_free[pr % 2] = s_pe.n

    def v_proj(par, ntsub):
        for tsub in range(ntsub):
            ps = (A4a, A4b)[tsub % 2]
            P.wait("tensor", s_pe, vpe_free[tsub % 2])
            P.wait("tensor", s_te, te_free[tsub % 2])
            for dc in range(8):
                for nh in range(2):
                    P.do("tensor", lambda e, dc=dc, nh=nh, tsub=tsub, ps=ps, par=par:
                         e.matmul(ps[:, nh * 512:(nh + 1) * 512],
                                  xnT[:, par, dc, tsub * 128:(tsub + 1) * 128],
                                  wvT[:, dc, nh * 512:(nh + 1) * 512],
                                  start=(dc == 0), stop=(dc == 7)),
                         s_pj if (dc == 7 and nh == 1) else None)
            P.wait("vector", s_pj)
            P.do("vector", lambda e, tsub=tsub, par=par, ps=ps: e.tensor_copy(
                v1s[:, par, tsub, :].rearrange("p (h c) -> p h c", c=65)[:, :, 0:64],
                ps[:].rearrange("p (h c) -> p h c", c=64)), s_pe, 1)
            P.do("vector", lambda e, tsub=tsub, par=par: e.memset(
                v1s[:, par, tsub, :].rearrange("p (h c) -> p h c", c=65)[:, :, 64:65],
                1.0), s_pe, 1)
            vpe_free[tsub % 2] = s_pe.n

    te_free, pe_free, vpe_free = [0, 0], [0, 0], [0, 0]
    xbf_free = [0, 0]
    xnb_free = [0, 0]
    ss_done = [0, 0]
    ap_done = {}

    chunks = ([("a", j) for j in range(NSLOT)] + [("b", t) for t in range(16)])

    def load_chunk(i):
        kind, idx = chunks[i]
        par = i % 2
        P.wait("sync", s_ap, xbf_free[par])
        if kind == "a":
            P.do("sync", lambda e, idx=idx, par=par: e.dma_start(
                out=xbf[:, par, 0:2, :],
                in_=xq[idx * 256:(idx + 1) * 256, :].rearrange(
                    "(r p) d -> p r d", p=128)), s_xl, 16)
        else:
            P.do("sync", lambda e, idx=idx, par=par: e.dma_start(
                out=xbf[:, par, :, :],
                in_=xb[idx * 512:(idx + 1) * 512, :].rearrange(
                    "(r p) d -> p r d", p=128)), s_xl, 16)

    def ln_chunk(i):
        kind, idx = chunks[i]
        par = i % 2
        nrow = 2 if kind == "a" else 4
        P.wait("vector", s_xl)
        P.wait("vector", s_tr, xnb_free[par])
        ln_tiles(par, nrow, lambda r, par=par: xbf[:, par, r, :])
        xbf_free[par] = s_ap.n
        ap_done[i] = s_ap.n

    def work_chunk(i):
        kind, idx = chunks[i]
        par = i % 2
        nrow = 2 if kind == "a" else 4
        P.wait("tensor", s_ap, ap_done[i])
        transpose_tiles(par, nrow)
        xnb_free[par] = s_tr.n
        P.wait("tensor", s_te)
        P.wait("vector", s_ss, ss_done[par])

        if kind == "a":
            def q_evac(pr, ps, par=par):
                P.do("vector", lambda e, pr=pr, ps=ps: e.tensor_copy(
                    qst[:, par, pr * 256:(pr + 1) * 256], ps[:, 0:256]), s_pe, 1)

            def k_evac(pr, ps, par=par):
                P.do("vector", lambda e, pr=pr, ps=ps: e.tensor_copy(
                    kst[:, par, 0:2, pr * 128:(pr + 1) * 128],
                    ps[:, 0:256].rearrange("p (t k) -> p t k", k=128)), s_pe, 1)

            kq_proj(par, 256, wqT, q_evac)
            kq_proj(par, 256, wkT, k_evac)
            v_proj(par, 2)
            P.wait("gpsimd", s_pe)
            P.do("gpsimd", lambda e, idx=idx, par=par: e.dma_start(
                out=qt_d[idx], in_=qst[:, par, :]), s_ss, 16)
            P.wait("gpsimd", s_te)
            P.do("gpsimd", lambda e, idx=idx, par=par: e.dma_start(
                out=xnt_d[idx].rearrange("p (c f) -> p c f", f=256),
                in_=xnT[:, par, :, 0:256]), s_ss, 16)
            P.do("gpsimd", lambda e, idx=idx, par=par: e.dma_start(
                out=kto_d[2 * idx:2 * idx + 2].rearrange("t p c -> p t c"),
                in_=kst[:, par, 0:2, :]), s_ss, 16)
            P.do("gpsimd", lambda e, idx=idx, par=par: e.dma_start(
                out=v1o_d[2 * idx:2 * idx + 2].rearrange("t p c -> p t c"),
                in_=v1s[:, par, 0:2, :]), s_ss, 16)
        else:
            def k_evac_b(pr, ps, par=par):
                P.do("vector", lambda e, pr=pr, ps=ps: e.tensor_copy(
                    kst[:, par, :, pr * 128:(pr + 1) * 128],
                    ps[:, 0:512].rearrange("p (t k) -> p t k", k=128)), s_pe, 1)

            kq_proj(par, 512, wkT, k_evac_b)
            v_proj(par, 4)
            P.wait("gpsimd", s_pe)
            P.do("gpsimd", lambda e, idx=idx, par=par: e.dma_start(
                out=kt_d[4 * idx:4 * idx + 4].rearrange("t p c -> p t c"),
                in_=kst[:, par, :, :]), s_ss, 16)
            P.do("gpsimd", lambda e, idx=idx, par=par: e.dma_start(
                out=v1_d[4 * idx:4 * idx + 4].rearrange("t p c -> p t c"),
                in_=v1s[:, par, :, :]), s_ss, 16)
        ss_done[par] = s_ss.n

    for i in range(len(chunks) + 1):
        if i < len(chunks):
            load_chunk(i)
            ln_chunk(i)
        if i >= 1:
            work_chunk(i - 1)

    P.wait("gpsimd", s_ss)
    barrier(P)
    P.emit(nc)
    if upto == "ab":
        return finish(dict(kt=kt_d, v1=v1_d, kto=kto_d, v1o=v1o_d, qt=qt_d,
                           xnt=xnt_d))
    ab.close()
    wab.close()

    dph = ExitStack()
    DP = dph.enter_context
    amidT = DP(nc.sbuf_tensor("amidT", [128, 2, 8, 256], BF16))
    ktm = DP(nc.sbuf_tensor("ktm", [128, 2, 8, 1024], BF16))
    v1m = DP(nc.sbuf_tensor("v1m", [128, 2, 8, 1040], BF16))
    qts = DP(nc.sbuf_tensor("qts", [128, 2048], BF16))
    ptb = DP(nc.sbuf_tensor("ptb", [128, 3, 1024], BF16))
    acc = DP(nc.sbuf_tensor("acc", [65, 4096], F32))
    rcp = DP(nc.sbuf_tensor("rcp", [1, 16, 256], F32))
    w1s = DP(nc.sbuf_tensor("w1s", [128, 8, 512], BF16))
    w2s = DP(nc.sbuf_tensor("w2s", [128, 32, 512], BF16))
    xnte = DP(nc.sbuf_tensor("xnte", [128, 8, 256], BF16))
    hts = DP(nc.sbuf_tensor("hts", [128, 32, 256], BF16))
    xqe = DP(nc.sbuf_tensor("xqe", [128, 2, 512], F32))
    yb = DP(nc.sbuf_tensor("yb", [128, 2, 512], F32))

    P = Plan()
    PBCOL = [sum(GJ[:k]) for k in range(NSLOT)]
    LOOK = 2
    sg_free = [0, 0]
    pt_free = [0, 0, 0]
    pv_free = [0, 0]
    km_free = [[0, 0], [0, 0]]
    qt_free = [0, 0]
    am_free = [0, 0]
    am_rd = [0, 0]
    acc_free = [0]
    pending_norm = [None]
    pending_e = [None]
    uidx = 0
    xnte_free = [0, 0]
    xq_free = [0]
    yb_free = [0]
    ht_free = {}
    w1_free = [0, 0]
    w2_free = [0, 0]
    a4c_norm_done = [0]

    def make_epieces(jj):
        epar = jj % 2
        pieces = []
        st = {}

        def p_loads():
            P.wait("sync", s_f1, xnte_free[0])
            P.do("sync", lambda e, jj=jj: e.dma_start(
                out=xnte[:],
                in_=xnt_d[jj].rearrange("p (c f) -> p c f", f=256)), s_el, 16)
            st["xnte"] = s_el.n
            P.wait("sync", s_yv, xq_free[0])
            P.do("sync", lambda e, jj=jj: e.dma_start(
                out=xqe[:],
                in_=xq[jj * 256:(jj + 1) * 256, 0:512].rearrange(
                    "(r p) d -> p r d", p=128)), s_el, 16)
            st["xqe"] = s_el.n
        pieces.append(p_loads)

        gl_tgt = {}
        for c in range(8):
            def p_ffn1(c=c):
                fc0 = c * 4
                P.wait("sync", s_f1, w1_free[0])
                P.do("sync", lambda e, fc0=fc0: e.dma_start(
                    out=w1s[:],
                    in_=w1t_d.rearrange("c p f -> p c f")[:, :, fc0 * 128:(fc0 + 4) * 128]),
                    s_el, 16)
                ld = s_el.n
                P.wait("tensor", s_el, ld)
                P.wait("tensor", s_el, st["xnte"])
                if c == 0:
                    P.wait("tensor", s_am, a4c_norm_done[0])
                for q in range(4):
                    fc = fc0 + q
                    hb = (fc % 2) * 512
                    P.wait("tensor", s_gl, gl_tgt.get(fc - 2, 0))
                    for dc in range(8):
                        P.do("tensor", lambda e, fc=fc, hb=hb, dc=dc,
                             epar=epar, q=q: e.matmul(
                                 A4c[:, hb:hb + 256],
                                 w1s[:, dc, q * 128:(q + 1) * 128],
                                 xnte[:, dc, :],
                                 start=(dc == 0), stop=(dc == 7)),
                             s_f1 if dc == 7 else None)
                    P.wait("scalar", s_f1)
                    P.wait("scalar", s_f2, ht_free.get(fc, 0))
                    P.do("scalar", lambda e, fc=fc, hb=hb: e.activation(
                        hts[:, fc, :], A4c[:, hb:hb + 256], AF.Gelu,
                        bias=sb_b1[:, fc:fc + 1]), s_gl, 1)
                    gl_tgt[fc] = s_gl.n
                w1_free[0] = s_f1.n
            pieces.append(p_ffn1)

        def p_gl_all():
            P.do("scalar", lambda e: e.copy(stats[0:1, 28:29], stats[0:1, 28:29]),
                 s_gl, 1)
            st["gl_all"] = s_gl.n
        pieces.append(p_gl_all)

        def p_w2load(h):
            def fn():
                P.wait("sync", s_f2, w2_free[0])
                P.do("sync", lambda e, h=h: e.dma_start(
                    out=w2s[:],
                    in_=w2t_d[:, :, h * 512:(h + 1) * 512].rearrange(
                        "c p f -> p c f")), s_el, 16)
                st["w2"] = s_el.n
            return fn
        pieces.append(p_w2load(0))

        for h in range(2):
            def p_pass(h=h):
                P.wait("tensor", s_el, st["w2"])
                P.wait("tensor", s_gl, st["gl_all"])
                P.wait("tensor", s_am, a4c_norm_done[0])
                if h == 1:
                    P.wait("tensor", s_yv, st["yv0"])
                for t in range(2):
                    P.do("tensor", lambda e, t=t, h=h: e.matmul(
                        A4c[:, t * 512:(t + 1) * 512], sb_onesf[0:1, :],
                        sb_b2[0:1, h * 512:(h + 1) * 512], start=True, stop=False))
                for fc in range(32):
                    for t in range(2):
                        P.do("tensor", lambda e, fc=fc, t=t: e.matmul(
                            A4c[:, t * 512:(t + 1) * 512],
                            hts[:, fc, t * 128:(t + 1) * 128],
                            w2s[:, fc, :], start=False, stop=False))
                for pr in range(NPAIR):
                    for t in range(2):
                        P.do("tensor", lambda e, pr=pr, t=t, epar=epar:
                             e.matmul(A4c[:, t * 512:(t + 1) * 512],
                                      amidT[:, epar, pr, t * 128:(t + 1) * 128],
                                      woT[:, pr, h * 512:(h + 1) * 512],
                                      start=False, stop=(pr == NPAIR - 1)),
                             s_f2 if (pr == NPAIR - 1 and t == 1) else None)
                w2_free[0] = s_f2.n
                if h == 1:
                    am_rd[epar] = s_f2.n
                    for fc in range(32):
                        ht_free[fc] = s_f2.n
            pieces.append(p_pass)

            def p_resid(h=h, jj=jj):
                P.wait("vector", s_f2)
                P.wait("vector", s_el, st["xqe"])
                P.wait("vector", s_ys, yb_free[0])
                for t in range(2):
                    P.do("vector", lambda e, t=t: e.tensor_add(
                        yb[:, t, :], A4c[:, t * 512:(t + 1) * 512],
                        xqe[:, t, :]), s_yv, 1)
                if h == 0:
                    st["yv0"] = s_yv.n
                else:
                    xq_free[0] = s_yv.n
                P.wait("gpsimd", s_yv)
                P.do("gpsimd", lambda e, jj=jj, h=h: e.dma_start(
                    out=y[jj * 256:(jj + 1) * 256,
                          h * 512:(h + 1) * 512].rearrange(
                        "(r p) d -> p r d", p=128),
                    in_=yb[:]), s_ys, 16)
                yb_free[0] = s_ys.n
            pieces.append(p_resid)
            if h == 0:
                def p_xq1(jj=jj):
                    P.wait("sync", s_yv, st["yv0"])
                    P.do("sync", lambda e, jj=jj: e.dma_start(
                        out=xqe[:],
                        in_=xq[jj * 256:(jj + 1) * 256, 512:1024].rearrange(
                            "(r p) d -> p r d", p=128)), s_el, 16)
                    st["xqe"] = s_el.n
                pieces.append(p_xq1)
                pieces.append(p_w2load(1))

        def p_close():
            xnte_free[0] = s_f1.n
        pieces.append(p_close)
        return pieces

    for j in reversed(range(NSLOT)):
        spar = j % 2
        P.wait("sync", s_qk, qt_free[0])
        P.do("sync", lambda e, j=j: e.dma_start(
            out=qts[:], in_=qt_d[j]), s_kl, 16)
        groups = GJ[j]
        macros = [list(range(m, min(m + MACRO, groups)))
                  for m in range(0, groups, MACRO)]
        macros.append(None)

        units = []
        for mi, mg in enumerate(macros):
            diag = mg is None
            ng = 1 if diag else len(mg)
            for pr in range(NPAIR):
                for gi in range(ng):
                    units.append(dict(mi=mi, mg=mg, diag=diag, ng=ng, pr=pr,
                                      gi=gi, mpar=mi % 2))
        nun = len(units)
        kl_tgt = {}

        def emit_macro_load(mi, j=j, macros=macros):
            mg = macros[mi]
            mpar = mi % 2
            P.wait("sync", s_qk, km_free[mpar][0])
            P.wait("sync", s_pv, km_free[mpar][1])
            if mg is None:
                P.do("sync", lambda e, j=j, mpar=mpar: e.dma_start(
                    out=ktm[:, mpar, 0:2, :],
                    in_=kto_d[2 * j:2 * j + 2].rearrange("t p c -> p t c")), s_kl, 16)
                P.do("sync", lambda e, j=j, mpar=mpar: e.dma_start(
                    out=v1m[:, mpar, 0:2, :],
                    in_=v1o_d[2 * j:2 * j + 2].rearrange("t p c -> p t c")), s_kl, 16)
            else:
                k0, nb = 2 * mg[0], 2 * len(mg)
                P.do("sync", lambda e, k0=k0, nb=nb, mpar=mpar: e.dma_start(
                    out=ktm[:, mpar, 0:nb, :],
                    in_=kt_d[k0:k0 + nb].rearrange("t p c -> p t c")), s_kl, 16)
                P.do("sync", lambda e, k0=k0, nb=nb, mpar=mpar: e.dma_start(
                    out=v1m[:, mpar, 0:nb, :],
                    in_=v1_d[k0:k0 + nb].rearrange("t p c -> p t c")), s_kl, 16)
            kl_tgt[mi] = s_kl.n

        qk_done = {}
        ex_done = {}
        ex_half = {}
        mk_done = {}
        last_qk_of_macro = {}
        epieces = pending_e[0] or []
        pending_e[0] = None
        cadence = max(1, (nun - 4) // max(1, len(epieces))) if epieces else 0
        ep_i = 0

        for step in range(nun + LOOK):
            if step == 2 and pending_norm[0] is not None:
                pending_norm[0]()
                pending_norm[0] = None
            if (epieces and step >= 3 and ep_i < len(epieces)
                    and (step - 3) % cadence == 0):
                epieces[ep_i]()
                ep_i += 1
            if step == 0:
                emit_macro_load(0)
            if step < nun:
                u = units[step]
                mi, pr, gi, mpar, diag = u["mi"], u["pr"], u["gi"], u["mpar"], u["diag"]
                if pr == 0 and gi == 0 and mi + 1 < len(macros):
                    emit_macro_load(mi + 1)
                sg_i = uidx % 2
                ppar = uidx % 3
                u["sg_i"], u["ppar"] = sg_i, ppar
                uidx += 1
                sgp = (A4a, A4b)[sg_i]
                u["sgp"] = sgp
                if pr == 0 and gi == 0:
                    P.wait("tensor", s_kl, kl_tgt[mi])
                P.wait("tensor", s_ex, sg_free[sg_i])
                for (kb, hp, off) in ((0, 0, 0), (1, 0, 256), (0, 1, 512), (1, 1, 768)):
                    first = off % 512 == 0
                    tp = (hp * 64, 0)
                    P.do("tensor", lambda e, kb=kb, hp=hp, off=off, tp=tp, gi=gi,
                         pr=pr, sgp=sgp, mpar=mpar, spar=spar, first=first:
                         e.matmul(sgp[:, off:off + 256],
                                  ktm[hp * 64:(hp + 1) * 64, mpar, 2 * gi + kb,
                                      pr * 128:(pr + 1) * 128],
                                  qts[hp * 64:(hp + 1) * 64,
                                      pr * 256:(pr + 1) * 256],
                                  start=first, stop=not first, tile_position=tp),
                         s_qk if off in (256, 768) else None)
                qk_done[step] = s_qk.n
                last_qk_of_macro[mi] = s_qk.n
                col = 136 if diag else PBCOL[j] + u["mg"][gi]
                P.wait("scalar", s_qk, qk_done[step] - 1)
                P.wait("scalar", s_pv, pt_free[ppar])
                P.do("scalar", lambda e, ppar=ppar, sgp=sgp, col=col: e.activation(
                    ptb[:, ppar, 0:512], sgp[:, 0:512], AF.Exp,
                    bias=sb_padbias[:, col:col + 1], scale=0.125), s_ex, 1)
                ex_half[step] = s_ex.n
                P.wait("scalar", s_qk, qk_done[step])
                P.do("scalar", lambda e, ppar=ppar, sgp=sgp, col=col: e.activation(
                    ptb[:, ppar, 512:1024], sgp[:, 512:1024], AF.Exp,
                    bias=sb_padbias[:, col:col + 1], scale=0.125), s_ex, 1)
                ex_done[step] = s_ex.n
                sg_free[sg_i] = s_ex.n
                if diag:
                    P.wait("vector", s_ex, ex_done[step])
                    P.do("vector", lambda e, ppar=ppar: e.tensor_mul(
                        ptb[:, ppar, :], ptb[:, ppar, :], sb_maskdd[:]), s_mk, 1)
                    P.do("vector", lambda e: e.tensor_copy(
                        stats[0:1, 29:30], stats[0:1, 29:30]), s_mk, 1)
                    mk_done[step] = s_mk.n
            if step >= LOOK:
                vstep = step - LOOK
                v = units[vstep]
                mi, pr, gi, mpar, diag = v["mi"], v["pr"], v["gi"], v["mpar"], v["diag"]
                pvp = (B2a, B2b)[pr % 2]
                ppar = v["ppar"]
                if gi == 0:
                    P.wait("tensor", s_ac, pv_free[pr % 2])
                if diag:
                    P.wait("tensor", s_mk, mk_done[vstep])
                else:
                    P.wait("tensor", s_ex, ex_half[vstep] + 1)
                for (kb, hp, poff) in ((0, 0, 0), (1, 0, 0), (0, 1, 256), (1, 1, 256)):
                    first = gi == 0 and kb == 0 and hp == 0
                    if hp == 1 and kb == 0 and not diag:
                        P.wait("tensor", s_ex, ex_done[vstep] + 1)
                    P.do("tensor", lambda e, kb=kb, hp=hp, poff=poff, gi=gi, pr=pr,
                         pvp=pvp, mpar=mpar, ppar=ppar, first=first:
                         e.matmul(pvp[0:65, poff:poff + 256],
                                  v1m[:, mpar, 2 * gi + kb,
                                      (2 * pr + hp) * 65:(2 * pr + hp + 1) * 65],
                                  ptb[:, ppar, hp * 512 + kb * 256:
                                      hp * 512 + kb * 256 + 256],
                                  start=first, stop=False),
                         s_pv if (kb == 1 and hp == 1) else None)
                pt_free[ppar] = s_pv.n
                if gi == v["ng"] - 1:
                    P.wait("vector", s_pv)
                    if mi == 0:
                        P.wait("vector", s_am, acc_free[0])
                        P.do("vector", lambda e, pr=pr, pvp=pvp: e.tensor_copy(
                            acc[:, pr * 512:(pr + 1) * 512], pvp[0:65, :]), s_ac, 1)
                    else:
                        P.do("vector", lambda e, pr=pr, pvp=pvp: e.tensor_add(
                            acc[:, pr * 512:(pr + 1) * 512],
                            acc[:, pr * 512:(pr + 1) * 512], pvp[0:65, :]), s_ac, 1)
                    pv_free[pr % 2] = s_ac.n
                    if pr == NPAIR - 1:
                        km_free[mpar] = [last_qk_of_macro[mi], s_pv.n]
        while epieces and ep_i < len(epieces):
            epieces[ep_i]()
            ep_i += 1
        P.do("scalar", lambda e: e.copy(stats[0:1, 28:29], stats[0:1, 28:29]),
             s_ex, 1)
        qt_free[0] = s_qk.n
        P.wait("vector", s_ac)
        for h in range(H):
            P.do("vector", lambda e, h=h: e.reciprocal(
                rcp[0:1, h, :], acc[64:65, h * 256:(h + 1) * 256]),
                s_nm if h == H - 1 else None)
        P.do("vector", lambda e: e.tensor_copy(
            stats[0:1, 29:30], stats[0:1, 29:30]), s_nm, 1)
        nm_tgt = s_nm.n

        def make_norm(j=j, spar=spar, nm_tgt=nm_tgt):
            def emit_norm():
                P.wait("tensor", s_nm, nm_tgt)
                P.wait("tensor", s_gl)
                P.wait("tensor", s_yv)
                P.wait("vector", s_f2, am_rd[spar])
                rp_tgt = {}
                for hp2 in range(H // 2 + 1):
                    if hp2 < H // 2:
                        P.wait("tensor", s_am, am_free[0])
                        P.do("tensor", lambda e, hp2=hp2: e.matmul(
                            A4c[0:64, 0:512] if hp2 % 2 == 0 else A4c[0:64, 512:1024],
                            sb_onesf[0:1, 0:64],
                            rcp[0:1, 2 * hp2, :].rearrange("a c -> a c") if False else
                            rcp[0:1, :, :].rearrange("a h c -> a (h c)")[
                                :, hp2 * 512:(hp2 + 1) * 512],
                            start=True, stop=True), s_rp, 1)
                        rp_tgt[hp2] = s_rp.n
                    if hp2 >= 1:
                        k = hp2 - 1
                        P.wait("vector", s_rp, rp_tgt[k])
                        for e2 in range(2):
                            hh = 2 * k + e2
                            P.do("vector", lambda e, hh=hh, e2=e2, k=k, spar=spar:
                                 e.tensor_mul(
                                amidT[(hh % 2) * 64:(hh % 2) * 64 + 64, spar,
                                      hh // 2, :],
                                acc[0:64, hh * 256:(hh + 1) * 256],
                                (A4c[0:64, 0:512] if k % 2 == 0 else
                                 A4c[0:64, 512:1024])[:, e2 * 256:(e2 + 1) * 256]),
                                s_am, 1)
                        am_free[0] = s_am.n
                P.do("vector", lambda e: e.tensor_copy(
                    stats[0:1, 29:30], stats[0:1, 29:30]), s_am, 1)
                acc_free[0] = s_am.n
                a4c_norm_done[0] = s_am.n
            return emit_norm

        pending_norm[0] = make_norm()
        pending_e[0] = make_epieces(j)

    if pending_norm[0] is not None:
        pending_norm[0]()
        pending_norm[0] = None
    for pc in (pending_e[0] or []):
        pc()
    pending_e[0] = None
    P.wait("gpsimd", s_ys)
    barrier(P)
    P.emit(nc)
    if upto == "d":
        return finish(dict(acc=acc[:], rcp=rcp[:]))
    dph.close()
    outer.close()
    return nc



def _consts():
    identb = np.eye(128, dtype=ml_dtypes.bfloat16)
    identf = np.eye(128, dtype=np.float32)
    onesf = np.ones((1, 128), dtype=np.float32)
    onescf = np.ones((128, 1), dtype=np.float32)
    tri = np.triu(np.ones((128, 128), np.float32))
    ones = np.ones((128, 128), np.float32)
    zero = np.zeros((128, 128), np.float32)
    m1 = np.concatenate([tri, ones], axis=1)
    m0 = np.concatenate([zero, tri], axis=1)
    maskdd = np.concatenate([m1, m0, m1, m0], axis=1).astype(ml_dtypes.bfloat16)
    return identb, identf, onesf, onescf, maskdd


def _padbias(cc):
    pb = np.full((128, 137), KILL, np.float32)
    col = 0
    for j in range(NSLOT):
        T = tile_T(j, cc)
        pb[:, col:col + T] = -SHIFT
        col += GJ[j]
    pb[:, 136] = -SHIFT
    return pb


def kernel(L0_attn_in_features, L0_attn_Wq, L0_attn_Wk, L0_attn_Wv, L0_attn_Wo,
           L0_ffn_W1, L0_ffn_b1, L0_ffn_W2, L0_ffn_b2):
    x = np.ascontiguousarray(np.asarray(L0_attn_in_features, dtype=np.float32))
    identb, identf, onesf, onescf, maskdd = _consts()
    in_maps = []
    for c in range(8):
        b, cc = c // 4, c % 4
        rows = np.concatenate([np.arange(256 * tile_T(j, cc),
                                         256 * tile_T(j, cc) + 256)
                               for j in range(NSLOT)])
        in_maps.append(dict(
            xq=np.ascontiguousarray(x[b][rows]),
            xb=np.ascontiguousarray(x[b]),
            wq=np.ascontiguousarray(np.asarray(L0_attn_Wq, np.float32)),
            wk=np.ascontiguousarray(np.asarray(L0_attn_Wk, np.float32)),
            wv=np.ascontiguousarray(np.asarray(L0_attn_Wv, np.float32)),
            wo=np.ascontiguousarray(np.asarray(L0_attn_Wo, np.float32)),
            w1=np.ascontiguousarray(np.asarray(L0_ffn_W1, np.float32)),
            b1=np.ascontiguousarray(np.asarray(L0_ffn_b1, np.float32)),
            w2=np.ascontiguousarray(np.asarray(L0_ffn_W2, np.float32)),
            b2=np.ascontiguousarray(np.asarray(L0_ffn_b2, np.float32)),
            identb=identb, identf=identf, onesf=onesf, onescf=onescf,
            maskdd=maskdd, padbias=_padbias(cc),
        ))
    nc = build_nc()
    res = run_bass_kernel_spmd(nc, in_maps, core_ids=list(range(8)))
    out = np.empty((B, S, D), np.float32)
    for c in range(8):
        b, cc = c // 4, c % 4
        yc = res.results[c]["y"]
        for j in range(NSLOT):
            T = tile_T(j, cc)
            out[b, 256 * T:256 * T + 256] = yc[256 * j:256 * j + 256]
    return out


if __name__ == "__main__":
    nc = build_nc()
    print("built ok,", nc.n_instructions(), "instructions")


# version 21
# speedup vs baseline: 1.0539x; 1.0265x over previous
import numpy as np
import ml_dtypes
from contextlib import ExitStack
import concourse.bass as bass
import concourse.mybir as mybir
from concourse.bass_utils import run_bass_kernel_spmd

F32 = mybir.dt.float32
BF16 = mybir.dt.bfloat16
AF = mybir.ActivationFunctionType
ALU = mybir.AluOpType
AX = mybir.AxisListType.X

B, S, D, H, DH, FF = 2, 8192, 1024, 16, 64, 4096
NPAIR, NSLOT, QT, KBLK = 8, 8, 256, 64
KJ = [64 - 8 * j for j in range(NSLOT)]
GJ = [(k - 2) // 2 for k in KJ]
MACRO = 4
SHIFT = 4.0
KILL = -1e15
ENGS = ("tensor", "vector", "scalar", "sync", "gpsimd")


def tile_T(j, cc):
    return 31 - 4 * j - ((cc + j) % 4)


class Sem:
    def __init__(self, stack, nc, name):
        self.h = stack.enter_context(nc.semaphore(name))
        self.n = 0


_PLAN_LOGS = []


class Plan:
    def __init__(self):
        self.q = {k: [] for k in ENGS}
        self.log = {k: [] for k in ENGS}

    def wait(self, eng, sem, tgt=None):
        v = sem.n if tgt is None else tgt
        if v and v > 0:
            self.log[eng].append(("w", sem.h.name if hasattr(sem.h, "name") else id(sem), v))
            self.q[eng].append(lambda e, h=sem.h, v=v: e.wait_ge(h, v))

    def do(self, eng, builder, sem=None, k=1):
        if sem is not None:
            sem.n += k
            self.log[eng].append(("i", sem.h.name if hasattr(sem.h, "name") else id(sem), k))
            self.q[eng].append(lambda e, b=builder, h=sem.h, kk=k: b(e).then_inc(h, kk))
        else:
            self.log[eng].append(("i", None, 0))
            self.q[eng].append(lambda e, b=builder: b(e))

    def emit(self, nc):
        _PLAN_LOGS.append(self.log)
        with nc.Block() as block:
            for name in ENGS:
                fns = self.q[name]
                getattr(block, name)(lambda e, fns=fns: [fn(e) for fn in fns])


def build_nc(upto=None, taps=()):
    nc = bass.Bass("TRN2", target_bir_lowering=False)

    xq = nc.dram_tensor("xq", [NSLOT * QT, D], F32, kind="ExternalInput").ap()
    xb = nc.dram_tensor("xb", [S, D], F32, kind="ExternalInput").ap()
    wq = nc.dram_tensor("wq", [D, D], F32, kind="ExternalInput").ap()
    wk = nc.dram_tensor("wk", [D, D], F32, kind="ExternalInput").ap()
    wv = nc.dram_tensor("wv", [D, D], F32, kind="ExternalInput").ap()
    wo = nc.dram_tensor("wo", [D, D], F32, kind="ExternalInput").ap()
    w1 = nc.dram_tensor("w1", [FF, D], F32, kind="ExternalInput").ap()
    b1 = nc.dram_tensor("b1", [FF], F32, kind="ExternalInput").ap()
    w2 = nc.dram_tensor("w2", [D, FF], F32, kind="ExternalInput").ap()
    b2 = nc.dram_tensor("b2", [D], F32, kind="ExternalInput").ap()
    identb = nc.dram_tensor("identb", [128, 128], BF16, kind="ExternalInput").ap()
    identf = nc.dram_tensor("identf", [128, 128], F32, kind="ExternalInput").ap()
    onesf = nc.dram_tensor("onesf", [1, 128], F32, kind="ExternalInput").ap()
    onescf = nc.dram_tensor("onescf", [128, 1], F32, kind="ExternalInput").ap()
    maskdd = nc.dram_tensor("maskdd", [128, 1024], BF16, kind="ExternalInput").ap()
    padbias = nc.dram_tensor("padbias", [128, 137], F32, kind="ExternalInput").ap()
    y = nc.dram_tensor("y", [NSLOT * QT, D], F32, kind="ExternalOutput").ap()

    kt_d = nc.dram_tensor("kt_d", [KBLK, 128, 1024], BF16).ap()
    v1_d = nc.dram_tensor("v1_d", [KBLK, 128, 1040], BF16).ap()
    kto_d = nc.dram_tensor("kto_d", [16, 128, 1024], BF16).ap()
    v1o_d = nc.dram_tensor("v1o_d", [16, 128, 1040], BF16).ap()
    qt_d = nc.dram_tensor("qt_d", [NSLOT, 128, 2048], BF16).ap()
    xnt_d = nc.dram_tensor("xnt_d", [NSLOT, 128, 2048], BF16).ap()
    w1t_d = nc.dram_tensor("w1t_d", [8, 128, FF], BF16).ap()
    w2t_d = nc.dram_tensor("w2t_d", [32, 128, D], BF16).ap()

    tap_outs = {}
    for tname, tshape, tdt in taps:
        tap_outs[tname] = nc.dram_tensor("tap_" + tname, tshape, tdt,
                                         kind="ExternalOutput").ap()

    outer = ExitStack()
    OE = outer.enter_context

    def sem(name):
        return Sem(outer, nc, name)

    s_cl, s_wl, s_wt, s_we, s_ws = [sem(n) for n in ("s_cl", "s_wl", "s_wt", "s_we", "s_ws")]
    s_xl, s_ln, s_rs, s_ap, s_tr = [sem(n) for n in ("s_xl", "s_ln", "s_rs", "s_ap", "s_tr")]
    s_x2 = sem("s_x2")
    s_te, s_pj, s_pe, s_ss = [sem(n) for n in ("s_te", "s_pj", "s_pe", "s_ss")]
    s_kl, s_qk, s_ex, s_mk, s_pv = [sem(n) for n in ("s_kl", "s_qk", "s_ex", "s_mk", "s_pv")]
    s_ac, s_nm, s_rp, s_am = [sem(n) for n in ("s_ac", "s_nm", "s_rp", "s_am")]
    s_el, s_f1, s_gl, s_f2, s_yv, s_ys = [sem(n) for n in
                                          ("s_el", "s_f1", "s_gl", "s_f2", "s_yv", "s_ys")]
    s_ph = sem("s_ph")

    def barrier(P):
        for _ in range(2):
            for eng in ENGS:
                s_ph.n += 1
                P.log[eng].append(("i", "s_ph", 1))
                P.q[eng].append(lambda e, h=s_ph.h: e.nop().then_inc(h, 1))
            tgt = s_ph.n
            for eng in ENGS:
                P.log[eng].append(("w", "s_ph", tgt))
                P.q[eng].append(lambda e, h=s_ph.h, v=tgt: e.wait_ge(h, v))

    def finish(sources):
        Pt = Plan()
        s_tp = sem("s_tp")
        for name, src in sources.items():
            if name in tap_outs:
                Pt.do("gpsimd", lambda e, d=tap_outs[name], s=src: e.dma_start(
                    out=d, in_=s), s_tp, 16)
        Pt.q["gpsimd"].append(lambda e, h=s_tp.h, v=s_tp.n: e.wait_ge(h, v) if v else None)
        Pt.emit(nc)
        return nc

    woT = OE(nc.sbuf_tensor("woT", [128, 8, 1024], BF16))
    sb_identb = OE(nc.sbuf_tensor("sb_identb", [128, 128], BF16))
    sb_identf = OE(nc.sbuf_tensor("sb_identf", [128, 128], F32))
    sb_onesf = OE(nc.sbuf_tensor("sb_onesf", [1, 128], F32))
    sb_onescf = OE(nc.sbuf_tensor("sb_onescf", [128, 1], F32))
    sb_ones64 = OE(nc.sbuf_tensor("sb_ones64", [128, 64], F32))
    sb_maskdd = OE(nc.sbuf_tensor("sb_maskdd", [128, 1024], BF16))
    sb_padbias = OE(nc.sbuf_tensor("sb_padbias", [128, 137], F32))
    sb_b1 = OE(nc.sbuf_tensor("sb_b1", [128, 32], F32))
    sb_b2 = OE(nc.sbuf_tensor("sb_b2", [1, 1024], F32))
    sb_eps = OE(nc.sbuf_tensor("sb_eps", [128, 1], F32))
    stats = OE(nc.sbuf_tensor("stats", [128, 64], F32))

    A4a = OE(nc.psum_tensor("A4a", [128, 1024], F32))
    A4b = OE(nc.psum_tensor("A4b", [128, 1024], F32))
    A4c = OE(nc.psum_tensor("A4c", [128, 1024], F32))
    B2a = OE(nc.psum_tensor("B2a", [128, 512], F32))
    B2b = OE(nc.psum_tensor("B2b", [128, 512], F32))

    wab = ExitStack()
    WE = wab.enter_context
    trash = WE(nc.sbuf_tensor("trash", [128, 2, 1024], F32))
    wqT = WE(nc.sbuf_tensor("wqT", [128, 8, 1024], BF16))
    wkT = WE(nc.sbuf_tensor("wkT", [128, 8, 1024], BF16))
    wvT = WE(nc.sbuf_tensor("wvT", [128, 8, 1024], BF16))

    wtmp = ExitStack()
    WT = wtmp.enter_context
    wnat = WT(nc.sbuf_tensor("wnat", [128, 8, 1024], BF16))
    wfb = WT(nc.sbuf_tensor("wfb", [128, 2, 4096], F32))
    absa = WT(nc.sbuf_tensor("absa", [128, 40], F32))
    sc1 = WT(nc.sbuf_tensor("sc1", [128, 8], F32))
    sinv = WT(nc.sbuf_tensor("sinv", [128, 2], F32))
    ternb = WT(nc.sbuf_tensor("ternb", [128, 4096], F32))
    tble = WT(nc.sbuf_tensor("tble", [128, 4096], F32))
    ternt = WT(nc.sbuf_tensor("ternt", [128, 4096], BF16))
    wstg = WT(nc.sbuf_tensor("wstg", [128, 4096], BF16))
    b1n = WT(nc.sbuf_tensor("b1n", [32, 128], F32))

    P = Plan()
    for dst, src in ((sb_identb, identb), (sb_identf, identf), (sb_onesf, onesf),
                     (sb_onescf, onescf), (sb_maskdd, maskdd),
                     (sb_padbias, padbias), (sb_b2, b2[None, :])):
        P.do("sync", lambda e, d=dst, s=src: e.dma_start(out=d[:], in_=s), s_cl, 16)
    P.do("sync", lambda e: e.dma_start(
        out=b1n[:], in_=b1.rearrange("(c p) -> c p", p=128)), s_cl, 16)
    P.do("vector", lambda e: e.memset(sb_eps[:], 1e-5), s_ln, 1)
    P.do("vector", lambda e: e.memset(sb_ones64[:], 1.0), s_ln, 1)

    for wi, (wsrc, wdst) in enumerate(((wq, wqT), (wk, wkT), (wv, wvT), (wo, woT))):
        if wi > 0:
            P.wait("gpsimd", s_we)
        P.do("gpsimd", lambda e, s=wsrc: e.dma_start(
            out=wnat[:], in_=s.rearrange("(r p) d -> p r d", p=128)), s_wl, 16)
        P.wait("tensor", s_wl)
        for r in range(8):
            ps = (A4a, A4b)[r % 2]
            if r >= 2:
                P.wait("tensor", s_we)
            for dc in range(8):
                P.do("tensor", lambda e, r=r, dc=dc, ps=ps: e.matmul(
                    ps[:, dc * 128:(dc + 1) * 128],
                    wnat[:, r, dc * 128:(dc + 1) * 128], sb_identb[:],
                    start=True, stop=True), s_wt if dc == 7 else None)
            P.wait("vector", s_wt)
            P.do("vector", lambda e, r=r, ps=ps, wd=wdst: e.tensor_copy(
                wd[:, :, r * 128:(r + 1) * 128],
                ps[:].rearrange("p (c f) -> p c f", f=128)), s_we, 1)

    P.wait("tensor", s_cl)
    P.do("tensor", lambda e: e.matmul(A4c[:, 0:32], b1n[:], sb_identf[0:32, 0:32],
                                      start=True, stop=True), s_wt, 1)
    P.wait("vector", s_wt)
    P.do("vector", lambda e: e.tensor_copy(sb_b1[:], A4c[:, 0:32]), s_we, 1)

    def w_abs_pass(wsrc, ntile, tw, acol0, fidx, nrows=1):
        for t in range(ntile):
            par = t % 2
            q = ("sync", "scalar")[t % 2]
            qs = (s_xl, s_x2)[t % 2]
            P.wait(q, s_ln, wfb_free[par])
            P.do(q, lambda e, t=t, par=par, s=wsrc, tw=tw, nr=nrows: e.dma_start(
                out=wfb[:, par, 0:tw].rearrange("p (r d) -> p r d", r=nr),
                in_=s.rearrange("(r p) d -> p r d", p=128)[:, t * nr:(t + 1) * nr]),
                qs, 16)
            P.wait("vector", qs)
            P.do("vector", lambda e, t=t, par=par, tw=tw, c=acol0: e.reduce_sum(
                absa[:, c + t:c + t + 1], wfb[:, par, 0:tw], axis=AX,
                apply_absolute_value=True), s_ln, 1)
            wfb_free[par] = s_ln.n
        P.wait("vector", s_ln)
        P.do("vector", lambda e, c=acol0, n=ntile, f=fidx: e.reduce_sum(
            sc1[:, f:f + 1], absa[:, c:c + n], axis=AX), s_ln, 1)
        P.wait("tensor", s_ln)
        P.do("tensor", lambda e, f=fidx: e.matmul(
            B2a[0:1, f:f + 1], sc1[:, f:f + 1], sb_onescf[:],
            start=True, stop=True), s_wt, 1)
        P.wait("vector", s_wt)
        P.do("vector", lambda e, f=fidx: e.tensor_scalar(
            sc1[0:1, 4 + f:5 + f], B2a[0:1, f:f + 1], 1.0 / (FF * D), 1e-8,
            ALU.mult, ALU.max), s_ln, 1)
        P.wait("vector", s_ln)
        P.do("vector", lambda e, f=fidx: e.reciprocal(
            sc1[0:1, 6 + f:7 + f], sc1[0:1, 4 + f:5 + f]), s_ln, 1)
        P.wait("tensor", s_ln)
        P.do("tensor", lambda e, f=fidx: e.matmul(
            B2b[:, f:f + 1], sb_onesf[:], sc1[0:1, 6 + f:7 + f],
            start=True, stop=True), s_wt, 1)
        P.wait("vector", s_wt)
        P.do("vector", lambda e, f=fidx: e.tensor_copy(
            sinv[:, f:f + 1], B2b[:, f:f + 1]), s_ln, 1)

    wfb_free = [0, 0]
    w_abs_pass(w1, 8, 4096, 0, 0, nrows=4)
    w_abs_pass(w2, 8, 4096, 32, 1)

    def w_tern_pass(wsrc, ntile, tw, fidx, wtd, nrows=1):
        nchunk = tw // 128
        for t in range(ntile):
            par = t % 2
            q = ("sync", "scalar")[t % 2]
            qs = (s_xl, s_x2)[t % 2]
            P.wait(q, s_ln, wfb_free[par])
            P.do(q, lambda e, t=t, par=par, s=wsrc, tw=tw, nr=nrows: e.dma_start(
                out=wfb[:, par, 0:tw].rearrange("p (r d) -> p r d", r=nr),
                in_=s.rearrange("(r p) d -> p r d", p=128)[:, t * nr:(t + 1) * nr]),
                qs, 16)
            P.wait("vector", qs)
            P.wait("vector", s_wt, tern_tr[0])
            P.do("vector", lambda e, par=par, tw=tw, f=fidx: e.tensor_scalar_mul(
                ternb[:, 0:tw], wfb[:, par, 0:tw], sinv[:, f:f + 1]), s_ln, 1)
            wfb_free[par] = s_ln.n
            P.wait("vector", s_ln)
            P.do("vector", lambda e, tw=tw: e.tensor_single_scalar(
                tble[:, 0:tw], ternb[:, 0:tw], -0.5, ALU.is_le), s_ln, 1)
            P.wait("vector", s_ln)
            P.do("vector", lambda e, tw=tw: e.scalar_tensor_tensor(
                ternt[:, 0:tw], ternb[:, 0:tw], 0.5, tble[:, 0:tw],
                ALU.is_ge, ALU.subtract), s_ap, 1)
            P.wait("tensor", s_ap)
            P.wait("tensor", s_ws, stg_done[0])
            for ch in range(nchunk):
                bank = (t * ((nchunk + 7) // 8) + ch // 8) % 2
                p8 = (A4a, A4b)[bank]
                if ch % 8 == 0:
                    P.wait("tensor", s_we, trn_bank_free[bank])
                P.do("tensor", lambda e, ch=ch, p8=p8: e.matmul(
                    p8[:, (ch % 8) * 128:(ch % 8 + 1) * 128],
                    ternt[:, ch * 128:(ch + 1) * 128], sb_identb[:],
                    start=True, stop=True), s_wt if ch % 8 == 7 or ch == nchunk - 1 else None)
                if ch % 8 == 7 or ch == nchunk - 1:
                    n8 = ch % 8 + 1
                    P.wait("vector", s_wt)
                    P.do("vector", lambda e, ch=ch, n8=n8, p8=p8: e.tensor_copy(
                        wstg[:, (ch - n8 + 1) * 128:(ch + 1) * 128],
                        p8[:, 0:n8 * 128]), s_we, 1)
                    trn_bank_free[bank] = s_we.n
            tern_tr[0] = s_wt.n
            P.wait("gpsimd", s_we)
            if nrows == 1:
                P.do("gpsimd", lambda e, t=t, w=wtd, nch=nchunk: e.dma_start(
                    out=w.rearrange("c p f -> p c f")[:, :, t * 128:(t + 1) * 128],
                    in_=wstg[:, 0:nch * 128].rearrange("p (c f) -> p c f", f=128)),
                    s_ws, 16)
            else:
                for r in range(nrows):
                    P.do("gpsimd", lambda e, t=t, r=r, w=wtd, nr=nrows: e.dma_start(
                        out=w.rearrange("c p f -> p c f")[
                            :, :, (t * nr + r) * 128:(t * nr + r + 1) * 128],
                        in_=wstg[:, r * 1024:(r + 1) * 1024].rearrange(
                            "p (c k) -> p c k", k=128)), s_ws, 16)
            stg_done[0] = s_ws.n

    tern_tr, stg_done = [0], [0]
    trn_bank_free = {0: 0, 1: 0}
    w_tern_pass(w1, 8, 4096, 0, w1t_d, nrows=4)
    w_tern_pass(w2, 8, 4096, 1, w2t_d)

    P.wait("gpsimd", s_ws)
    barrier(P)
    P.emit(nc)
    if upto == "w":
        return finish(dict(wqT=wqT[:], wvT=wvT[:], sinv=sinv[:], b1s=sb_b1[:],
                           w1t=w1t_d, w2t=w2t_d))
    wtmp.close()

    ab = ExitStack()
    AE = ab.enter_context
    xbf = AE(nc.sbuf_tensor("xbf", [128, 2, 4, 1024], F32))
    xnb = AE(nc.sbuf_tensor("xnb", [128, 2, 4, 1024], BF16))
    xnT = AE(nc.sbuf_tensor("xnT", [128, 2, 8, 512], BF16))
    kst = AE(nc.sbuf_tensor("kst", [128, 2, 4, 1024], BF16))
    v1s = AE(nc.sbuf_tensor("v1s", [128, 2, 4, 1040], BF16))
    qst = AE(nc.sbuf_tensor("qst", [128, 2, 2048], BF16))

    P = Plan()

    def ln_tiles(par, nrow, ap_src):
        sb = par * 32
        P.wait("scalar", s_xl)
        for r in range(nrow):
            P.do("scalar", lambda e, r=r: e.activation(
                trash[:, r % 2, :], ap_src(r), AF.Square,
                accum_out=stats[:, sb + r * 8 + 1:sb + r * 8 + 2]))
        for r in range(nrow):
            P.do("scalar", lambda e, r=r: e.activation(
                trash[:, r % 2, :], ap_src(r), AF.Copy,
                accum_out=stats[:, sb + r * 8:sb + r * 8 + 1]),
                s_rs if r == nrow - 1 else None)
        P.wait("vector", s_rs)
        for r in range(nrow):
            P.do("vector", lambda e, r=r: e.tensor_scalar_mul(
                stats[:, sb + r * 8 + 3:sb + r * 8 + 4],
                stats[:, sb + r * 8:sb + r * 8 + 1], -1.0 / D))
            P.do("vector", lambda e, r=r: e.tensor_scalar(
                stats[:, sb + r * 8 + 4:sb + r * 8 + 5],
                stats[:, sb + r * 8:sb + r * 8 + 1],
                stats[:, sb + r * 8:sb + r * 8 + 1], 1.0 / (D * D),
                ALU.mult, ALU.mult), s_ln if r == nrow - 1 else None)
        P.wait("vector", s_ln)
        for r in range(nrow):
            P.do("vector", lambda e, r=r: e.scalar_tensor_tensor(
                stats[:, sb + r * 8 + 5:sb + r * 8 + 6],
                stats[:, sb + r * 8 + 1:sb + r * 8 + 2], 1.0 / D,
                stats[:, sb + r * 8 + 4:sb + r * 8 + 5], ALU.mult, ALU.subtract),
                s_ln if r == nrow - 1 else None)
        P.wait("scalar", s_ln)
        for r in range(nrow):
            P.do("scalar", lambda e, r=r: e.activation(
                stats[:, sb + r * 8 + 6:sb + r * 8 + 7],
                stats[:, sb + r * 8 + 5:sb + r * 8 + 6],
                AF.Ln, bias=sb_eps[:]), s_rs if r == nrow - 1 else None)
        P.wait("scalar", s_rs)
        for r in range(nrow):
            P.do("scalar", lambda e, r=r: e.activation(
                stats[:, sb + r * 8 + 7:sb + r * 8 + 8],
                stats[:, sb + r * 8 + 6:sb + r * 8 + 7],
                AF.Exp, scale=-0.5), s_rs if r == nrow - 1 else None)
        P.wait("vector", s_rs)
        for r in range(nrow):
            P.do("vector", lambda e, r=r, par=par: e.tensor_scalar(
                xnb[:, par, r, :], ap_src(r), stats[:, sb + r * 8 + 3:sb + r * 8 + 4],
                stats[:, sb + r * 8 + 7:sb + r * 8 + 8], ALU.add, ALU.mult),
                s_ap if r == nrow - 1 else None)

    def transpose_tiles(par, nrow):
        P.wait("tensor", s_pe, vpe_free[0])
        P.wait("tensor", s_pe, vpe_free[1])
        for r in range(nrow):
            ps = (A4a, A4b)[r % 2]
            if r >= 2:
                P.wait("tensor", s_te, te_free[r % 2])
            for dc in range(8):
                P.do("tensor", lambda e, r=r, dc=dc, ps=ps, par=par: e.matmul(
                    ps[:, dc * 128:(dc + 1) * 128],
                    xnb[:, par, r, dc * 128:(dc + 1) * 128], sb_identb[:],
                    start=True, stop=True), s_tr if dc == 7 else None)
            P.wait("vector", s_tr)
            P.do("vector", lambda e, r=r, par=par, ps=ps: e.tensor_copy(
                xnT[:, par, :, r * 128:(r + 1) * 128],
                ps[:].rearrange("p (c f) -> p c f", f=128)), s_te, 1)
            te_free[r % 2] = s_te.n

    def kq_proj(par, ntok, wT, dst_cb):
        for pr in range(NPAIR):
            ps = (B2a, B2b)[pr % 2]
            P.wait("tensor", s_pe, pe_free[pr % 2])
            for dc in range(8):
                P.do("tensor", lambda e, pr=pr, dc=dc, ps=ps, par=par, wT=wT, n=ntok:
                     e.matmul(ps[:, 0:n], wT[:, dc, pr * 128:(pr + 1) * 128],
                              xnT[:, par, dc, 0:n], start=(dc == 0), stop=(dc == 7)),
                     s_pj if dc == 7 else None)
            P.wait("vector", s_pj)
            dst_cb(pr, ps)
            pe_free[pr % 2] = s_pe.n

    def v_proj(par, ntsub):
        for tsub in range(ntsub):
            ps = (A4a, A4b)[tsub % 2]
            P.wait("tensor", s_pe, vpe_free[tsub % 2])
            P.wait("tensor", s_te, te_free[tsub % 2])
            for dc in range(8):
                for nh in range(2):
                    P.do("tensor", lambda e, dc=dc, nh=nh, tsub=tsub, ps=ps, par=par:
                         e.matmul(ps[:, nh * 512:(nh + 1) * 512],
                                  xnT[:, par, dc, tsub * 128:(tsub + 1) * 128],
                                  wvT[:, dc, nh * 512:(nh + 1) * 512],
                                  start=(dc == 0), stop=(dc == 7)),
                         s_pj if (dc == 7 and nh == 1) else None)
            P.wait("vector", s_pj)
            P.do("vector", lambda e, tsub=tsub, par=par, ps=ps: e.tensor_copy(
                v1s[:, par, tsub, :].rearrange("p (h c) -> p h c", c=65)[:, :, 0:64],
                ps[:].rearrange("p (h c) -> p h c", c=64)), s_pe, 1)
            P.do("vector", lambda e, tsub=tsub, par=par: e.memset(
                v1s[:, par, tsub, :].rearrange("p (h c) -> p h c", c=65)[:, :, 64:65],
                1.0), s_pe, 1)
            vpe_free[tsub % 2] = s_pe.n

    te_free, pe_free, vpe_free = [0, 0], [0, 0], [0, 0]
    xbf_free = [0, 0]
    xnb_free = [0, 0]
    ss_done = [0, 0]
    ap_done = {}

    chunks = ([("a", j) for j in range(NSLOT)] + [("b", t) for t in range(16)])

    def load_chunk(i):
        kind, idx = chunks[i]
        par = i % 2
        P.wait("sync", s_ap, xbf_free[par])
        if kind == "a":
            P.do("sync", lambda e, idx=idx, par=par: e.dma_start(
                out=xbf[:, par, 0:2, :],
                in_=xq[idx * 256:(idx + 1) * 256, :].rearrange(
                    "(r p) d -> p r d", p=128)), s_xl, 16)
        else:
            P.do("sync", lambda e, idx=idx, par=par: e.dma_start(
                out=xbf[:, par, :, :],
                in_=xb[idx * 512:(idx + 1) * 512, :].rearrange(
                    "(r p) d -> p r d", p=128)), s_xl, 16)

    def ln_chunk(i):
        kind, idx = chunks[i]
        par = i % 2
        nrow = 2 if kind == "a" else 4
        P.wait("vector", s_xl)
        P.wait("vector", s_tr, xnb_free[par])
        ln_tiles(par, nrow, lambda r, par=par: xbf[:, par, r, :])
        xbf_free[par] = s_ap.n
        ap_done[i] = s_ap.n

    def work_chunk(i):
        kind, idx = chunks[i]
        par = i % 2
        nrow = 2 if kind == "a" else 4
        P.wait("tensor", s_ap, ap_done[i])
        transpose_tiles(par, nrow)
        xnb_free[par] = s_tr.n
        P.wait("tensor", s_te)
        P.wait("vector", s_ss, ss_done[par])

        if kind == "a":
            def q_evac(pr, ps, par=par):
                P.do("vector", lambda e, pr=pr, ps=ps: e.tensor_copy(
                    qst[:, par, pr * 256:(pr + 1) * 256], ps[:, 0:256]), s_pe, 1)

            def k_evac(pr, ps, par=par):
                P.do("vector", lambda e, pr=pr, ps=ps: e.tensor_copy(
                    kst[:, par, 0:2, pr * 128:(pr + 1) * 128],
                    ps[:, 0:256].rearrange("p (t k) -> p t k", k=128)), s_pe, 1)

            kq_proj(par, 256, wqT, q_evac)
            kq_proj(par, 256, wkT, k_evac)
            v_proj(par, 2)
            P.wait("gpsimd", s_pe)
            P.do("gpsimd", lambda e, idx=idx, par=par: e.dma_start(
                out=qt_d[idx], in_=qst[:, par, :]), s_ss, 16)
            P.wait("gpsimd", s_te)
            P.do("gpsimd", lambda e, idx=idx, par=par: e.dma_start(
                out=xnt_d[idx].rearrange("p (c f) -> p c f", f=256),
                in_=xnT[:, par, :, 0:256]), s_ss, 16)
            P.do("gpsimd", lambda e, idx=idx, par=par: e.dma_start(
                out=kto_d[2 * idx:2 * idx + 2].rearrange("t p c -> p t c"),
                in_=kst[:, par, 0:2, :]), s_ss, 16)
            P.do("gpsimd", lambda e, idx=idx, par=par: e.dma_start(
                out=v1o_d[2 * idx:2 * idx + 2].rearrange("t p c -> p t c"),
                in_=v1s[:, par, 0:2, :]), s_ss, 16)
        else:
            def k_evac_b(pr, ps, par=par):
                P.do("vector", lambda e, pr=pr, ps=ps: e.tensor_copy(
                    kst[:, par, :, pr * 128:(pr + 1) * 128],
                    ps[:, 0:512].rearrange("p (t k) -> p t k", k=128)), s_pe, 1)

            kq_proj(par, 512, wkT, k_evac_b)
            v_proj(par, 4)
            P.wait("gpsimd", s_pe)
            P.do("gpsimd", lambda e, idx=idx, par=par: e.dma_start(
                out=kt_d[4 * idx:4 * idx + 4].rearrange("t p c -> p t c"),
                in_=kst[:, par, :, :]), s_ss, 16)
            P.do("gpsimd", lambda e, idx=idx, par=par: e.dma_start(
                out=v1_d[4 * idx:4 * idx + 4].rearrange("t p c -> p t c"),
                in_=v1s[:, par, :, :]), s_ss, 16)
        ss_done[par] = s_ss.n

    for i in range(len(chunks) + 1):
        if i < len(chunks):
            load_chunk(i)
            ln_chunk(i)
        if i >= 1:
            work_chunk(i - 1)

    P.wait("gpsimd", s_ss)
    barrier(P)
    P.emit(nc)
    if upto == "ab":
        return finish(dict(kt=kt_d, v1=v1_d, kto=kto_d, v1o=v1o_d, qt=qt_d,
                           xnt=xnt_d))
    ab.close()
    wab.close()

    dph = ExitStack()
    DP = dph.enter_context
    amidT = DP(nc.sbuf_tensor("amidT", [128, 2, 8, 256], BF16))
    ktm = DP(nc.sbuf_tensor("ktm", [128, 2, 8, 1024], BF16))
    v1m = DP(nc.sbuf_tensor("v1m", [128, 2, 8, 1040], BF16))
    qts = DP(nc.sbuf_tensor("qts", [128, 2, 2048], BF16))
    ptb = DP(nc.sbuf_tensor("ptb", [128, 3, 1024], BF16))
    acc = DP(nc.sbuf_tensor("acc", [65, 4096], F32))
    w1s = DP(nc.sbuf_tensor("w1s", [128, 8, 512], BF16))
    w2s = DP(nc.sbuf_tensor("w2s", [128, 32, 512], BF16))
    xnte = DP(nc.sbuf_tensor("xnte", [128, 8, 256], BF16))
    hts = DP(nc.sbuf_tensor("hts", [128, 32, 256], BF16))
    xqe = DP(nc.sbuf_tensor("xqe", [128, 2, 512], F32))
    yb = DP(nc.sbuf_tensor("yb", [128, 2, 512], F32))

    P = Plan()
    PBCOL = [sum(GJ[:k]) for k in range(NSLOT)]
    LOOK = 2
    sg_free = [0, 0]
    pt_free = [0, 0, 0]
    pv_free = [0, 0]
    km_free = [[0, 0], [0, 0]]
    qt_free = [0, 0]
    am_free = [0, 0]
    am_rd = [0, 0]
    acc_free = [0]
    pending_norm = [None]
    pending_e = [None]
    uidx = 0
    xnte_free = [0, 0]
    xq_free = [0]
    yb_free = [0]
    ht_free = {}
    w1_free = [0, 0]
    w2_free = [0, 0]
    a4c_norm_done = [0]

    def make_epieces(jj):
        epar = jj % 2
        pieces = []
        st = {}

        def p_loads():
            P.wait("sync", s_f1, xnte_free[0])
            P.do("sync", lambda e, jj=jj: e.dma_start(
                out=xnte[:],
                in_=xnt_d[jj].rearrange("p (c f) -> p c f", f=256)), s_el, 16)
            st["xnte"] = s_el.n
            P.wait("sync", s_yv, xq_free[0])
            P.do("sync", lambda e, jj=jj: e.dma_start(
                out=xqe[:],
                in_=xq[jj * 256:(jj + 1) * 256, 0:512].rearrange(
                    "(r p) d -> p r d", p=128)), s_el, 16)
            st["xqe"] = s_el.n
        pieces.append(p_loads)

        gl_tgt = {}
        for c in range(8):
            def p_ffn1(c=c):
                fc0 = c * 4
                P.wait("sync", s_f1, w1_free[0])
                P.do("sync", lambda e, fc0=fc0: e.dma_start(
                    out=w1s[:],
                    in_=w1t_d.rearrange("c p f -> p c f")[:, :, fc0 * 128:(fc0 + 4) * 128]),
                    s_el, 16)
                ld = s_el.n
                P.wait("tensor", s_el, ld)
                P.wait("tensor", s_el, st["xnte"])
                if c == 0:
                    P.wait("tensor", s_am, a4c_norm_done[0])
                for q in range(4):
                    fc = fc0 + q
                    hb = (fc % 2) * 512
                    P.wait("tensor", s_gl, gl_tgt.get(fc - 2, 0))
                    for dc in range(8):
                        P.do("tensor", lambda e, fc=fc, hb=hb, dc=dc,
                             epar=epar, q=q: e.matmul(
                                 A4c[:, hb:hb + 256],
                                 w1s[:, dc, q * 128:(q + 1) * 128],
                                 xnte[:, dc, :],
                                 start=(dc == 0), stop=(dc == 7)),
                             s_f1 if dc == 7 else None)
                    P.wait("scalar", s_f1)
                    P.wait("scalar", s_f2, ht_free.get(fc, 0))
                    P.do("scalar", lambda e, fc=fc, hb=hb: e.activation(
                        hts[:, fc, :], A4c[:, hb:hb + 256], AF.Gelu,
                        bias=sb_b1[:, fc:fc + 1]), s_gl, 1)
                    gl_tgt[fc] = s_gl.n
                w1_free[0] = s_f1.n
            pieces.append(p_ffn1)

        def p_gl_all():
            P.do("scalar", lambda e: e.copy(stats[0:1, 28:29], stats[0:1, 28:29]),
                 s_gl, 1)
            st["gl_all"] = s_gl.n
        pieces.append(p_gl_all)

        def p_w2load(h):
            def fn():
                P.wait("sync", s_f2, w2_free[0])
                P.do("sync", lambda e, h=h: e.dma_start(
                    out=w2s[:],
                    in_=w2t_d[:, :, h * 512:(h + 1) * 512].rearrange(
                        "c p f -> p c f")), s_el, 16)
                st["w2"] = s_el.n
            return fn
        pieces.append(p_w2load(0))

        for h in range(2):
            def p_pass(h=h):
                P.wait("tensor", s_el, st["w2"])
                P.wait("tensor", s_gl, st["gl_all"])
                P.wait("tensor", s_am, a4c_norm_done[0])
                if h == 1:
                    P.wait("tensor", s_yv, st["yv0"])
                for t in range(2):
                    P.do("tensor", lambda e, t=t, h=h: e.matmul(
                        A4c[:, t * 512:(t + 1) * 512], sb_onesf[0:1, :],
                        sb_b2[0:1, h * 512:(h + 1) * 512], start=True, stop=False))
                for fc in range(32):
                    for t in range(2):
                        P.do("tensor", lambda e, fc=fc, t=t: e.matmul(
                            A4c[:, t * 512:(t + 1) * 512],
                            hts[:, fc, t * 128:(t + 1) * 128],
                            w2s[:, fc, :], start=False, stop=False))
                for pr in range(NPAIR):
                    for t in range(2):
                        P.do("tensor", lambda e, pr=pr, t=t, epar=epar:
                             e.matmul(A4c[:, t * 512:(t + 1) * 512],
                                      amidT[:, epar, pr, t * 128:(t + 1) * 128],
                                      woT[:, pr, h * 512:(h + 1) * 512],
                                      start=False, stop=(pr == NPAIR - 1)),
                             s_f2 if (pr == NPAIR - 1 and t == 1) else None)
                w2_free[0] = s_f2.n
                if h == 1:
                    am_rd[epar] = s_f2.n
                    for fc in range(32):
                        ht_free[fc] = s_f2.n
            pieces.append(p_pass)

            def p_resid(h=h, jj=jj):
                P.wait("vector", s_f2)
                P.wait("vector", s_el, st["xqe"])
                P.wait("vector", s_ys, yb_free[0])
                for t in range(2):
                    P.do("vector", lambda e, t=t: e.tensor_add(
                        yb[:, t, :], A4c[:, t * 512:(t + 1) * 512],
                        xqe[:, t, :]), s_yv, 1)
                if h == 0:
                    st["yv0"] = s_yv.n
                else:
                    xq_free[0] = s_yv.n
                P.wait("gpsimd", s_yv)
                P.do("gpsimd", lambda e, jj=jj, h=h: e.dma_start(
                    out=y[jj * 256:(jj + 1) * 256,
                          h * 512:(h + 1) * 512].rearrange(
                        "(r p) d -> p r d", p=128),
                    in_=yb[:]), s_ys, 16)
                yb_free[0] = s_ys.n
            pieces.append(p_resid)
            if h == 0:
                def p_xq1(jj=jj):
                    P.wait("sync", s_yv, st["yv0"])
                    P.do("sync", lambda e, jj=jj: e.dma_start(
                        out=xqe[:],
                        in_=xq[jj * 256:(jj + 1) * 256, 512:1024].rearrange(
                            "(r p) d -> p r d", p=128)), s_el, 16)
                    st["xqe"] = s_el.n
                pieces.append(p_xq1)
                pieces.append(p_w2load(1))

        def p_close():
            xnte_free[0] = s_f1.n
        pieces.append(p_close)
        return pieces

    for j in reversed(range(NSLOT)):
        spar = j % 2
        P.wait("sync", s_qk, qt_free[spar])
        P.do("sync", lambda e, j=j, spar=spar: e.dma_start(
            out=qts[:, spar, :], in_=qt_d[j]), s_kl, 16)
        groups = GJ[j]
        macros = [list(range(m, min(m + MACRO, groups)))
                  for m in range(0, groups, MACRO)]
        macros.append(None)

        units = []
        for mi, mg in enumerate(macros):
            diag = mg is None
            ng = 1 if diag else len(mg)
            for pr in range(NPAIR):
                for gi in range(ng):
                    units.append(dict(mi=mi, mg=mg, diag=diag, ng=ng, pr=pr,
                                      gi=gi, mpar=mi % 2))
        nun = len(units)
        kl_tgt = {}

        def emit_macro_load(mi, j=j, macros=macros):
            mg = macros[mi]
            mpar = mi % 2
            P.wait("sync", s_qk, km_free[mpar][0])
            P.wait("sync", s_pv, km_free[mpar][1])
            if mg is None:
                P.do("sync", lambda e, j=j, mpar=mpar: e.dma_start(
                    out=ktm[:, mpar, 0:2, :],
                    in_=kto_d[2 * j:2 * j + 2].rearrange("t p c -> p t c")), s_kl, 16)
                P.do("sync", lambda e, j=j, mpar=mpar: e.dma_start(
                    out=v1m[:, mpar, 0:2, :],
                    in_=v1o_d[2 * j:2 * j + 2].rearrange("t p c -> p t c")), s_kl, 16)
            else:
                k0, nb = 2 * mg[0], 2 * len(mg)
                P.do("sync", lambda e, k0=k0, nb=nb, mpar=mpar: e.dma_start(
                    out=ktm[:, mpar, 0:nb, :],
                    in_=kt_d[k0:k0 + nb].rearrange("t p c -> p t c")), s_kl, 16)
                P.do("sync", lambda e, k0=k0, nb=nb, mpar=mpar: e.dma_start(
                    out=v1m[:, mpar, 0:nb, :],
                    in_=v1_d[k0:k0 + nb].rearrange("t p c -> p t c")), s_kl, 16)
            kl_tgt[mi] = s_kl.n

        qk_done = {}
        ex_done = {}
        ex_half = {}
        mk_done = {}
        last_qk_of_macro = {}
        epieces = pending_e[0] or []
        pending_e[0] = None
        cadence = max(1, (nun - 4) // max(1, len(epieces))) if epieces else 0
        ep_i = 0

        for step in range(nun + LOOK):
            if step == 2 and pending_norm[0] is not None:
                pending_norm[0]()
                pending_norm[0] = None
            if (epieces and step >= 3 and ep_i < len(epieces)
                    and (step - 3) % cadence == 0):
                epieces[ep_i]()
                ep_i += 1
            if step == 0:
                emit_macro_load(0)
            if step < nun:
                u = units[step]
                mi, pr, gi, mpar, diag = u["mi"], u["pr"], u["gi"], u["mpar"], u["diag"]
                if pr == 0 and gi == 0 and mi + 1 < len(macros):
                    emit_macro_load(mi + 1)
                sg_i = uidx % 2
                ppar = uidx % 3
                u["sg_i"], u["ppar"] = sg_i, ppar
                uidx += 1
                sgp = (A4a, A4b)[sg_i]
                u["sgp"] = sgp
                if pr == 0 and gi == 0:
                    P.wait("tensor", s_kl, kl_tgt[mi])
                P.wait("tensor", s_ex, sg_free[sg_i])
                for (kb, hp, off) in ((0, 0, 0), (1, 0, 256), (0, 1, 512), (1, 1, 768)):
                    first = off % 512 == 0
                    tp = (hp * 64, 0)
                    P.do("tensor", lambda e, kb=kb, hp=hp, off=off, tp=tp, gi=gi,
                         pr=pr, sgp=sgp, mpar=mpar, spar=spar, first=first:
                         e.matmul(sgp[:, off:off + 256],
                                  ktm[hp * 64:(hp + 1) * 64, mpar, 2 * gi + kb,
                                      pr * 128:(pr + 1) * 128],
                                  qts[hp * 64:(hp + 1) * 64, spar,
                                      pr * 256:(pr + 1) * 256],
                                  start=first, stop=not first, tile_position=tp),
                         s_qk if off in (256, 768) else None)
                qk_done[step] = s_qk.n
                last_qk_of_macro[mi] = s_qk.n
                col = 136 if diag else PBCOL[j] + u["mg"][gi]
                P.wait("scalar", s_qk, qk_done[step] - 1)
                P.wait("scalar", s_pv, pt_free[ppar])
                P.do("scalar", lambda e, ppar=ppar, sgp=sgp, col=col: e.activation(
                    ptb[:, ppar, 0:512], sgp[:, 0:512], AF.Exp,
                    bias=sb_padbias[:, col:col + 1], scale=0.125), s_ex, 1)
                ex_half[step] = s_ex.n
                P.wait("scalar", s_qk, qk_done[step])
                P.do("scalar", lambda e, ppar=ppar, sgp=sgp, col=col: e.activation(
                    ptb[:, ppar, 512:1024], sgp[:, 512:1024], AF.Exp,
                    bias=sb_padbias[:, col:col + 1], scale=0.125), s_ex, 1)
                ex_done[step] = s_ex.n
                sg_free[sg_i] = s_ex.n
                if diag:
                    P.wait("vector", s_ex, ex_done[step])
                    P.do("vector", lambda e, ppar=ppar: e.tensor_mul(
                        ptb[:, ppar, :], ptb[:, ppar, :], sb_maskdd[:]), s_mk, 1)
                    P.do("vector", lambda e: e.tensor_copy(
                        stats[0:1, 29:30], stats[0:1, 29:30]), s_mk, 1)
                    mk_done[step] = s_mk.n
            if step >= LOOK:
                vstep = step - LOOK
                v = units[vstep]
                mi, pr, gi, mpar, diag = v["mi"], v["pr"], v["gi"], v["mpar"], v["diag"]
                pvp = (B2a, B2b)[pr % 2]
                ppar = v["ppar"]
                if gi == 0:
                    P.wait("tensor", s_ac, pv_free[pr % 2])
                if diag:
                    P.wait("tensor", s_mk, mk_done[vstep])
                else:
                    P.wait("tensor", s_ex, ex_half[vstep] + 1)
                for (kb, hp, poff) in ((0, 0, 0), (1, 0, 0), (0, 1, 256), (1, 1, 256)):
                    first = gi == 0 and kb == 0 and hp == 0
                    if hp == 1 and kb == 0 and not diag:
                        P.wait("tensor", s_ex, ex_done[vstep] + 1)
                    P.do("tensor", lambda e, kb=kb, hp=hp, poff=poff, gi=gi, pr=pr,
                         pvp=pvp, mpar=mpar, ppar=ppar, first=first:
                         e.matmul(pvp[0:65, poff:poff + 256],
                                  v1m[:, mpar, 2 * gi + kb,
                                      (2 * pr + hp) * 65:(2 * pr + hp + 1) * 65],
                                  ptb[:, ppar, hp * 512 + kb * 256:
                                      hp * 512 + kb * 256 + 256],
                                  start=first, stop=False),
                         s_pv if (kb == 1 and hp == 1) else None)
                pt_free[ppar] = s_pv.n
                if gi == v["ng"] - 1:
                    P.wait("vector", s_pv)
                    if mi == 0:
                        P.wait("vector", s_am, acc_free[0])
                        P.do("vector", lambda e, pr=pr, pvp=pvp: e.tensor_copy(
                            acc[:, pr * 512:(pr + 1) * 512], pvp[0:65, :]), s_ac, 1)
                    else:
                        P.do("vector", lambda e, pr=pr, pvp=pvp: e.tensor_add(
                            acc[:, pr * 512:(pr + 1) * 512],
                            acc[:, pr * 512:(pr + 1) * 512], pvp[0:65, :]), s_ac, 1)
                    pv_free[pr % 2] = s_ac.n
                    if pr == NPAIR - 1:
                        km_free[mpar] = [last_qk_of_macro[mi], s_pv.n]
        while epieces and ep_i < len(epieces):
            epieces[ep_i]()
            ep_i += 1
        P.do("scalar", lambda e: e.copy(stats[0:1, 28:29], stats[0:1, 28:29]),
             s_ex, 1)
        qt_free[spar] = s_qk.n
        P.do("vector", lambda e: e.tensor_copy(
            stats[0:1, 29:30], stats[0:1, 29:30]), s_ac, 1)
        ac_tgt = s_ac.n
        P.wait("scalar", s_ac, ac_tgt)
        for pr in range(NPAIR):
            P.do("scalar", lambda e, pr=pr: e.activation(
                acc[64:65, pr * 512:(pr + 1) * 512],
                acc[64:65, pr * 512:(pr + 1) * 512], AF.Ln),
                s_nm if pr == NPAIR - 1 else None)
        P.wait("scalar", s_nm)
        for pr in range(NPAIR):
            P.do("scalar", lambda e, pr=pr: e.activation(
                acc[64:65, pr * 512:(pr + 1) * 512],
                acc[64:65, pr * 512:(pr + 1) * 512], AF.Exp, scale=-1.0),
                s_nm if pr == NPAIR - 1 else None)
        P.do("scalar", lambda e: e.copy(stats[0:1, 28:29], stats[0:1, 28:29]), s_nm, 1)
        nm_tgt = s_nm.n

        def make_norm(j=j, spar=spar, nm_tgt=nm_tgt):
            def emit_norm():
                P.wait("tensor", s_nm, nm_tgt)
                P.wait("tensor", s_gl)
                P.wait("tensor", s_yv)
                P.wait("vector", s_f2, am_rd[spar])
                rp_tgt = {}
                for hp2 in range(H // 2 + 1):
                    if hp2 < H // 2:
                        P.wait("tensor", s_am, am_free[0])
                        P.do("tensor", lambda e, hp2=hp2: e.matmul(
                            A4c[0:64, 0:512] if hp2 % 2 == 0 else A4c[0:64, 512:1024],
                            sb_ones64[64:65, :],
                            acc[64:65, hp2 * 512:(hp2 + 1) * 512],
                            start=True, stop=True), s_rp, 1)
                        rp_tgt[hp2] = s_rp.n
                    if hp2 >= 1:
                        k = hp2 - 1
                        P.wait("vector", s_rp, rp_tgt[k])
                        for e2 in range(2):
                            hh = 2 * k + e2
                            P.do("vector", lambda e, hh=hh, e2=e2, k=k, spar=spar:
                                 e.tensor_mul(
                                amidT[(hh % 2) * 64:(hh % 2) * 64 + 64, spar,
                                      hh // 2, :],
                                acc[0:64, hh * 256:(hh + 1) * 256],
                                (A4c[0:64, 0:512] if k % 2 == 0 else
                                 A4c[0:64, 512:1024])[:, e2 * 256:(e2 + 1) * 256]),
                                s_am, 1)
                        am_free[0] = s_am.n
                P.do("vector", lambda e: e.tensor_copy(
                    stats[0:1, 29:30], stats[0:1, 29:30]), s_am, 1)
                acc_free[0] = s_am.n
                a4c_norm_done[0] = s_am.n
            return emit_norm

        pending_norm[0] = make_norm()
        pending_e[0] = make_epieces(j)

    if pending_norm[0] is not None:
        pending_norm[0]()
        pending_norm[0] = None
    for pc in (pending_e[0] or []):
        pc()
    pending_e[0] = None
    P.wait("gpsimd", s_ys)
    barrier(P)
    P.emit(nc)
    if upto == "d":
        return finish(dict(acc=acc[:]))
    dph.close()
    outer.close()
    return nc



def _consts():
    identb = np.eye(128, dtype=ml_dtypes.bfloat16)
    identf = np.eye(128, dtype=np.float32)
    onesf = np.ones((1, 128), dtype=np.float32)
    onescf = np.ones((128, 1), dtype=np.float32)
    tri = np.triu(np.ones((128, 128), np.float32))
    ones = np.ones((128, 128), np.float32)
    zero = np.zeros((128, 128), np.float32)
    m1 = np.concatenate([tri, ones], axis=1)
    m0 = np.concatenate([zero, tri], axis=1)
    maskdd = np.concatenate([m1, m0, m1, m0], axis=1).astype(ml_dtypes.bfloat16)
    return identb, identf, onesf, onescf, maskdd


def _padbias(cc):
    pb = np.full((128, 137), KILL, np.float32)
    col = 0
    for j in range(NSLOT):
        T = tile_T(j, cc)
        pb[:, col:col + T] = -SHIFT
        col += GJ[j]
    pb[:, 136] = -SHIFT
    return pb


def kernel(L0_attn_in_features, L0_attn_Wq, L0_attn_Wk, L0_attn_Wv, L0_attn_Wo,
           L0_ffn_W1, L0_ffn_b1, L0_ffn_W2, L0_ffn_b2):
    x = np.ascontiguousarray(np.asarray(L0_attn_in_features, dtype=np.float32))
    identb, identf, onesf, onescf, maskdd = _consts()
    in_maps = []
    for c in range(8):
        b, cc = c // 4, c % 4
        rows = np.concatenate([np.arange(256 * tile_T(j, cc),
                                         256 * tile_T(j, cc) + 256)
                               for j in range(NSLOT)])
        in_maps.append(dict(
            xq=np.ascontiguousarray(x[b][rows]),
            xb=np.ascontiguousarray(x[b]),
            wq=np.ascontiguousarray(np.asarray(L0_attn_Wq, np.float32)),
            wk=np.ascontiguousarray(np.asarray(L0_attn_Wk, np.float32)),
            wv=np.ascontiguousarray(np.asarray(L0_attn_Wv, np.float32)),
            wo=np.ascontiguousarray(np.asarray(L0_attn_Wo, np.float32)),
            w1=np.ascontiguousarray(np.asarray(L0_ffn_W1, np.float32)),
            b1=np.ascontiguousarray(np.asarray(L0_ffn_b1, np.float32)),
            w2=np.ascontiguousarray(np.asarray(L0_ffn_W2, np.float32)),
            b2=np.ascontiguousarray(np.asarray(L0_ffn_b2, np.float32)),
            identb=identb, identf=identf, onesf=onesf, onescf=onescf,
            maskdd=maskdd, padbias=_padbias(cc),
        ))
    nc = build_nc()
    res = run_bass_kernel_spmd(nc, in_maps, core_ids=list(range(8)))
    out = np.empty((B, S, D), np.float32)
    for c in range(8):
        b, cc = c // 4, c % 4
        yc = res.results[c]["y"]
        for j in range(NSLOT):
            T = tile_T(j, cc)
            out[b, 256 * T:256 * T + 256] = yc[256 * j:256 * j + 256]
    return out


if __name__ == "__main__":
    nc = build_nc()
    print("built ok,", nc.n_instructions(), "instructions")


# version 22
# speedup vs baseline: 1.0981x; 1.0419x over previous
import numpy as np
import ml_dtypes
from contextlib import ExitStack
import concourse.bass as bass
import concourse.mybir as mybir
from concourse.bass_utils import run_bass_kernel_spmd

F32 = mybir.dt.float32
BF16 = mybir.dt.bfloat16
AF = mybir.ActivationFunctionType
ALU = mybir.AluOpType
AX = mybir.AxisListType.X

B, S, D, H, DH, FF = 2, 8192, 1024, 16, 64, 4096
NPAIR, NSLOT, QT, KBLK = 8, 8, 256, 64
KJ = [64 - 8 * j for j in range(NSLOT)]
GJ = [(k - 2) // 2 for k in KJ]
MACRO = 4
SHIFT = 4.0
KILL = -1e15
ENGS = ("tensor", "vector", "scalar", "sync", "gpsimd")


def tile_T(j, cc):
    return 31 - 4 * j - ((cc + j) % 4)


class Sem:
    def __init__(self, stack, nc, name):
        self.h = stack.enter_context(nc.semaphore(name))
        self.n = 0


_PLAN_LOGS = []


class Plan:
    def __init__(self):
        self.q = {k: [] for k in ENGS}
        self.log = {k: [] for k in ENGS}

    def wait(self, eng, sem, tgt=None):
        v = sem.n if tgt is None else tgt
        if v and v > 0:
            self.log[eng].append(("w", sem.h.name if hasattr(sem.h, "name") else id(sem), v))
            self.q[eng].append(lambda e, h=sem.h, v=v: e.wait_ge(h, v))

    def do(self, eng, builder, sem=None, k=1):
        if sem is not None:
            sem.n += k
            self.log[eng].append(("i", sem.h.name if hasattr(sem.h, "name") else id(sem), k))
            self.q[eng].append(lambda e, b=builder, h=sem.h, kk=k: b(e).then_inc(h, kk))
        else:
            self.log[eng].append(("i", None, 0))
            self.q[eng].append(lambda e, b=builder: b(e))

    def emit(self, nc):
        _PLAN_LOGS.append(self.log)
        with nc.Block() as block:
            for name in ENGS:
                fns = self.q[name]
                getattr(block, name)(lambda e, fns=fns: [fn(e) for fn in fns])


def build_nc(upto=None, taps=()):
    nc = bass.Bass("TRN2", target_bir_lowering=False)

    xq = nc.dram_tensor("xq", [NSLOT * QT, D], F32, kind="ExternalInput").ap()
    xb = nc.dram_tensor("xb", [S, D], F32, kind="ExternalInput").ap()
    wq = nc.dram_tensor("wq", [D, D], F32, kind="ExternalInput").ap()
    wk = nc.dram_tensor("wk", [D, D], F32, kind="ExternalInput").ap()
    wv = nc.dram_tensor("wv", [D, D], F32, kind="ExternalInput").ap()
    wo = nc.dram_tensor("wo", [D, D], F32, kind="ExternalInput").ap()
    w1 = nc.dram_tensor("w1", [FF, D], F32, kind="ExternalInput").ap()
    b1 = nc.dram_tensor("b1", [FF], F32, kind="ExternalInput").ap()
    w2 = nc.dram_tensor("w2", [D, FF], F32, kind="ExternalInput").ap()
    b2 = nc.dram_tensor("b2", [D], F32, kind="ExternalInput").ap()
    identb = nc.dram_tensor("identb", [128, 128], BF16, kind="ExternalInput").ap()
    identf = nc.dram_tensor("identf", [128, 128], F32, kind="ExternalInput").ap()
    onesf = nc.dram_tensor("onesf", [1, 128], F32, kind="ExternalInput").ap()
    onescf = nc.dram_tensor("onescf", [128, 1], F32, kind="ExternalInput").ap()
    maskdd = nc.dram_tensor("maskdd", [128, 1024], BF16, kind="ExternalInput").ap()
    padbias = nc.dram_tensor("padbias", [128, 137], F32, kind="ExternalInput").ap()
    y = nc.dram_tensor("y", [NSLOT * QT, D], F32, kind="ExternalOutput").ap()

    kt_d = nc.dram_tensor("kt_d", [KBLK, 128, 1024], BF16).ap()
    v1_d = nc.dram_tensor("v1_d", [KBLK, 128, 1040], BF16).ap()
    kto_d = nc.dram_tensor("kto_d", [16, 128, 1024], BF16).ap()
    v1o_d = nc.dram_tensor("v1o_d", [16, 128, 1040], BF16).ap()
    qt_d = nc.dram_tensor("qt_d", [NSLOT, 128, 2048], BF16).ap()
    xnt_d = nc.dram_tensor("xnt_d", [NSLOT, 128, 2048], BF16).ap()
    w1t_d = nc.dram_tensor("w1t_d", [8, 128, FF], BF16).ap()
    w2t_d = nc.dram_tensor("w2t_d", [32, 128, D], BF16).ap()

    tap_outs = {}
    for tname, tshape, tdt in taps:
        tap_outs[tname] = nc.dram_tensor("tap_" + tname, tshape, tdt,
                                         kind="ExternalOutput").ap()

    outer = ExitStack()
    OE = outer.enter_context

    def sem(name):
        return Sem(outer, nc, name)

    s_cl, s_wl, s_wt, s_we, s_ws = [sem(n) for n in ("s_cl", "s_wl", "s_wt", "s_we", "s_ws")]
    s_xl, s_ln, s_rs, s_ap, s_tr = [sem(n) for n in ("s_xl", "s_ln", "s_rs", "s_ap", "s_tr")]
    s_x2 = sem("s_x2")
    s_te, s_pj, s_pe, s_ss = [sem(n) for n in ("s_te", "s_pj", "s_pe", "s_ss")]
    s_kl, s_qk, s_ex, s_mk, s_pv = [sem(n) for n in ("s_kl", "s_qk", "s_ex", "s_mk", "s_pv")]
    s_ac, s_nm, s_rp, s_am = [sem(n) for n in ("s_ac", "s_nm", "s_rp", "s_am")]
    s_el, s_f1, s_gl, s_f2, s_yv, s_ys = [sem(n) for n in
                                          ("s_el", "s_f1", "s_gl", "s_f2", "s_yv", "s_ys")]
    s_ph = sem("s_ph")

    def barrier(P):
        for _ in range(2):
            for eng in ENGS:
                s_ph.n += 1
                P.log[eng].append(("i", "s_ph", 1))
                P.q[eng].append(lambda e, h=s_ph.h: e.nop().then_inc(h, 1))
            tgt = s_ph.n
            for eng in ENGS:
                P.log[eng].append(("w", "s_ph", tgt))
                P.q[eng].append(lambda e, h=s_ph.h, v=tgt: e.wait_ge(h, v))

    def finish(sources):
        Pt = Plan()
        s_tp = sem("s_tp")
        for name, src in sources.items():
            if name in tap_outs:
                Pt.do("gpsimd", lambda e, d=tap_outs[name], s=src: e.dma_start(
                    out=d, in_=s), s_tp, 16)
        Pt.q["gpsimd"].append(lambda e, h=s_tp.h, v=s_tp.n: e.wait_ge(h, v) if v else None)
        Pt.emit(nc)
        return nc

    woT = OE(nc.sbuf_tensor("woT", [128, 8, 1024], BF16))
    sb_identb = OE(nc.sbuf_tensor("sb_identb", [128, 128], BF16))
    sb_identf = OE(nc.sbuf_tensor("sb_identf", [128, 128], F32))
    sb_onesf = OE(nc.sbuf_tensor("sb_onesf", [1, 128], F32))
    sb_onescf = OE(nc.sbuf_tensor("sb_onescf", [128, 1], F32))
    sb_ones64 = OE(nc.sbuf_tensor("sb_ones64", [128, 64], F32))
    sb_maskdd = OE(nc.sbuf_tensor("sb_maskdd", [128, 1024], BF16))
    sb_padbias = OE(nc.sbuf_tensor("sb_padbias", [128, 137], F32))
    sb_b1 = OE(nc.sbuf_tensor("sb_b1", [128, 32], F32))
    sb_b2 = OE(nc.sbuf_tensor("sb_b2", [1, 1024], F32))
    sb_eps = OE(nc.sbuf_tensor("sb_eps", [128, 1], F32))
    stats = OE(nc.sbuf_tensor("stats", [128, 64], F32))

    A4a = OE(nc.psum_tensor("A4a", [128, 1024], F32))
    A4b = OE(nc.psum_tensor("A4b", [128, 1024], F32))
    A4c = OE(nc.psum_tensor("A4c", [128, 1024], F32))
    B2a = OE(nc.psum_tensor("B2a", [128, 512], F32))
    B2b = OE(nc.psum_tensor("B2b", [128, 512], F32))

    wab = ExitStack()
    WE = wab.enter_context
    trash = WE(nc.sbuf_tensor("trash", [128, 2, 1024], F32))
    wqT = WE(nc.sbuf_tensor("wqT", [128, 8, 1024], BF16))
    wkT = WE(nc.sbuf_tensor("wkT", [128, 8, 1024], BF16))
    wvT = WE(nc.sbuf_tensor("wvT", [128, 8, 1024], BF16))

    wtmp = ExitStack()
    WT = wtmp.enter_context
    wnat = WT(nc.sbuf_tensor("wnat", [128, 8, 1024], BF16))
    wfb = WT(nc.sbuf_tensor("wfb", [128, 2, 4096], F32))
    absa = WT(nc.sbuf_tensor("absa", [128, 40], F32))
    sc1 = WT(nc.sbuf_tensor("sc1", [128, 8], F32))
    sinv = WT(nc.sbuf_tensor("sinv", [128, 2], F32))
    ternb = WT(nc.sbuf_tensor("ternb", [128, 4096], F32))
    tble = WT(nc.sbuf_tensor("tble", [128, 4096], F32))
    ternt = WT(nc.sbuf_tensor("ternt", [128, 2, 4096], BF16))
    wstg = WT(nc.sbuf_tensor("wstg", [128, 4096], BF16))
    b1n = WT(nc.sbuf_tensor("b1n", [32, 128], F32))

    P = Plan()
    for dst, src in ((sb_identb, identb), (sb_identf, identf), (sb_onesf, onesf),
                     (sb_onescf, onescf), (sb_maskdd, maskdd),
                     (sb_padbias, padbias), (sb_b2, b2[None, :])):
        P.do("sync", lambda e, d=dst, s=src: e.dma_start(out=d[:], in_=s), s_cl, 16)
    P.do("sync", lambda e: e.dma_start(
        out=b1n[:], in_=b1.rearrange("(c p) -> c p", p=128)), s_cl, 16)
    P.do("vector", lambda e: e.memset(sb_eps[:], 1e-5), s_ln, 1)
    P.do("vector", lambda e: e.memset(sb_ones64[:], 1.0), s_ln, 1)

    for wi, (wsrc, wdst) in enumerate(((wq, wqT), (wk, wkT), (wv, wvT), (wo, woT))):
        if wi > 0:
            P.wait("gpsimd", s_we)
        P.do("gpsimd", lambda e, s=wsrc: e.dma_start(
            out=wnat[:], in_=s.rearrange("(r p) d -> p r d", p=128)), s_wl, 16)
        P.wait("tensor", s_wl)
        for r in range(8):
            ps = (A4a, A4b)[r % 2]
            if r >= 2:
                P.wait("tensor", s_we)
            for dc in range(8):
                P.do("tensor", lambda e, r=r, dc=dc, ps=ps: e.matmul(
                    ps[:, dc * 128:(dc + 1) * 128],
                    wnat[:, r, dc * 128:(dc + 1) * 128], sb_identb[:],
                    start=True, stop=True), s_wt if dc == 7 else None)
            P.wait("vector", s_wt)
            P.do("vector", lambda e, r=r, ps=ps, wd=wdst: e.tensor_copy(
                wd[:, :, r * 128:(r + 1) * 128],
                ps[:].rearrange("p (c f) -> p c f", f=128)), s_we, 1)

    P.wait("tensor", s_cl)
    P.do("tensor", lambda e: e.matmul(A4c[:, 0:32], b1n[:], sb_identf[0:32, 0:32],
                                      start=True, stop=True), s_wt, 1)
    P.wait("vector", s_wt)
    P.do("vector", lambda e: e.tensor_copy(sb_b1[:], A4c[:, 0:32]), s_we, 1)

    def w_abs_pass(wsrc, ntile, tw, acol0, fidx, nrows=1):
        for t in range(ntile):
            par = t % 2
            q = ("sync", "scalar")[t % 2]
            qs = (s_xl, s_x2)[t % 2]
            P.wait(q, s_ln, wfb_free[par])
            P.do(q, lambda e, t=t, par=par, s=wsrc, tw=tw, nr=nrows: e.dma_start(
                out=wfb[:, par, 0:tw].rearrange("p (r d) -> p r d", r=nr),
                in_=s.rearrange("(r p) d -> p r d", p=128)[:, t * nr:(t + 1) * nr]),
                qs, 16)
            P.wait("vector", qs)
            P.do("vector", lambda e, t=t, par=par, tw=tw, c=acol0: e.reduce_sum(
                absa[:, c + t:c + t + 1], wfb[:, par, 0:tw], axis=AX,
                apply_absolute_value=True), s_ln, 1)
            wfb_free[par] = s_ln.n
        P.wait("vector", s_ln)
        P.do("vector", lambda e, c=acol0, n=ntile, f=fidx: e.reduce_sum(
            sc1[:, f:f + 1], absa[:, c:c + n], axis=AX), s_ln, 1)
        P.wait("tensor", s_ln)
        P.do("tensor", lambda e, f=fidx: e.matmul(
            B2a[0:1, f:f + 1], sc1[:, f:f + 1], sb_onescf[:],
            start=True, stop=True), s_wt, 1)
        P.wait("vector", s_wt)
        P.do("vector", lambda e, f=fidx: e.tensor_scalar(
            sc1[0:1, 4 + f:5 + f], B2a[0:1, f:f + 1], 1.0 / (FF * D), 1e-8,
            ALU.mult, ALU.max), s_ln, 1)
        P.wait("vector", s_ln)
        P.do("vector", lambda e, f=fidx: e.reciprocal(
            sc1[0:1, 6 + f:7 + f], sc1[0:1, 4 + f:5 + f]), s_ln, 1)
        P.wait("tensor", s_ln)
        P.do("tensor", lambda e, f=fidx: e.matmul(
            B2b[:, f:f + 1], sb_onesf[:], sc1[0:1, 6 + f:7 + f],
            start=True, stop=True), s_wt, 1)
        P.wait("vector", s_wt)
        P.do("vector", lambda e, f=fidx: e.tensor_copy(
            sinv[:, f:f + 1], B2b[:, f:f + 1]), s_ln, 1)

    wfb_free = [0, 0]
    w_abs_pass(w1, 8, 4096, 0, 0, nrows=4)
    w_abs_pass(w2, 8, 4096, 32, 1)

    def w_tern_pass(wsrc, ntile, tw, fidx, wtd, nrows=1):
        nchunk = tw // 128
        for t in range(ntile):
            par = t % 2
            q = ("sync", "scalar")[t % 2]
            qs = (s_xl, s_x2)[t % 2]
            P.wait(q, s_ln, wfb_free[par])
            P.do(q, lambda e, t=t, par=par, s=wsrc, tw=tw, nr=nrows: e.dma_start(
                out=wfb[:, par, 0:tw].rearrange("p (r d) -> p r d", r=nr),
                in_=s.rearrange("(r p) d -> p r d", p=128)[:, t * nr:(t + 1) * nr]),
                qs, 16)
            P.wait("vector", qs)
            P.wait("vector", s_wt, tern_tr[par])
            P.do("vector", lambda e, par=par, tw=tw, f=fidx: e.tensor_scalar_mul(
                ternb[:, 0:tw], wfb[:, par, 0:tw], sinv[:, f:f + 1]), s_ln, 1)
            wfb_free[par] = s_ln.n
            P.wait("vector", s_ln)
            P.do("vector", lambda e, tw=tw: e.tensor_single_scalar(
                tble[:, 0:tw], ternb[:, 0:tw], -0.5, ALU.is_le), s_ln, 1)
            P.wait("vector", s_ln)
            P.do("vector", lambda e, par=par, tw=tw: e.scalar_tensor_tensor(
                ternt[:, par, 0:tw], ternb[:, 0:tw], 0.5, tble[:, 0:tw],
                ALU.is_ge, ALU.subtract), s_ap, 1)
            P.wait("tensor", s_ap)
            P.wait("tensor", s_ws, stg_done[0])
            for ch in range(nchunk):
                bank = (t * ((nchunk + 7) // 8) + ch // 8) % 2
                p8 = (A4a, A4b)[bank]
                if ch % 8 == 0:
                    P.wait("tensor", s_we, trn_bank_free[bank])
                P.do("tensor", lambda e, ch=ch, p8=p8, par=par: e.matmul(
                    p8[:, (ch % 8) * 128:(ch % 8 + 1) * 128],
                    ternt[:, par, ch * 128:(ch + 1) * 128], sb_identb[:],
                    start=True, stop=True), s_wt if ch % 8 == 7 or ch == nchunk - 1 else None)
                if ch % 8 == 7 or ch == nchunk - 1:
                    n8 = ch % 8 + 1
                    P.wait("vector", s_wt)
                    P.do("vector", lambda e, ch=ch, n8=n8, p8=p8: e.tensor_copy(
                        wstg[:, (ch - n8 + 1) * 128:(ch + 1) * 128],
                        p8[:, 0:n8 * 128]), s_we, 1)
                    trn_bank_free[bank] = s_we.n
            tern_tr[par] = s_wt.n
            P.wait("gpsimd", s_we)
            if nrows == 1:
                P.do("gpsimd", lambda e, t=t, w=wtd, nch=nchunk: e.dma_start(
                    out=w.rearrange("c p f -> p c f")[:, :, t * 128:(t + 1) * 128],
                    in_=wstg[:, 0:nch * 128].rearrange("p (c f) -> p c f", f=128)),
                    s_ws, 16)
            else:
                for r in range(nrows):
                    P.do("gpsimd", lambda e, t=t, r=r, w=wtd, nr=nrows: e.dma_start(
                        out=w.rearrange("c p f -> p c f")[
                            :, :, (t * nr + r) * 128:(t * nr + r + 1) * 128],
                        in_=wstg[:, r * 1024:(r + 1) * 1024].rearrange(
                            "p (c k) -> p c k", k=128)), s_ws, 16)
            stg_done[0] = s_ws.n

    tern_tr, stg_done = [0, 0], [0]
    trn_bank_free = {0: 0, 1: 0}
    w_tern_pass(w1, 8, 4096, 0, w1t_d, nrows=4)
    w_tern_pass(w2, 8, 4096, 1, w2t_d)

    P.wait("gpsimd", s_ws)
    barrier(P)
    P.emit(nc)
    if upto == "w":
        return finish(dict(wqT=wqT[:], wvT=wvT[:], sinv=sinv[:], b1s=sb_b1[:],
                           w1t=w1t_d, w2t=w2t_d))
    wtmp.close()

    ab = ExitStack()
    AE = ab.enter_context
    xbf = AE(nc.sbuf_tensor("xbf", [128, 2, 4, 1024], F32))
    xnb = AE(nc.sbuf_tensor("xnb", [128, 2, 4, 1024], BF16))
    xnT = AE(nc.sbuf_tensor("xnT", [128, 2, 8, 512], BF16))
    kst = AE(nc.sbuf_tensor("kst", [128, 2, 4, 1024], BF16))
    v1s = AE(nc.sbuf_tensor("v1s", [128, 2, 4, 1040], BF16))
    qst = AE(nc.sbuf_tensor("qst", [128, 2, 2048], BF16))

    P = Plan()

    def ln_tiles(par, nrow, ap_src):
        sb = par * 32
        P.wait("scalar", s_xl)
        for r in range(nrow):
            P.do("scalar", lambda e, r=r: e.activation(
                trash[:, r % 2, :], ap_src(r), AF.Square,
                accum_out=stats[:, sb + r * 8 + 1:sb + r * 8 + 2]))
        for r in range(nrow):
            P.do("scalar", lambda e, r=r: e.activation(
                trash[:, r % 2, :], ap_src(r), AF.Copy,
                accum_out=stats[:, sb + r * 8:sb + r * 8 + 1]),
                s_rs if r == nrow - 1 else None)
        P.wait("vector", s_rs)
        for r in range(nrow):
            P.do("vector", lambda e, r=r: e.tensor_scalar_mul(
                stats[:, sb + r * 8 + 3:sb + r * 8 + 4],
                stats[:, sb + r * 8:sb + r * 8 + 1], -1.0 / D))
            P.do("vector", lambda e, r=r: e.tensor_scalar(
                stats[:, sb + r * 8 + 4:sb + r * 8 + 5],
                stats[:, sb + r * 8:sb + r * 8 + 1],
                stats[:, sb + r * 8:sb + r * 8 + 1], 1.0 / (D * D),
                ALU.mult, ALU.mult), s_ln if r == nrow - 1 else None)
        P.wait("vector", s_ln)
        for r in range(nrow):
            P.do("vector", lambda e, r=r: e.scalar_tensor_tensor(
                stats[:, sb + r * 8 + 5:sb + r * 8 + 6],
                stats[:, sb + r * 8 + 1:sb + r * 8 + 2], 1.0 / D,
                stats[:, sb + r * 8 + 4:sb + r * 8 + 5], ALU.mult, ALU.subtract),
                s_ln if r == nrow - 1 else None)
        P.wait("scalar", s_ln)
        for r in range(nrow):
            P.do("scalar", lambda e, r=r: e.activation(
                stats[:, sb + r * 8 + 6:sb + r * 8 + 7],
                stats[:, sb + r * 8 + 5:sb + r * 8 + 6],
                AF.Ln, bias=sb_eps[:]), s_rs if r == nrow - 1 else None)
        P.wait("scalar", s_rs)
        for r in range(nrow):
            P.do("scalar", lambda e, r=r: e.activation(
                stats[:, sb + r * 8 + 7:sb + r * 8 + 8],
                stats[:, sb + r * 8 + 6:sb + r * 8 + 7],
                AF.Exp, scale=-0.5), s_rs if r == nrow - 1 else None)
        P.wait("vector", s_rs)
        for r in range(nrow):
            P.do("vector", lambda e, r=r, par=par: e.tensor_scalar(
                xnb[:, par, r, :], ap_src(r), stats[:, sb + r * 8 + 3:sb + r * 8 + 4],
                stats[:, sb + r * 8 + 7:sb + r * 8 + 8], ALU.add, ALU.mult),
                s_ap if r == nrow - 1 else None)

    def transpose_tiles(par, nrow):
        P.wait("tensor", s_pe, vpe_free[0])
        P.wait("tensor", s_pe, vpe_free[1])
        for r in range(nrow):
            ps = (A4a, A4b)[r % 2]
            if r >= 2:
                P.wait("tensor", s_te, te_free[r % 2])
            for dc in range(8):
                P.do("tensor", lambda e, r=r, dc=dc, ps=ps, par=par: e.matmul(
                    ps[:, dc * 128:(dc + 1) * 128],
                    xnb[:, par, r, dc * 128:(dc + 1) * 128], sb_identb[:],
                    start=True, stop=True), s_tr if dc == 7 else None)
            P.wait("vector", s_tr)
            P.do("vector", lambda e, r=r, par=par, ps=ps: e.tensor_copy(
                xnT[:, par, :, r * 128:(r + 1) * 128],
                ps[:].rearrange("p (c f) -> p c f", f=128)), s_te, 1)
            te_free[r % 2] = s_te.n

    def kq_proj(par, ntok, wT, dst_cb):
        for pr in range(NPAIR):
            ps = (B2a, B2b)[pr % 2]
            P.wait("tensor", s_pe, pe_free[pr % 2])
            for dc in range(8):
                P.do("tensor", lambda e, pr=pr, dc=dc, ps=ps, par=par, wT=wT, n=ntok:
                     e.matmul(ps[:, 0:n], wT[:, dc, pr * 128:(pr + 1) * 128],
                              xnT[:, par, dc, 0:n], start=(dc == 0), stop=(dc == 7)),
                     s_pj if dc == 7 else None)
            P.wait("vector", s_pj)
            dst_cb(pr, ps)
            pe_free[pr % 2] = s_pe.n

    def v_proj(par, ntsub):
        for tsub in range(ntsub):
            ps = (A4a, A4b)[tsub % 2]
            P.wait("tensor", s_pe, vpe_free[tsub % 2])
            P.wait("tensor", s_te, te_free[tsub % 2])
            for dc in range(8):
                for nh in range(2):
                    P.do("tensor", lambda e, dc=dc, nh=nh, tsub=tsub, ps=ps, par=par:
                         e.matmul(ps[:, nh * 512:(nh + 1) * 512],
                                  xnT[:, par, dc, tsub * 128:(tsub + 1) * 128],
                                  wvT[:, dc, nh * 512:(nh + 1) * 512],
                                  start=(dc == 0), stop=(dc == 7)),
                         s_pj if (dc == 7 and nh == 1) else None)
            P.wait("vector", s_pj)
            P.do("vector", lambda e, tsub=tsub, par=par, ps=ps: e.tensor_copy(
                v1s[:, par, tsub, :].rearrange("p (h c) -> p h c", c=65)[:, :, 0:64],
                ps[:].rearrange("p (h c) -> p h c", c=64)), s_pe, 1)
            P.do("vector", lambda e, tsub=tsub, par=par: e.memset(
                v1s[:, par, tsub, :].rearrange("p (h c) -> p h c", c=65)[:, :, 64:65],
                1.0), s_pe, 1)
            vpe_free[tsub % 2] = s_pe.n

    te_free, pe_free, vpe_free = [0, 0], [0, 0], [0, 0]
    xbf_free = [0, 0]
    xnb_free = [0, 0]
    ss_done = [0, 0]
    ap_done = {}

    chunks = ([("a", j) for j in range(NSLOT)] + [("b", t) for t in range(16)])

    def load_chunk(i):
        kind, idx = chunks[i]
        par = i % 2
        P.wait("sync", s_ap, xbf_free[par])
        if kind == "a":
            P.do("sync", lambda e, idx=idx, par=par: e.dma_start(
                out=xbf[:, par, 0:2, :],
                in_=xq[idx * 256:(idx + 1) * 256, :].rearrange(
                    "(r p) d -> p r d", p=128)), s_xl, 16)
        else:
            P.do("sync", lambda e, idx=idx, par=par: e.dma_start(
                out=xbf[:, par, :, :],
                in_=xb[idx * 512:(idx + 1) * 512, :].rearrange(
                    "(r p) d -> p r d", p=128)), s_xl, 16)

    def ln_chunk(i):
        kind, idx = chunks[i]
        par = i % 2
        nrow = 2 if kind == "a" else 4
        P.wait("vector", s_xl)
        P.wait("vector", s_tr, xnb_free[par])
        ln_tiles(par, nrow, lambda r, par=par: xbf[:, par, r, :])
        xbf_free[par] = s_ap.n
        ap_done[i] = s_ap.n

    def work_chunk(i):
        kind, idx = chunks[i]
        par = i % 2
        nrow = 2 if kind == "a" else 4
        P.wait("tensor", s_ap, ap_done[i])
        transpose_tiles(par, nrow)
        xnb_free[par] = s_tr.n
        P.wait("tensor", s_te)
        P.wait("vector", s_ss, ss_done[par])

        if kind == "a":
            def q_evac(pr, ps, par=par):
                P.do("vector", lambda e, pr=pr, ps=ps: e.tensor_copy(
                    qst[:, par, pr * 256:(pr + 1) * 256], ps[:, 0:256]), s_pe, 1)

            def k_evac(pr, ps, par=par):
                P.do("vector", lambda e, pr=pr, ps=ps: e.tensor_copy(
                    kst[:, par, 0:2, pr * 128:(pr + 1) * 128],
                    ps[:, 0:256].rearrange("p (t k) -> p t k", k=128)), s_pe, 1)

            kq_proj(par, 256, wqT, q_evac)
            kq_proj(par, 256, wkT, k_evac)
            v_proj(par, 2)
            P.wait("gpsimd", s_pe)
            P.do("gpsimd", lambda e, idx=idx, par=par: e.dma_start(
                out=qt_d[idx], in_=qst[:, par, :]), s_ss, 16)
            P.wait("gpsimd", s_te)
            P.do("gpsimd", lambda e, idx=idx, par=par: e.dma_start(
                out=xnt_d[idx].rearrange("p (c f) -> p c f", f=256),
                in_=xnT[:, par, :, 0:256]), s_ss, 16)
            P.do("gpsimd", lambda e, idx=idx, par=par: e.dma_start(
                out=kto_d[2 * idx:2 * idx + 2].rearrange("t p c -> p t c"),
                in_=kst[:, par, 0:2, :]), s_ss, 16)
            P.do("gpsimd", lambda e, idx=idx, par=par: e.dma_start(
                out=v1o_d[2 * idx:2 * idx + 2].rearrange("t p c -> p t c"),
                in_=v1s[:, par, 0:2, :]), s_ss, 16)
        else:
            def k_evac_b(pr, ps, par=par):
                P.do("vector", lambda e, pr=pr, ps=ps: e.tensor_copy(
                    kst[:, par, :, pr * 128:(pr + 1) * 128],
                    ps[:, 0:512].rearrange("p (t k) -> p t k", k=128)), s_pe, 1)

            kq_proj(par, 512, wkT, k_evac_b)
            v_proj(par, 4)
            P.wait("gpsimd", s_pe)
            P.do("gpsimd", lambda e, idx=idx, par=par: e.dma_start(
                out=kt_d[4 * idx:4 * idx + 4].rearrange("t p c -> p t c"),
                in_=kst[:, par, :, :]), s_ss, 16)
            P.do("gpsimd", lambda e, idx=idx, par=par: e.dma_start(
                out=v1_d[4 * idx:4 * idx + 4].rearrange("t p c -> p t c"),
                in_=v1s[:, par, :, :]), s_ss, 16)
        ss_done[par] = s_ss.n

    for i in range(len(chunks) + 1):
        if i < len(chunks):
            load_chunk(i)
            ln_chunk(i)
        if i >= 1:
            work_chunk(i - 1)

    P.wait("gpsimd", s_ss)
    barrier(P)
    P.emit(nc)
    if upto == "ab":
        return finish(dict(kt=kt_d, v1=v1_d, kto=kto_d, v1o=v1o_d, qt=qt_d,
                           xnt=xnt_d))
    ab.close()
    wab.close()

    dph = ExitStack()
    DP = dph.enter_context
    amidT = DP(nc.sbuf_tensor("amidT", [128, 2, 8, 256], BF16))
    ktm = DP(nc.sbuf_tensor("ktm", [128, 2, 8, 1024], BF16))
    v1m = DP(nc.sbuf_tensor("v1m", [128, 2, 8, 1040], BF16))
    qts = DP(nc.sbuf_tensor("qts", [128, 2, 2048], BF16))
    ptb = DP(nc.sbuf_tensor("ptb", [128, 3, 1024], BF16))
    acc = DP(nc.sbuf_tensor("acc", [65, 4096], F32))
    w1s = DP(nc.sbuf_tensor("w1s", [128, 8, 512], BF16))
    w2s = DP(nc.sbuf_tensor("w2s", [128, 32, 512], BF16))
    xnte = DP(nc.sbuf_tensor("xnte", [128, 8, 256], BF16))
    hts = DP(nc.sbuf_tensor("hts", [128, 32, 256], BF16))
    xqe = DP(nc.sbuf_tensor("xqe", [128, 2, 512], F32))
    yb = DP(nc.sbuf_tensor("yb", [128, 2, 512], F32))

    P = Plan()
    PBCOL = [sum(GJ[:k]) for k in range(NSLOT)]
    LOOK = 2
    sg_free = [0, 0]
    pt_free = [0, 0, 0]
    pv_free = [0, 0]
    km_free = [[0, 0], [0, 0]]
    qt_free = [0, 0]
    am_free = [0, 0]
    am_rd = [0, 0]
    acc_free = [0]
    pending_norm = [None]
    pending_e = [None]
    uidx = 0
    xnte_free = [0, 0]
    xq_free = [0]
    yb_free = [0]
    ht_free = {}
    w1_free = [0, 0]
    w2_free = [0, 0]
    a4c_norm_done = [0]

    def make_epieces(jj):
        epar = jj % 2
        pieces = []
        st = {}

        def p_loads():
            P.wait("sync", s_f1, xnte_free[0])
            P.do("sync", lambda e, jj=jj: e.dma_start(
                out=xnte[:],
                in_=xnt_d[jj].rearrange("p (c f) -> p c f", f=256)), s_el, 16)
            st["xnte"] = s_el.n
            P.wait("sync", s_yv, xq_free[0])
            P.do("sync", lambda e, jj=jj: e.dma_start(
                out=xqe[:],
                in_=xq[jj * 256:(jj + 1) * 256, 0:512].rearrange(
                    "(r p) d -> p r d", p=128)), s_el, 16)
            st["xqe"] = s_el.n
        pieces.append(p_loads)

        gl_tgt = {}
        for c in range(8):
            def p_ffn1(c=c):
                fc0 = c * 4
                P.wait("sync", s_f1, w1_free[0])
                P.do("sync", lambda e, fc0=fc0: e.dma_start(
                    out=w1s[:],
                    in_=w1t_d.rearrange("c p f -> p c f")[:, :, fc0 * 128:(fc0 + 4) * 128]),
                    s_el, 16)
                ld = s_el.n
                P.wait("tensor", s_el, ld)
                P.wait("tensor", s_el, st["xnte"])
                if c == 0:
                    P.wait("tensor", s_am, a4c_norm_done[0])
                for q in range(4):
                    fc = fc0 + q
                    hb = (fc % 2) * 512
                    P.wait("tensor", s_gl, gl_tgt.get(fc - 2, 0))
                    for dc in range(8):
                        P.do("tensor", lambda e, fc=fc, hb=hb, dc=dc,
                             epar=epar, q=q: e.matmul(
                                 A4c[:, hb:hb + 256],
                                 w1s[:, dc, q * 128:(q + 1) * 128],
                                 xnte[:, dc, :],
                                 start=(dc == 0), stop=(dc == 7)),
                             s_f1 if dc == 7 else None)
                    P.wait("scalar", s_f1)
                    P.wait("scalar", s_f2, ht_free.get(fc, 0))
                    P.do("scalar", lambda e, fc=fc, hb=hb: e.activation(
                        hts[:, fc, :], A4c[:, hb:hb + 256], AF.Gelu,
                        bias=sb_b1[:, fc:fc + 1]), s_gl, 1)
                    gl_tgt[fc] = s_gl.n
                w1_free[0] = s_f1.n
            pieces.append(p_ffn1)

        def p_gl_all():
            P.do("scalar", lambda e: e.copy(stats[0:1, 28:29], stats[0:1, 28:29]),
                 s_gl, 1)
            st["gl_all"] = s_gl.n
        pieces.append(p_gl_all)

        def p_w2load(h):
            def fn():
                P.wait("sync", s_f2, w2_free[0])
                P.do("sync", lambda e, h=h: e.dma_start(
                    out=w2s[:],
                    in_=w2t_d[:, :, h * 512:(h + 1) * 512].rearrange(
                        "c p f -> p c f")), s_el, 16)
                st["w2"] = s_el.n
            return fn
        pieces.append(p_w2load(0))

        for h in range(2):
            def p_pass(h=h):
                P.wait("tensor", s_el, st["w2"])
                P.wait("tensor", s_gl, st["gl_all"])
                P.wait("tensor", s_am, a4c_norm_done[0])
                if h == 1:
                    P.wait("tensor", s_yv, st["yv0"])
                for t in range(2):
                    P.do("tensor", lambda e, t=t, h=h: e.matmul(
                        A4c[:, t * 512:(t + 1) * 512], sb_onesf[0:1, :],
                        sb_b2[0:1, h * 512:(h + 1) * 512], start=True, stop=False))
                for fc in range(32):
                    for t in range(2):
                        P.do("tensor", lambda e, fc=fc, t=t: e.matmul(
                            A4c[:, t * 512:(t + 1) * 512],
                            hts[:, fc, t * 128:(t + 1) * 128],
                            w2s[:, fc, :], start=False, stop=False))
                for pr in range(NPAIR):
                    for t in range(2):
                        P.do("tensor", lambda e, pr=pr, t=t, epar=epar:
                             e.matmul(A4c[:, t * 512:(t + 1) * 512],
                                      amidT[:, epar, pr, t * 128:(t + 1) * 128],
                                      woT[:, pr, h * 512:(h + 1) * 512],
                                      start=False, stop=(pr == NPAIR - 1)),
                             s_f2 if (pr == NPAIR - 1 and t == 1) else None)
                w2_free[0] = s_f2.n
                if h == 1:
                    am_rd[epar] = s_f2.n
                    for fc in range(32):
                        ht_free[fc] = s_f2.n
            pieces.append(p_pass)

            def p_resid(h=h, jj=jj):
                P.wait("vector", s_f2)
                P.wait("vector", s_el, st["xqe"])
                P.wait("vector", s_ys, yb_free[0])
                for t in range(2):
                    P.do("vector", lambda e, t=t: e.tensor_add(
                        yb[:, t, :], A4c[:, t * 512:(t + 1) * 512],
                        xqe[:, t, :]), s_yv, 1)
                if h == 0:
                    st["yv0"] = s_yv.n
                else:
                    xq_free[0] = s_yv.n
                P.wait("gpsimd", s_yv)
                P.do("gpsimd", lambda e, jj=jj, h=h: e.dma_start(
                    out=y[jj * 256:(jj + 1) * 256,
                          h * 512:(h + 1) * 512].rearrange(
                        "(r p) d -> p r d", p=128),
                    in_=yb[:]), s_ys, 16)
                yb_free[0] = s_ys.n
            pieces.append(p_resid)
            if h == 0:
                def p_xq1(jj=jj):
                    P.wait("sync", s_yv, st["yv0"])
                    P.do("sync", lambda e, jj=jj: e.dma_start(
                        out=xqe[:],
                        in_=xq[jj * 256:(jj + 1) * 256, 512:1024].rearrange(
                            "(r p) d -> p r d", p=128)), s_el, 16)
                    st["xqe"] = s_el.n
                pieces.append(p_xq1)
                pieces.append(p_w2load(1))

        def p_close():
            xnte_free[0] = s_f1.n
        pieces.append(p_close)
        return pieces

    for j in reversed(range(NSLOT)):
        spar = j % 2
        P.wait("sync", s_qk, qt_free[spar])
        P.do("sync", lambda e, j=j, spar=spar: e.dma_start(
            out=qts[:, spar, :], in_=qt_d[j]), s_kl, 16)
        groups = GJ[j]
        macros = [list(range(m, min(m + MACRO, groups)))
                  for m in range(0, groups, MACRO)]
        macros.append(None)

        units = []
        for mi, mg in enumerate(macros):
            diag = mg is None
            ng = 1 if diag else len(mg)
            for pr in range(NPAIR):
                for gi in range(ng):
                    units.append(dict(mi=mi, mg=mg, diag=diag, ng=ng, pr=pr,
                                      gi=gi, mpar=mi % 2))
        nun = len(units)
        kl_tgt = {}

        def emit_macro_load(mi, j=j, macros=macros):
            mg = macros[mi]
            mpar = mi % 2
            P.wait("sync", s_qk, km_free[mpar][0])
            P.wait("sync", s_pv, km_free[mpar][1])
            if mg is None:
                P.do("sync", lambda e, j=j, mpar=mpar: e.dma_start(
                    out=ktm[:, mpar, 0:2, :],
                    in_=kto_d[2 * j:2 * j + 2].rearrange("t p c -> p t c")), s_kl, 16)
                P.do("sync", lambda e, j=j, mpar=mpar: e.dma_start(
                    out=v1m[:, mpar, 0:2, :],
                    in_=v1o_d[2 * j:2 * j + 2].rearrange("t p c -> p t c")), s_kl, 16)
            else:
                k0, nb = 2 * mg[0], 2 * len(mg)
                P.do("sync", lambda e, k0=k0, nb=nb, mpar=mpar: e.dma_start(
                    out=ktm[:, mpar, 0:nb, :],
                    in_=kt_d[k0:k0 + nb].rearrange("t p c -> p t c")), s_kl, 16)
                P.do("sync", lambda e, k0=k0, nb=nb, mpar=mpar: e.dma_start(
                    out=v1m[:, mpar, 0:nb, :],
                    in_=v1_d[k0:k0 + nb].rearrange("t p c -> p t c")), s_kl, 16)
            kl_tgt[mi] = s_kl.n

        qk_done = {}
        ex_done = {}
        ex_half = {}
        mk_done = {}
        last_qk_of_macro = {}
        epieces = pending_e[0] or []
        pending_e[0] = None
        cadence = max(1, (nun - 4) // max(1, len(epieces))) if epieces else 0
        ep_i = 0

        for step in range(nun + LOOK):
            if step == 2 and pending_norm[0] is not None:
                pending_norm[0]()
                pending_norm[0] = None
            if (epieces and step >= 3 and ep_i < len(epieces)
                    and (step - 3) % cadence == 0):
                epieces[ep_i]()
                ep_i += 1
            if step == 0:
                emit_macro_load(0)
            if step < nun:
                u = units[step]
                mi, pr, gi, mpar, diag = u["mi"], u["pr"], u["gi"], u["mpar"], u["diag"]
                if pr == 0 and gi == 0 and mi + 1 < len(macros):
                    emit_macro_load(mi + 1)
                sg_i = uidx % 2
                ppar = uidx % 3
                u["sg_i"], u["ppar"] = sg_i, ppar
                uidx += 1
                sgp = (A4a, A4b)[sg_i]
                u["sgp"] = sgp
                if pr == 0 and gi == 0:
                    P.wait("tensor", s_kl, kl_tgt[mi])
                P.wait("tensor", s_ex, sg_free[sg_i])
                for (kb, hp, off) in ((0, 0, 0), (1, 0, 256), (0, 1, 512), (1, 1, 768)):
                    first = off % 512 == 0
                    tp = (hp * 64, 0)
                    P.do("tensor", lambda e, kb=kb, hp=hp, off=off, tp=tp, gi=gi,
                         pr=pr, sgp=sgp, mpar=mpar, spar=spar, first=first:
                         e.matmul(sgp[:, off:off + 256],
                                  ktm[hp * 64:(hp + 1) * 64, mpar, 2 * gi + kb,
                                      pr * 128:(pr + 1) * 128],
                                  qts[hp * 64:(hp + 1) * 64, spar,
                                      pr * 256:(pr + 1) * 256],
                                  start=first, stop=not first, tile_position=tp),
                         s_qk if off in (256, 768) else None)
                qk_done[step] = s_qk.n
                last_qk_of_macro[mi] = s_qk.n
                col = 136 if diag else PBCOL[j] + u["mg"][gi]
                P.wait("scalar", s_qk, qk_done[step])
                P.wait("scalar", s_pv, pt_free[ppar])
                P.do("scalar", lambda e, ppar=ppar, sgp=sgp, col=col: e.activation(
                    ptb[:, ppar, 0:1024], sgp[:, 0:1024], AF.Exp,
                    bias=sb_padbias[:, col:col + 1], scale=0.125), s_ex, 1)
                ex_done[step] = s_ex.n
                sg_free[sg_i] = s_ex.n
                if diag:
                    P.wait("vector", s_ex, ex_done[step])
                    P.do("vector", lambda e, ppar=ppar: e.tensor_mul(
                        ptb[:, ppar, :], ptb[:, ppar, :], sb_maskdd[:]), s_mk, 1)
                    P.do("vector", lambda e: e.tensor_copy(
                        stats[0:1, 29:30], stats[0:1, 29:30]), s_mk, 1)
                    mk_done[step] = s_mk.n
            if step >= LOOK:
                vstep = step - LOOK
                v = units[vstep]
                mi, pr, gi, mpar, diag = v["mi"], v["pr"], v["gi"], v["mpar"], v["diag"]
                pvp = (B2a, B2b)[pr % 2]
                ppar = v["ppar"]
                if gi == 0:
                    P.wait("tensor", s_ac, pv_free[pr % 2])
                if diag:
                    P.wait("tensor", s_mk, mk_done[vstep])
                else:
                    P.wait("tensor", s_ex, ex_done[vstep])
                for (kb, hp, poff) in ((0, 0, 0), (1, 0, 0), (0, 1, 256), (1, 1, 256)):
                    first = gi == 0 and kb == 0 and hp == 0
                    P.do("tensor", lambda e, kb=kb, hp=hp, poff=poff, gi=gi, pr=pr,
                         pvp=pvp, mpar=mpar, ppar=ppar, first=first:
                         e.matmul(pvp[0:65, poff:poff + 256],
                                  v1m[:, mpar, 2 * gi + kb,
                                      (2 * pr + hp) * 65:(2 * pr + hp + 1) * 65],
                                  ptb[:, ppar, hp * 512 + kb * 256:
                                      hp * 512 + kb * 256 + 256],
                                  start=first, stop=False),
                         s_pv if (kb == 1 and hp == 1) else None)
                pt_free[ppar] = s_pv.n
                if gi == v["ng"] - 1:
                    P.wait("vector", s_pv)
                    if mi == 0:
                        P.wait("vector", s_am, acc_free[0])
                        P.do("vector", lambda e, pr=pr, pvp=pvp: e.tensor_copy(
                            acc[:, pr * 512:(pr + 1) * 512], pvp[0:65, :]), s_ac, 1)
                    else:
                        P.do("vector", lambda e, pr=pr, pvp=pvp: e.tensor_add(
                            acc[:, pr * 512:(pr + 1) * 512],
                            acc[:, pr * 512:(pr + 1) * 512], pvp[0:65, :]), s_ac, 1)
                    pv_free[pr % 2] = s_ac.n
                    if pr == NPAIR - 1:
                        km_free[mpar] = [last_qk_of_macro[mi], s_pv.n]
        while epieces and ep_i < len(epieces):
            epieces[ep_i]()
            ep_i += 1
        P.do("scalar", lambda e: e.copy(stats[0:1, 28:29], stats[0:1, 28:29]),
             s_ex, 1)
        qt_free[spar] = s_qk.n
        P.do("vector", lambda e: e.tensor_copy(
            stats[0:1, 29:30], stats[0:1, 29:30]), s_ac, 1)
        ac_tgt = s_ac.n
        P.wait("scalar", s_ac, ac_tgt)
        for pr in range(NPAIR):
            P.do("scalar", lambda e, pr=pr: e.activation(
                acc[64:65, pr * 512:(pr + 1) * 512],
                acc[64:65, pr * 512:(pr + 1) * 512], AF.Ln),
                s_nm if pr == NPAIR - 1 else None)
        P.wait("scalar", s_nm)
        for pr in range(NPAIR):
            P.do("scalar", lambda e, pr=pr: e.activation(
                acc[64:65, pr * 512:(pr + 1) * 512],
                acc[64:65, pr * 512:(pr + 1) * 512], AF.Exp, scale=-1.0),
                s_nm if pr == NPAIR - 1 else None)
        P.do("scalar", lambda e: e.copy(stats[0:1, 28:29], stats[0:1, 28:29]), s_nm, 1)
        nm_tgt = s_nm.n

        def make_norm(j=j, spar=spar, nm_tgt=nm_tgt):
            def emit_norm():
                P.wait("tensor", s_nm, nm_tgt)
                P.wait("tensor", s_gl)
                P.wait("tensor", s_yv)
                P.wait("vector", s_f2, am_rd[spar])
                rp_tgt = {}
                for hp2 in range(H // 2 + 1):
                    if hp2 < H // 2:
                        P.wait("tensor", s_am, am_free[0])
                        P.do("tensor", lambda e, hp2=hp2: e.matmul(
                            A4c[0:64, 0:512] if hp2 % 2 == 0 else A4c[0:64, 512:1024],
                            sb_ones64[64:65, :],
                            acc[64:65, hp2 * 512:(hp2 + 1) * 512],
                            start=True, stop=True), s_rp, 1)
                        rp_tgt[hp2] = s_rp.n
                    if hp2 >= 1:
                        k = hp2 - 1
                        P.wait("vector", s_rp, rp_tgt[k])
                        for e2 in range(2):
                            hh = 2 * k + e2
                            P.do("vector", lambda e, hh=hh, e2=e2, k=k, spar=spar:
                                 e.tensor_mul(
                                amidT[(hh % 2) * 64:(hh % 2) * 64 + 64, spar,
                                      hh // 2, :],
                                acc[0:64, hh * 256:(hh + 1) * 256],
                                (A4c[0:64, 0:512] if k % 2 == 0 else
                                 A4c[0:64, 512:1024])[:, e2 * 256:(e2 + 1) * 256]),
                                s_am, 1)
                        am_free[0] = s_am.n
                P.do("vector", lambda e: e.tensor_copy(
                    stats[0:1, 29:30], stats[0:1, 29:30]), s_am, 1)
                acc_free[0] = s_am.n
                a4c_norm_done[0] = s_am.n
            return emit_norm

        pending_norm[0] = make_norm()
        pending_e[0] = make_epieces(j)

    if pending_norm[0] is not None:
        pending_norm[0]()
        pending_norm[0] = None
    for pc in (pending_e[0] or []):
        pc()
    pending_e[0] = None
    P.wait("gpsimd", s_ys)
    barrier(P)
    P.emit(nc)
    if upto == "d":
        return finish(dict(acc=acc[:]))
    dph.close()
    outer.close()
    return nc



def _consts():
    identb = np.eye(128, dtype=ml_dtypes.bfloat16)
    identf = np.eye(128, dtype=np.float32)
    onesf = np.ones((1, 128), dtype=np.float32)
    onescf = np.ones((128, 1), dtype=np.float32)
    tri = np.triu(np.ones((128, 128), np.float32))
    ones = np.ones((128, 128), np.float32)
    zero = np.zeros((128, 128), np.float32)
    m1 = np.concatenate([tri, ones], axis=1)
    m0 = np.concatenate([zero, tri], axis=1)
    maskdd = np.concatenate([m1, m0, m1, m0], axis=1).astype(ml_dtypes.bfloat16)
    return identb, identf, onesf, onescf, maskdd


def _padbias(cc):
    pb = np.full((128, 137), KILL, np.float32)
    col = 0
    for j in range(NSLOT):
        T = tile_T(j, cc)
        pb[:, col:col + T] = -SHIFT
        col += GJ[j]
    pb[:, 136] = -SHIFT
    return pb


def kernel(L0_attn_in_features, L0_attn_Wq, L0_attn_Wk, L0_attn_Wv, L0_attn_Wo,
           L0_ffn_W1, L0_ffn_b1, L0_ffn_W2, L0_ffn_b2):
    x = np.ascontiguousarray(np.asarray(L0_attn_in_features, dtype=np.float32))
    identb, identf, onesf, onescf, maskdd = _consts()
    in_maps = []
    for c in range(8):
        b, cc = c // 4, c % 4
        rows = np.concatenate([np.arange(256 * tile_T(j, cc),
                                         256 * tile_T(j, cc) + 256)
                               for j in range(NSLOT)])
        in_maps.append(dict(
            xq=np.ascontiguousarray(x[b][rows]),
            xb=np.ascontiguousarray(x[b]),
            wq=np.ascontiguousarray(np.asarray(L0_attn_Wq, np.float32)),
            wk=np.ascontiguousarray(np.asarray(L0_attn_Wk, np.float32)),
            wv=np.ascontiguousarray(np.asarray(L0_attn_Wv, np.float32)),
            wo=np.ascontiguousarray(np.asarray(L0_attn_Wo, np.float32)),
            w1=np.ascontiguousarray(np.asarray(L0_ffn_W1, np.float32)),
            b1=np.ascontiguousarray(np.asarray(L0_ffn_b1, np.float32)),
            w2=np.ascontiguousarray(np.asarray(L0_ffn_W2, np.float32)),
            b2=np.ascontiguousarray(np.asarray(L0_ffn_b2, np.float32)),
            identb=identb, identf=identf, onesf=onesf, onescf=onescf,
            maskdd=maskdd, padbias=_padbias(cc),
        ))
    nc = build_nc()
    res = run_bass_kernel_spmd(nc, in_maps, core_ids=list(range(8)))
    out = np.empty((B, S, D), np.float32)
    for c in range(8):
        b, cc = c // 4, c % 4
        yc = res.results[c]["y"]
        for j in range(NSLOT):
            T = tile_T(j, cc)
            out[b, 256 * T:256 * T + 256] = yc[256 * j:256 * j + 256]
    return out


if __name__ == "__main__":
    nc = build_nc()
    print("built ok,", nc.n_instructions(), "instructions")
